# version 21
# baseline (speedup 1.0000x reference)
import contextlib

import numpy as np
import ml_dtypes

import concourse.bass as bass
import concourse.mybir as mybir
from concourse.bass_utils import run_bass_kernel_spmd

F32 = mybir.dt.float32
BF16 = mybir.dt.bfloat16
I32 = mybir.dt.int32
AF = mybir.ActivationFunctionType
ALU = mybir.AluOpType
AX = mybir.AxisListType

D = 1024
T = 2048
NCH = 8
NTT = 4
TT = 512
MAGIC = 12582912.0

_UPTO = "all"
_DEBUG = False
_NHEADS = 16
_NHEADS1 = 16

ENGS = ("pe", "act", "dve", "sp", "gp")


class Builder:
    def __init__(self):
        self.stream = {e: [] for e in ENGS}
        self.ctr = {e: 0 for e in ENGS}
        self.dmatot = {}
        self.waited = {}

    def op(self, eng, fn, inc=False):
        if inc:
            self.ctr[eng] += 1
            self.stream[eng].append(("op", fn, eng))
            return self.ctr[eng]
        self.stream[eng].append(("op", fn, None))
        return None

    def wait(self, eng, sem, val):
        if val is None or val <= 0:
            return
        key = (eng, sem)
        if self.waited.get(key, 0) >= val:
            return
        self.waited[key] = val
        self.stream[eng].append(("wait", sem, val))

    def dma(self, eng, sem, fn):
        self.dmatot[sem] = self.dmatot.get(sem, 0) + 16
        self.stream[eng].append(("dma", fn, sem))
        return self.dmatot[sem]


class BankPool:
    def __init__(self, b, tensors):
        self.b = b
        self.tensors = tensors
        self.free_q = list(range(len(tensors)))
        self.cons = {i: None for i in range(len(tensors))}

    def acquire(self):
        i = self.free_q.pop(0)
        c = self.cons[i]
        if c is not None:
            self.b.wait("pe", c[0], c[1])
        return self.tensors[i], i

    def acquire_pair(self):
        order = [i // 2 for i in self.free_q]
        cand = [j for k, j in enumerate(order) if j in order[k + 1:]]
        for j in cand + list(range(len(self.tensors) // 2)):
            if 2 * j in self.free_q and 2 * j + 1 in self.free_q:
                self.free_q.remove(2 * j)
                self.free_q.remove(2 * j + 1)
                for i in (2 * j, 2 * j + 1):
                    c = self.cons[i]
                    if c is not None:
                        self.b.wait("pe", c[0], c[1])
                return j
        raise RuntimeError("no aligned psum pair free")

    def acquire_deferred(self):
        i = self.free_q.pop(0)
        return self.tensors[i], i, self.cons[i]

    def acquire_pair_deferred(self):
        order = [i // 2 for i in self.free_q]
        cand = [j for k, j in enumerate(order) if j in order[k + 1:]]
        for j in cand + list(range(len(self.tensors) // 2)):
            if 2 * j in self.free_q and 2 * j + 1 in self.free_q:
                self.free_q.remove(2 * j)
                self.free_q.remove(2 * j + 1)
                return j, [self.cons[2 * j], self.cons[2 * j + 1]]
        raise RuntimeError("no aligned psum pair free")

    def release(self, i, eng, val):
        self.cons[i] = (eng, val)
        self.free_q.append(i)


def build_nc(upto="all", debug=False):
    _NHEADS = globals()["_NHEADS"]
    _NHEADS1 = globals()["_NHEADS1"]
    nc = bass.Bass("TRN2", target_bir_lowering=False)
    dr = {}

    def din(name, shape, dt=F32):
        dr[name] = nc.dram_tensor(name, list(shape), dt, kind="ExternalInput").ap()

    din("xT", (D, T))
    din("wq0", (8, 128, 1024)); din("wk0", (8, 128, 1024)); din("wv0", (8, 128, 1024))
    din("wp0", (8, 128, 1024))
    din("ker0", (8, 128, 1024))
    din("a0", (128, 128)); din("b0", (16, 1024)); din("bias0", (128, 8))
    din("wq1", (8, 128, 1024)); din("wk1", (2, 128, 1024)); din("wv1", (2, 128, 1024))
    din("wo1", (8, 128, 1024))
    din("lam", (1, 16)); din("rmsw", (64, 1))
    din("wp1", (128, 1024)); din("wp2", (128, 8)); din("bp1", (128, 1)); din("bp2", (1, 8))
    din("w1e", (8, 16 * 128 * 1024)); din("w2e", (8, 8 * 128 * 2048))
    din("b1e", (8, 2048)); din("b2e", (8, 1024))
    din("wof", (8, 128, 1024)); din("bof", (128, 8))
    din("cs", (128, 2 * T), BF16)
    din("mask", (128, 128), BF16)
    din("maskn", (128, 128), BF16)
    din("ident", (128, 128), BF16)
    din("identf", (128, 128))
    din("grp2", (128, 2), BF16); din("grp4", (128, 4), BF16)
    din("io99", (1, 8))

    yT = nc.dram_tensor("yT", [D, T], F32, kind="ExternalOutput").ap()
    dbg_xn = dbg_big = dbg_stat = None
    if debug:
        dbg_xn = nc.dram_tensor("dbg_xn", [128, NCH * T], BF16, kind="ExternalOutput").ap()
        dbg_big = nc.dram_tensor("dbg_big", [128, 49408], BF16, kind="ExternalOutput").ap()
        dbg_stat = nc.dram_tensor("dbg_stat", [128, 832], F32, kind="ExternalOutput").ap()

    b = Builder()

    with contextlib.ExitStack() as ctx:
        def sbuf(name, shape, dt):
            return ctx.enter_context(nc.sbuf_tensor(name, list(shape), dt))

        X = sbuf("X", (128, NCH * T), F32)
        XN = sbuf("XN", (128, NCH * T), BF16)
        BIG = sbuf("BIGB", (128, 49408), BF16)
        WB = sbuf("WB", (128, 2048), BF16)
        SCR = sbuf("SCR", (128, 2048), BF16)
        SCRF = sbuf("SCRF", (128, 512), F32)
        STAT = sbuf("STAT", (128, 832), F32)
        SMALL = sbuf("SMALL", (128, 72), BF16)
        MASKT = sbuf("MASKT", (128, 128), BF16)
        MASKN = sbuf("MASKN", (128, 128), BF16)
        IDT = sbuf("IDT", (128, 128), BF16)
        IDF = sbuf("IDF", (128, 128), F32)
        IDXI = sbuf("IDXI", (1, 1), I32)
        ps_big = [ctx.enter_context(nc.psum_tensor(f"psb{i}", [128, 1024], F32))
                  for i in range(4)]
        ps = []
        for i in range(4):
            ps.append(ps_big[i][0:128, 0:512])
            ps.append(ps_big[i][0:128, 512:1024])

        sems = {}
        for e in ENGS:
            sems["c_" + e] = ctx.enter_context(nc.semaphore("c_" + e))
        for d in ("d_pre", "d_x", "d_wa", "d_wb", "d_qs", "d_cs", "d_kb",
                  "d_sm", "d_y", "d_mx", "d_dbg"):
            sems[d] = ctx.enter_context(nc.semaphore(d))

        HN_Q, HN_QT, HN_K, HN_KT = 0, 8, 16, 24
        ONES_F = 32
        RSTD = 160
        _statc = [672]

        def stat_col(n=1):
            c = _statc[0]; _statc[0] += n
            assert _statc[0] <= 744
            return c

        col_negB = [stat_col(), stat_col()]
        col_isc = stat_col(); col_sc = stat_col()
        col_bias0 = stat_col(8)
        col_bo = stat_col(8); col_bo3 = stat_col(8)
        col_b1s = stat_col(16); col_b2s = stat_col(8)
        col_bp1 = stat_col(); col_rmsw = stat_col()
        col_g = stat_col(); col_wp2 = stat_col(8)
        col_mx = stat_col(); col_mxt = stat_col()
        col_eps5 = stat_col(); col_eps6 = stat_col()
        _p0c = [744]

        def p0_col(n=1):
            c = _p0c[0]; _p0c[0] += n
            assert _p0c[0] <= 832
            return c

        p_io99 = p0_col(8); p_bp2 = p0_col(8); p_log = p0_col(8)
        p_eq = p0_col(8); p_cand = p0_col(8)
        p_m = p0_col(); p_idx = p0_col(); p_prod = p0_col(); p_t1 = p0_col()

        SM_G2, SM_G4, ONES_B = 0, 2, 6

        L0Q, L0K, L0VT = 0, 16384, 32768
        L1Q, L1KA, L1KB, L1VT, L1CS = 0, 16384, 20480, 24576, 28736
        L1H, L1RS = 0, 32768
        PT = (0, 512)
        SC_Y, SC_U = 1024, 1536

        pool = BankPool(b, ps)

        def MM(out_ap, lhsT, rhs, start, stop, inc=False, tp=None):
            def fn(e, o=out_ap, l=lhsT, r=rhs, st=start, sp=stop, t=tp):
                return e.matmul(o, l, r, start=st, stop=sp, tile_position=t)
            return b.op("pe", fn, inc=inc)

        def ACT(out_ap, in_ap, func, bias=0.0, scale=1.0, inc=True):
            def fn(e, o=out_ap, i=in_ap, f=func, bi=bias, sc=scale):
                return e.activation(o, i, f, bias=bi, scale=sc)
            return b.op("act", fn, inc=inc)

        def DVE(meth, *args, inc=True, **kw):
            def fn(e, m=meth, a=args, k=kw):
                return getattr(e, m)(*a, **k)
            return b.op("dve", fn, inc=inc)

        def DMA(eng, sem, out_ap, in_ap, **kw):
            def fn(e, o=out_ap, i=in_ap, k=kw):
                return e.dma_start(out=o, in_=i, **k)
            return b.dma(eng, sem, fn)

        xload = 0
        for c in range(NCH):
            xload = DMA("sp", "d_x", X[:, c * T:(c + 1) * T],
                        dr["xT"][c * 128:(c + 1) * 128, :])
        pre = 0
        pre = DMA("sp", "d_pre", MASKT[:, :], dr["mask"][:, :])
        pre = DMA("sp", "d_pre", MASKN[:, :], dr["maskn"][:, :])
        pre = DMA("sp", "d_pre", IDT[:, :], dr["ident"][:, :])
        pre = DMA("sp", "d_pre", IDF[:, :], dr["identf"][:, :])
        pre = DMA("sp", "d_pre", SMALL[:, SM_G2:SM_G2 + 2], dr["grp2"][:, :])
        pre = DMA("sp", "d_pre", SMALL[:, SM_G4:SM_G4 + 4], dr["grp4"][:, :])
        pre = DMA("sp", "d_pre", STAT[0:1, p_io99:p_io99 + 8], dr["io99"][:, :])
        pre = DMA("sp", "d_pre", STAT[0:1, p_bp2:p_bp2 + 8], dr["bp2"][:, :])
        pre = DMA("sp", "d_pre", STAT[:, col_bias0:col_bias0 + 8], dr["bias0"][:, :])
        pre = DMA("sp", "d_pre", STAT[:, col_bo:col_bo + 8], dr["bof"][:, :])
        pre = DMA("sp", "d_pre", STAT[:, col_bp1:col_bp1 + 1], dr["bp1"][:, :])
        pre = DMA("sp", "d_pre", STAT[0:64, col_rmsw:col_rmsw + 1], dr["rmsw"][:, :])
        pre = DMA("sp", "d_pre", STAT[64:65, 176:192], dr["lam"][:, :])
        pre = DMA("sp", "d_pre", STAT[:, col_wp2:col_wp2 + 8], dr["wp2"][:, :])
        pre_tot = pre

        e5_v = DVE("memset", STAT[:, col_eps5:col_eps5 + 1], 1e-5, inc=True)
        e6_v = DVE("memset", STAT[:, col_eps6:col_eps6 + 1], 1e-6, inc=True)
        ov1 = DVE("memset", STAT[:, ONES_F:ONES_F + 128], 1.0, inc=True)
        b.wait("dve", "dve", ov1)
        ones_v = DVE("memset", SMALL[:, ONES_B:ONES_B + 64], 1.0, inc=True)

        def vt_ones(base, per_tt):
            ap3 = BIG[:, base:base + 16 * per_tt].rearrange(
                "p (t x) -> p t x", x=per_tt).rearrange(
                "p t (h o) -> p (t h) o", o=65)[:, :, 64:65]
            return DVE("memset", ap3, 1.0, inc=True)

        wslot = {"par": 0, "cons": {0: None, 1: None}, "sem": ("d_wa", "d_wb")}
        scrf_st = {"free": None}

        def slot_begin():
            par = wslot["par"]
            wslot["par"] ^= 1
            return par, wslot["cons"][par]

        def band_load(in_ap, cols=1024):
            par, c = slot_begin()
            if c is not None:
                b.wait("gp", c[0], c[1])
            tot = DMA("gp", wslot["sem"][par],
                      WB[:, par * 1024:par * 1024 + cols], in_ap)
            return par, wslot["sem"][par], tot

        def band_release(par, eng, val):
            wslot["cons"][par] = (eng, val)

        def layernorm(layer, x_ready):
            b.wait("act", "d_x", xload)
            if x_ready is not None:
                b.wait("act", x_ready[0], x_ready[1])
            xb = {}
            for c in range(NCH):
                xb[c] = ACT(XN[:, c * T:(c + 1) * T], X[:, c * T:(c + 1) * T], AF.Copy)
            kept = None
            last_n = None
            prev_br = None
            tevs = []
            slot_user = {0: None, 1: None}
            for j in range(NTT):
                bk_sx, i_sx = pool.acquire()
                bk_s2, i_s2 = pool.acquire()
                sqv = {}
                vx = v2 = None
                for c in range(NCH):
                    sl = (c % 2) * 512
                    if slot_user[c % 2] is not None:
                        b.wait("act", "pe", slot_user[c % 2])
                    sqv[c] = ACT(SCR[:, sl:sl + 512],
                                 XN[:, c * T + j * TT: c * T + j * TT + TT],
                                 AF.Square)
                    b.wait("pe", "act", xb[c])
                    vx = MM(bk_sx[0:1, 0:TT], SMALL[0:128, ONES_B:ONES_B + 1],
                            XN[:, c * T + j * TT: c * T + j * TT + TT],
                            start=(c == 0), stop=(c == NCH - 1), inc=(c == NCH - 1))
                    b.wait("pe", "act", sqv[c])
                    v2 = MM(bk_s2[0:1, 0:TT], SMALL[0:128, ONES_B:ONES_B + 1],
                            SCR[:, sl:sl + 512],
                            start=(c == 0), stop=(c == NCH - 1), inc=True)
                    slot_user[c % 2] = v2
                b.wait("dve", "pe", vx)
                if scrf_st["free"] is not None:
                    b.wait("dve", scrf_st["free"][0], scrf_st["free"][1])
                m_v = DVE("tensor_scalar_mul", SCRF[0:1, 0:TT], bk_sx[0:1, 0:TT],
                          1.0 / 1024.0)
                pool.release(i_sx, "dve", m_v)
                b.wait("dve", "dve", m_v)
                b.wait("dve", "pe", prev_br)
                t_v = DVE("tensor_mul", STAT[0:1, RSTD:RSTD + TT],
                          SCRF[0:1, 0:TT], SCRF[0:1, 0:TT])
                b.wait("dve", "pe", v2)
                b.wait("dve", "dve", t_v)
                var_v = DVE("scalar_tensor_tensor", bk_s2[0:1, 0:TT],
                            bk_s2[0:1, 0:TT], 1.0 / 1024.0,
                            STAT[0:1, RSTD:RSTD + TT], ALU.mult, ALU.subtract)
                b.wait("act", "dve", var_v)
                b.wait("act", "dve", e5_v)
                ln_v = ACT(STAT[0:1, RSTD:RSTD + TT], bk_s2[0:1, 0:TT],
                           AF.Ln, bias=STAT[0:1, col_eps5:col_eps5 + 1],
                           scale=1.0)
                pool.release(i_s2, "act", ln_v)
                b.wait("act", "act", ln_v)
                ex_v = ACT(STAT[0:1, RSTD:RSTD + TT], STAT[0:1, RSTD:RSTD + TT],
                           AF.Exp, scale=-0.5)
                bk_m, i_m = pool.acquire()
                bk_r, i_r = pool.acquire()
                b.wait("pe", "dve", ov1)
                b.wait("pe", "dve", m_v)
                bm = MM(bk_m[0:128, 0:TT], STAT[0:1, ONES_F:ONES_F + 128],
                        SCRF[0:1, 0:TT], start=True, stop=True, inc=True)
                b.wait("pe", "act", ex_v)
                br = MM(bk_r[0:128, 0:TT], STAT[0:1, ONES_F:ONES_F + 128],
                        STAT[0:1, RSTD:RSTD + TT], start=True, stop=True, inc=True)
                prev_br = br
                for c in range(NCH):
                    b.wait("dve", "pe", bm)
                    tv = DVE("tensor_sub", SCRF[:, 0:TT],
                             X[:, c * T + j * TT: c * T + j * TT + TT],
                             bk_m[0:128, 0:TT])
                    b.wait("dve", "pe", br)
                    b.wait("dve", "dve", tv)
                    last_n = DVE("tensor_mul",
                                 XN[:, c * T + j * TT: c * T + j * TT + TT],
                                 SCRF[:, 0:TT], bk_r[0:128, 0:TT])
                if layer == 1 and j == NTT - 1:
                    kept = (bk_m, i_m, bk_r, i_r)
                else:
                    pool.release(i_m, "dve", last_n)
                    pool.release(i_r, "dve", last_n)
                scrf_st["free"] = ("dve", last_n)
                tevs.append(("dve", last_n))
            return last_n, kept, tevs

        def evac_to(base):
            def dest(mb, j, bk):
                ev = ACT(BIG[:, base + mb * T + j * TT: base + mb * T + j * TT + TT],
                         bk[0:128, 0:TT], AF.Copy)
                return ("act", ev)
            return dest

        def projection(img, mbs, dest_fn, nkc=NCH, rhs=None, rhs_base=0,
                       rhs_ready=None, rhs_tevs=None):
            src = rhs if rhs is not None else XN
            last_dest = None
            for mb in mbs:
                par, sem, tot = band_load(dr[img][mb:mb + 1, :, :nkc * 128],
                                          cols=nkc * 128)
                b.wait("pe", sem, tot)
                if rhs_ready is not None:
                    b.wait("pe", rhs_ready[0], rhs_ready[1])
                banks = [pool.acquire() for _ in range(NTT)]
                lastv = None
                if rhs_tevs is not None:
                    for j in range(NTT):
                        b.wait("pe", rhs_tevs[j][0], rhs_tevs[j][1])
                        for c in range(nkc):
                            lastv = MM(
                                banks[j][0][0:128, 0:TT],
                                WB[:, par * 1024 + c * 128:
                                   par * 1024 + c * 128 + 128],
                                src[:, rhs_base + c * T + j * TT:
                                    rhs_base + c * T + j * TT + TT],
                                start=(c == 0), stop=(c == nkc - 1),
                                inc=(c == nkc - 1 and j == NTT - 1))
                else:
                    for c in range(nkc):
                        for j in range(NTT):
                            lastv = MM(
                                banks[j][0][0:128, 0:TT],
                                WB[:, par * 1024 + c * 128:
                                   par * 1024 + c * 128 + 128],
                                src[:, rhs_base + c * T + j * TT:
                                    rhs_base + c * T + j * TT + TT],
                                start=(c == 0), stop=(c == nkc - 1),
                                inc=(c == nkc - 1 and j == NTT - 1))
                band_release(par, "pe", lastv)
                for j in range(NTT):
                    bk, ib = banks[j]
                    b.wait("act", "pe", lastv)
                    b.wait("dve", "pe", lastv)
                    eng, val = dest_fn(mb, j, bk)
                    pool.release(ib, eng, val)
                    last_dest = (eng, val)
            return last_dest

        def v_project(img, mbs, vt_base, heads_per_mb, per_tt):
            for mi, mb in enumerate(mbs):
                par, sem, tot = band_load(dr[img][mb:mb + 1, :, :], cols=1024)
                b.wait("pe", sem, tot)
                banks = [pool.acquire() for _ in range(NTT)]
                lastv = None
                for c in range(NCH):
                    for j in range(NTT):
                        lastv = MM(
                            banks[j][0][0:128, 0:TT],
                            WB[:, par * 1024 + c * 128: par * 1024 + c * 128 + 128],
                            XN[:, c * T + j * TT: c * T + j * TT + TT],
                            start=(c == 0), stop=(c == NCH - 1),
                            inc=(c == NCH - 1 and j == NTT - 1))
                band_release(par, "pe", lastv)
                tp_prev = None
                for wave in range(2):
                    evs = {}
                    for jj in range(2):
                        j = wave * 2 + jj
                        bk, ib = banks[j]
                        b.wait("act", "pe", lastv)
                        b.wait("act", "pe", tp_prev)
                        evs[jj] = ACT(SCR[:, jj * TT:(jj + 1) * TT],
                                      bk[0:128, 0:TT], AF.Copy)
                        pool.release(ib, "act", evs[jj])
                    for ttw in range(8):
                        tt = wave * 8 + ttw
                        bk, ib = pool.acquire()
                        b.wait("pe", "act", evs[ttw // 4])
                        b.wait("pe", "d_pre", pre_tot)
                        tp_out = bk[0:128, 0:64].bitcast(BF16)

                        def tpfn(e, o=tp_out,
                                 i=SCR[:, ttw * 128:ttw * 128 + 128],
                                 idn=IDT[:, :]):
                            return e.transpose(o, i, idn)
                        tv = b.op("pe", tpfn, inc=True)
                        tp_prev = tv
                        base = vt_base + tt * per_tt + mi * heads_per_mb * 65
                        o_ap = BIG[:, base:base + heads_per_mb * 65].rearrange(
                            "p (h o) -> p h o", o=65)[:, :, 0:64]
                        i_ap = tp_out.rearrange("p (h o) -> p h o", o=64)[
                            :, 0:heads_per_mb, :]
                        b.wait("act", "pe", tv)
                        ev2 = ACT(o_ap, i_ap, AF.Copy)
                        pool.release(ib, "act", ev2)
            return

        def out_proj(img, o_ready, o_base):
            lastadd = None
            for mb in range(8):
                par, sem, tot = band_load(dr[img][mb:mb + 1, :, :], cols=1024)
                b.wait("pe", sem, tot)
                b.wait("pe", o_ready[0], o_ready[1])
                banks = [pool.acquire() for _ in range(NTT)]
                lastv = None
                for c in range(NCH):
                    for j in range(NTT):
                        lastv = MM(banks[j][0][0:128, 0:TT],
                                   WB[:, par * 1024 + c * 128: par * 1024 + c * 128 + 128],
                                   BIG[:, o_base + c * T + j * TT:
                                       o_base + c * T + j * TT + TT],
                                   start=(c == 0), stop=(c == NCH - 1),
                                   inc=(c == NCH - 1 and j == NTT - 1))
                band_release(par, "pe", lastv)
                for j in range(NTT):
                    bk, ib = banks[j]
                    b.wait("dve", "pe", lastv)
                    av = DVE("tensor_add", X[:, mb * T + j * TT: mb * T + j * TT + TT],
                             X[:, mb * T + j * TT: mb * T + j * TT + TT],
                             bk[0:128, 0:TT])
                    pool.release(ib, "dve", av)
                    lastadd = av
            return lastadd

        def bscan(base_col, nch, acc_col, ready, init=True):
            if init:
                DVE("memset", STAT[:, acc_col:acc_col + 1], 0.0)
            last = None
            for c in range(nch):
                for j in range(NTT):
                    if ready is not None:
                        b.wait("dve", ready[0], ready[1])
                    rv = DVE("tensor_reduce", STAT[:, HN_QT:HN_QT + 1],
                             BIG[:, base_col + c * T + j * TT:
                                 base_col + c * T + j * TT + TT],
                             AX.X, ALU.max, apply_absolute_value=True)
                    b.wait("dve", "dve", rv)
                    last = DVE("tensor_max", STAT[:, acc_col:acc_col + 1],
                               STAT[:, acc_col:acc_col + 1],
                               STAT[:, HN_QT:HN_QT + 1])
            return last

        def bfinal(layer, scale, lq, lk):
            bq_, ibq = pool.acquire()
            bk_, ibk = pool.acquire()
            b.wait("pe", "dve", max(lq, lk))
            b.wait("pe", "d_pre", pre_tot)
            tq = MM(bq_[0:1, 0:128], STAT[:, HN_Q:HN_Q + 1], IDF[:, :],
                    start=True, stop=True, inc=True)
            tk = MM(bk_[0:1, 0:128], STAT[:, HN_K:HN_K + 1], IDF[:, :],
                    start=True, stop=True, inc=True)
            b.wait("dve", "pe", tk)
            rq = DVE("tensor_reduce", STAT[0:1, HN_Q:HN_Q + 1],
                     bq_[0:1, 0:128], AX.X, ALU.max)
            rk = DVE("tensor_reduce", STAT[0:1, HN_K:HN_K + 1],
                     bk_[0:1, 0:128], AX.X, ALU.max)
            pool.release(ibq, "dve", rq)
            pool.release(ibk, "dve", rk)
            b.wait("dve", "dve", rk)
            pv = DVE("tensor_mul", STAT[0:1, p_prod:p_prod + 1],
                     STAT[0:1, HN_Q:HN_Q + 1], STAT[0:1, HN_K:HN_K + 1])
            b.wait("dve", "dve", pv)
            nv = DVE("tensor_scalar_mul", STAT[0:1, p_t1:p_t1 + 1],
                     STAT[0:1, p_prod:p_prod + 1], -1.02 * scale)
            bk, ib = pool.acquire()
            b.wait("pe", "dve", nv)
            bv = MM(bk[0:128, 0:1], STAT[0:1, ONES_F:ONES_F + 128],
                    STAT[0:1, p_t1:p_t1 + 1], start=True, stop=True, inc=True)
            b.wait("dve", "pe", bv)
            cv = DVE("tensor_copy", STAT[:, col_negB[layer]:col_negB[layer] + 1],
                     bk[0:128, 0:1])
            pool.release(ib, "dve", cv)
            return cv

        PT1_BASE = 33344
        pt_state = {"i": 0, "cons": [], "slots": []}
        fin1_cell = [None]
        oev1_cell = [None]
        comb1_cell = [None]
        ffnq = []

        def pt_setup(slots, init_cons=None):
            pt_state["i"] = 0
            pt_state["slots"] = slots
            pt_state["cons"] = [init_cons] * len(slots)

        def pt_begin():
            i = pt_state["i"]
            pt_state["i"] = (i + 1) % len(pt_state["slots"])
            c = pt_state["cons"][i]
            if c is not None:
                b.wait("act", c[0], c[1])
            return i

        def pt_ap(i, N):
            tens, base = pt_state["slots"][i]
            return tens[:, base:base + N]

        def attn_l0(h, qt, negB_v, pend, ocell, allow_ffn=True):
            bh = 64 * (h % 2)
            ch = h // 2
            bkO, iO, bkO_cons = pool.acquire_deferred()
            bkO_wait = [bkO_cons]
            stb = []
            for _ in range(2):
                t_, i_, c_ = pool.acquire_deferred()
                stb.append([t_, i_, None, c_])
            n = 4 * qt + 4
            stage = []
            pv_last = [None]
            pend = list(pend) if pend else []

            def drain(k):
                while k > 0:
                    if pend:
                        pend.pop(0)()
                    elif ffnq:
                        ffnq.pop(0)()
                    else:
                        return
                    k -= 1

            def emit_A(idx):
                i = idx
                r = i - 4 * qt
                col0 = 128 * r if r >= 0 else 0
                N = 512 - col0
                bkS, iS, prev_ev, first_c = stb[idx % 2]
                if first_c is not None:
                    b.wait("pe", first_c[0], first_c[1])
                    stb[idx % 2][3] = None
                if prev_ev is not None:
                    b.wait("pe", "act", prev_ev)
                sv = MM(bkS[0:128, 0:N],
                        BIG[bh:bh + 64,
                            L0K + ch * T + i * 128: L0K + ch * T + i * 128 + 128],
                        BIG[bh:bh + 64,
                            L0Q + ch * T + qt * TT + col0: L0Q + ch * T + qt * TT + 512],
                        start=True, stop=(r < 0), inc=(r < 0))
                if r >= 0:
                    sv = MM(bkS[0:128, 0:128], MASKN[:, :], IDT[:, :],
                            start=False, stop=True, inc=True)
                par = pt_begin()
                b.wait("act", "pe", sv)
                b.wait("act", "dve", negB_v)
                ev = ACT(pt_ap(par, N), bkS[0:128, 0:N], AF.Exp,
                         bias=STAT[:, col_negB[0]:col_negB[0] + 1], scale=0.125)
                stb[idx % 2][2] = ev
                pw = ("act", ev)
                stage.append((idx, par, pw, col0, N))

            def emit_B():
                idx, par, pw, col0, N = stage.pop(0)
                if bkO_wait[0] is not None:
                    b.wait("pe", bkO_wait[0][0], bkO_wait[0][1])
                    bkO_wait[0] = None
                b.wait("pe", pw[0], pw[1])
                pv = MM(bkO[0:65, col0:512],
                        BIG[:, L0VT + idx * 1040 + h * 65:
                            L0VT + idx * 1040 + h * 65 + 65],
                        pt_ap(par, N),
                        start=(idx == 0), stop=(idx == n - 1), inc=True)
                pt_state["cons"][par] = ("pe", pv)
                pv_last[0] = pv

            LAG = 3
            for idx in range(n):
                emit_A(idx)
                if idx >= 3 and pend:
                    pend.pop(0)()
                elif allow_ffn and ffnq:
                    ffnq.pop(0)()
                if idx >= LAG:
                    emit_B()
            while stage:
                emit_B()
            drain(len(pend))
            for bkS, iS, prev_ev, _fc in stb:
                pool.release(iS, "act", prev_ev)

            env = {}

            b.wait("dve", "pe", pv_last[0])
            env["rv"] = DVE("reciprocal", SCR[64:65, SC_Y:SC_Y + 512],
                            bkO[64:65, 0:512])

            def s_bcast():
                bkB, iB = pool.acquire()
                env["bkB"], env["iB"] = bkB, iB
                b.wait("pe", "dve", env["rv"])
                env["bv"] = MM(bkB[0:64, 0:512],
                               SMALL[64:65, ONES_B:ONES_B + 64],
                               SCR[64:65, SC_Y:SC_Y + 512],
                               start=True, stop=True, inc=True)

            def s_bev():
                b.wait("act", "pe", env["bv"])
                bev = ACT(SCR[0:64, SC_U:SC_U + 512],
                          env["bkB"][0:64, 0:512], AF.Copy)
                env["bev"] = bev
                pool.release(env["iB"], "act", bev)

            def s_final():
                ocol = L0Q + ch * T + qt * TT
                b.wait("dve", "act", env["bev"])
                if h % 2 == 0:
                    cv = DVE("tensor_mul", BIG[0:64, ocol:ocol + TT],
                             bkO[0:64, 0:512], SCR[0:64, SC_U:SC_U + 512])
                    pool.release(iO, "dve", cv)
                else:
                    yv = DVE("tensor_mul", SCR[0:64, SC_U:SC_U + 512],
                             bkO[0:64, 0:512], SCR[0:64, SC_U:SC_U + 512])
                    pool.release(iO, "dve", yv)
                    bkR, iR = pool.acquire()
                    b.wait("pe", "dve", yv)
                    rrv = MM(bkR[64:128, 0:512], IDT[0:64, 0:64],
                             SCR[0:64, SC_U:SC_U + 512], start=True, stop=True,
                             inc=True, tp=(0, 64))
                    b.wait("dve", "pe", rrv)
                    cv = DVE("tensor_copy", BIG[64:128, ocol:ocol + TT],
                             bkR[64:128, 0:512])
                    pool.release(iR, "dve", cv)
                ocell[0] = cv

            return [s_bcast, s_bev, s_final]

        def attn_l1(h, qt, negB_v, kb_tot, lam_v, pend, ocell):
            g = h // 4
            bh = 64 * (h % 2)
            ksrc = L1KA if 64 * (g % 2) == bh else L1KB
            kcol = ksrc + (g // 2) * T
            jO, jO_cons = pool.acquire_pair_deferred()
            bkO1, iO1 = ps[2 * jO], 2 * jO
            bkO2, iO2 = ps[2 * jO + 1], 2 * jO + 1
            jS_d = [pool.acquire_pair_deferred(), pool.acquire_pair_deferred()]
            jSs = [jS_d[0][0], jS_d[1][0]]
            st_wait = [jS_d[0][1], jS_d[1][1]]
            jO_wait = [jO_cons]
            st_ev = [None, None]
            n = 4 * qt + 4
            stage = []
            pv_last = [None]
            b.wait("pe", "d_kb", kb_tot)
            pend = list(pend) if pend else []

            def drain(k):
                while pend and k > 0:
                    pend.pop(0)()
                    k -= 1

            def emit_A(idx):
                i = idx
                r = i - 4 * qt
                col0 = 128 * r if r >= 0 else 0
                N = 512 - col0
                w = idx % 2
                jS = jSs[w]
                if st_wait[w] is not None:
                    for c_ in st_wait[w]:
                        if c_ is not None:
                            b.wait("pe", c_[0], c_[1])
                    st_wait[w] = None
                if st_ev[w] is not None:
                    b.wait("pe", "act", st_ev[w])
                svs = None
                for s in range(2):
                    rb = bh + 32 * s
                    svs = MM(ps[2 * jS + s][0:128, 0:N],
                             BIG[rb:rb + 32, kcol + i * 128: kcol + i * 128 + 128],
                             BIG[rb:rb + 32,
                                 L1Q + (h // 2) * T + qt * TT + col0:
                                 L1Q + (h // 2) * T + qt * TT + 512],
                             start=True, stop=(r < 0), inc=(r < 0 and s == 1),
                             tp=(rb, 0))
                    if r >= 0:
                        svs = MM(ps[2 * jS + s][0:128, 0:128],
                                 MASKN[:, :], IDT[:, :],
                                 start=False, stop=True, inc=(s == 1))
                par = pt_begin()
                ptt, ptb = pt_state["slots"][par]
                in_ap = ps_big[jS][0:128, 0:1024].rearrange(
                    "p (s x) -> p s x", s=2)[:, :, 0:N]
                out_ap = ptt[:, ptb:ptb + 1024].rearrange(
                    "p (s x) -> p s x", s=2)[:, :, 0:N]
                b.wait("act", "pe", svs)
                b.wait("act", "dve", negB_v)
                ev = ACT(out_ap, in_ap, AF.Exp,
                         bias=STAT[:, col_negB[1]:col_negB[1] + 1],
                         scale=0.17677669529663687)
                st_ev[w] = ev
                pw = ("act", ev)
                stage.append((idx, par, pw, col0, N))

            def emit_B():
                idx, par, pw, col0, N = stage.pop(0)
                ptt, ptb = pt_state["slots"][par]
                pv = None
                if jO_wait[0] is not None:
                    for c_ in jO_wait[0]:
                        if c_ is not None:
                            b.wait("pe", c_[0], c_[1])
                    jO_wait[0] = None
                for s in range(2):
                    b.wait("pe", pw[0], pw[1])
                    bkT = bkO1 if s == 0 else bkO2
                    pv = MM(bkT[0:65, col0:512],
                            BIG[:, L1VT + idx * 260 + g * 65:
                                L1VT + idx * 260 + g * 65 + 65],
                            ptt[:, ptb + 512 * s: ptb + 512 * s + N],
                            start=(idx == 0), stop=(idx == n - 1), inc=(s == 1))
                pt_state["cons"][par] = ("pe", pv)
                pv_last[0] = pv

            LAG = 3
            for idx in range(n):
                emit_A(idx)
                if idx >= 2:
                    drain(1)
                if idx >= LAG:
                    emit_B()
            while stage:
                emit_B()
            while len(pend) > 4:
                pend.pop(0)()
            for w in range(2):
                for s in range(2):
                    pool.release(2 * jSs[w] + s, "act", st_ev[w])

            env = {}

            b.wait("act", "pe", pv_last[0])
            if comb1_cell[0] is not None:
                b.wait("act", "dve", comb1_cell[0])
            oe1 = ACT(SCR[0:65, 0:512], bkO1[0:65, 0:512], AF.Copy)
            oe2 = ACT(SCR[0:65, 512:1024], bkO2[0:65, 0:512], AF.Copy)
            env["oe"] = oe2
            pool.release(iO1, "act", oe2)
            pool.release(iO2, "act", oe2)
            b.wait("dve", "act", oe2)
            env["r1"] = DVE("reciprocal", SCR[64:65, SC_Y:SC_Y + 512],
                            SCR[64:65, 0:512])
            env["r2"] = DVE("reciprocal", SCR[64:65, SC_U:SC_U + 512],
                            SCR[64:65, 512:1024])
            b.wait("dve", "dve", env["r2"])
            b.wait("dve", "act", lam_v)
            env["r2l"] = DVE("tensor_scalar_mul",
                             SCR[64:65, SC_U:SC_U + 512],
                             SCR[64:65, SC_U:SC_U + 512],
                             STAT[64:65, 160 + h:161 + h])

            def s_bc1():
                jB = pool.acquire_pair()
                env["bkB1"], env["iB1"] = ps[2 * jB], 2 * jB
                env["bkB2"], env["iB2"] = ps[2 * jB + 1], 2 * jB + 1
                b.wait("pe", "dve", env["r1"])
                env["bv1"] = MM(env["bkB1"][0:64, 0:512],
                                SMALL[64:65, ONES_B:ONES_B + 64],
                                SCR[64:65, SC_Y:SC_Y + 512],
                                start=True, stop=True, inc=True)

            def s_bc2():
                b.wait("pe", "dve", env["r2l"])
                env["bv2"] = MM(env["bkB2"][0:64, 0:512],
                                SMALL[64:65, ONES_B:ONES_B + 64],
                                SCR[64:65, SC_U:SC_U + 512],
                                start=True, stop=True, inc=True)

            def s_t1():
                b.wait("dve", "pe", env["bv1"])
                b.wait("dve", "act", env["oe"])
                env["t1"] = DVE("tensor_mul", SCR[0:64, SC_Y:SC_Y + 512],
                                SCR[0:64, 0:512], env["bkB1"][0:64, 0:512])

            def s_u():
                b.wait("dve", "pe", env["bv2"])
                u = DVE("tensor_mul", SCR[0:64, SC_U:SC_U + 512],
                        SCR[0:64, 512:1024], env["bkB2"][0:64, 0:512])
                comb1_cell[0] = u
                b.wait("dve", "dve", u)
                yv = DVE("tensor_sub", SCR[0:64, SC_Y:SC_Y + 512],
                         SCR[0:64, SC_Y:SC_Y + 512],
                         SCR[0:64, SC_U:SC_U + 512])
                b.wait("dve", "dve", yv)
                env["sqv"] = DVE("tensor_mul", SCR[0:64, SC_U:SC_U + 512],
                                 SCR[0:64, SC_Y:SC_Y + 512],
                                 SCR[0:64, SC_Y:SC_Y + 512])
                env["uv"] = u

            def s_ss():
                b.wait("pe", "dve", env["sqv"])
                env["ssv"] = MM(env["bkB1"][0:1, 0:512],
                                SMALL[0:64, ONES_B:ONES_B + 1],
                                SCR[0:64, SC_U:SC_U + 512],
                                start=True, stop=True, inc=True)

            def s_ln():
                b.wait("act", "pe", env["ssv"])
                b.wait("act", "dve", e6_v)
                c = scrf_st["free"]
                if c is not None:
                    b.wait("act", c[0], c[1])
                lnv = ACT(SCRF[0:1, 0:512], env["bkB1"][0:1, 0:512], AF.Ln,
                          bias=STAT[0:1, col_eps6:col_eps6 + 1], scale=1.0 / 64.0)
                pool.release(env["iB1"], "act", lnv)
                env["lnv"] = lnv

            def s_exp():
                b.wait("act", "act", env["lnv"])
                env["xev"] = ACT(BIG[0:1, 32832:32832 + 512],
                                 SCRF[0:1, 0:512], AF.Exp, scale=-0.5)

            def s_rb():
                b.wait("pe", "dve", env["uv"])
                b.wait("pe", "act", env["xev"])
                env["rbv"] = MM(env["bkB2"][0:64, 0:512],
                                SMALL[0:1, ONES_B:ONES_B + 64],
                                BIG[0:1, 32832:32832 + 512],
                                start=True, stop=True, inc=True)

            def s_final():
                ocol = L1Q + (h // 2) * T + qt * TT
                b.wait("dve", "pe", env["rbv"])
                if h % 2 == 0:
                    cv = DVE("scalar_tensor_tensor", BIG[0:64, ocol:ocol + TT],
                             SCR[0:64, SC_Y:SC_Y + 512],
                             STAT[0:64, col_rmsw:col_rmsw + 1],
                             env["bkB2"][0:64, 0:512], ALU.mult, ALU.mult)
                    pool.release(env["iB2"], "dve", cv)
                else:
                    yv2 = DVE("scalar_tensor_tensor", SCR[0:64, SC_U:SC_U + 512],
                              SCR[0:64, SC_Y:SC_Y + 512],
                              STAT[0:64, col_rmsw:col_rmsw + 1],
                              env["bkB2"][0:64, 0:512], ALU.mult, ALU.mult)
                    pool.release(env["iB2"], "dve", yv2)
                    bkR, iR = pool.acquire()
                    b.wait("pe", "dve", yv2)
                    rrv = MM(bkR[64:128, 0:512], IDT[0:64, 0:64],
                             SCR[0:64, SC_U:SC_U + 512], start=True, stop=True,
                             inc=True, tp=(0, 64))
                    b.wait("dve", "pe", rrv)
                    cv = DVE("tensor_copy", BIG[64:128, ocol:ocol + TT],
                             bkR[64:128, 0:512])
                    pool.release(iR, "dve", cv)
                ocell[0] = cv
                fin1_cell[0] = cv

            return pend + [s_bc1, s_bc2, s_t1, s_u, s_ss,
                           s_ln, s_exp, s_rb, s_final]

        def finish():
            for e in ("pe", "act", "dve", "gp"):
                b.wait("sp", e, b.ctr[e])
            ytot = 0
            for c in range(NCH):
                ytot = DMA("sp", "d_y", yT[c * 128:(c + 1) * 128, :],
                           X[:, c * T:(c + 1) * T])
            if debug:
                dtot = DMA("sp", "d_dbg", dbg_xn[:, :], XN[:, :])
                dtot = DMA("sp", "d_dbg", dbg_big[:, :], BIG[:, :])
                dtot = DMA("sp", "d_dbg", dbg_stat[:, :], STAT[:, :])
                b.wait("sp", "d_dbg", dtot)
            b.wait("sp", "d_y", ytot)

        done = [False]

        def phase_end(name):
            if upto == name:
                finish()
                done[0] = True
            return done[0]

        while True:
            ln0_v, _, ln0_tevs = layernorm(0, None)
            if phase_end("ln0"):
                break

            vt0_ones = vt_ones(L0VT, 1040)
            qd = projection("wq0", list(range(8)), evac_to(L0Q),
                            rhs_tevs=ln0_tevs)
            lq0 = bscan(L0Q, NCH, HN_Q, qd)
            kd = projection("wk0", list(range(8)), evac_to(L0K))
            lk0 = bscan(L0K, NCH, HN_K, kd)
            b.wait("act", "dve", vt0_ones)
            v_project("wv0", list(range(8)), L0VT, 2, 1040)
            negB0_v = bfinal(0, 0.125, lq0, lk0)

            qst = {"mx_last": None}

            def quant_scan_step(hb):
                def f():
                    if hb == 0:
                        qst["mx_last"] = DVE("memset",
                                             STAT[:, col_mx:col_mx + 1], 0.0)
                    b.wait("sp", scrf_st["free"][0], scrf_st["free"][1])
                    qs_tot = DMA("sp", "d_qs", SCRF[:, 0:512],
                                 dr["ker0"][hb // 2, :,
                                            (hb % 2) * 512:(hb % 2) * 512 + 512])
                    b.wait("dve", "d_qs", qs_tot)
                    b.wait("dve", "dve", qst["mx_last"])
                    rv = DVE("tensor_reduce", STAT[:, col_mxt:col_mxt + 1],
                             SCRF[:, 0:512], AX.X, ALU.max,
                             apply_absolute_value=True)
                    b.wait("dve", "dve", rv)
                    qst["mx_last"] = DVE("tensor_max",
                                         STAT[:, col_mx:col_mx + 1],
                                         STAT[:, col_mx:col_mx + 1],
                                         STAT[:, col_mxt:col_mxt + 1])
                    scrf_st["free"] = ("dve", rv)
                return f

            for hb in range(16):
                ffnq.append(quant_scan_step(hb))
            qsc = {}

            def quant_tail():
                bmx, imx = pool.acquire()
                b.wait("pe", "dve", qst["mx_last"])
                b.wait("pe", "d_pre", pre_tot)
                tmx = MM(bmx[0:1, 0:128], STAT[:, col_mx:col_mx + 1], IDF[:, :],
                         start=True, stop=True, inc=True)
                b.wait("dve", "pe", tmx)
                mxr = DVE("tensor_reduce", STAT[0:1, p_t1:p_t1 + 1],
                          bmx[0:1, 0:128], AX.X, ALU.max)
                pool.release(imx, "dve", mxr)
                b.wait("act", "dve", mxr)
                sv2 = ACT(STAT[0:1, p_idx:p_idx + 1], STAT[0:1, p_t1:p_t1 + 1],
                          AF.Copy, scale=1.0 / 7.0)
                b.wait("dve", "act", sv2)
                iv = DVE("reciprocal", STAT[0:1, p_m:p_m + 1],
                         STAT[0:1, p_idx:p_idx + 1])
                bki, ii = pool.acquire()
                bks, is_ = pool.acquire()
                b.wait("pe", "dve", iv)
                b.wait("pe", "dve", ov1)
                v1 = MM(bki[0:128, 0:1], STAT[0:1, ONES_F:ONES_F + 128],
                        STAT[0:1, p_m:p_m + 1], start=True, stop=True, inc=True)
                b.wait("pe", "act", sv2)
                v2 = MM(bks[0:128, 0:1], STAT[0:1, ONES_F:ONES_F + 128],
                        STAT[0:1, p_idx:p_idx + 1], start=True, stop=True,
                        inc=True)
                b.wait("dve", "pe", v1)
                qsc["isc"] = DVE("tensor_copy", STAT[:, col_isc:col_isc + 1],
                                 bki[0:128, 0:1])
                b.wait("dve", "pe", v2)
                qsc["sc"] = DVE("tensor_copy", STAT[:, col_sc:col_sc + 1],
                                bks[0:128, 0:1])
                pool.release(ii, "dve", qsc["isc"])
                pool.release(is_, "dve", qsc["sc"])

            ffnq.append(quant_tail)

            if phase_end("qkv0"):
                break

            ffn_st = {"cons": wslot["cons"][0], "x_last": None}

            def ffn0_band_substeps(mb):
                bb = {}

                def s_q(half):
                    def f():
                        b.wait("sp", scrf_st["free"][0], scrf_st["free"][1])
                        qs_tot = DMA("sp", "d_qs", SCRF[:, 0:512],
                                     dr["ker0"][mb, :, half * 512:half * 512 + 512])
                        b.wait("dve", "d_qs", qs_tot)
                        b.wait("dve", "dve", qsc["isc"])
                        if ffn_st["cons"] is not None:
                            b.wait("dve", ffn_st["cons"][0], ffn_st["cons"][1])
                        t1v = DVE("tensor_scalar_mul", SCRF[:, 0:512],
                                  SCRF[:, 0:512], STAT[:, col_isc:col_isc + 1])
                        b.wait("dve", "dve", t1v)
                        t2v = DVE("tensor_scalar_add", SCRF[:, 0:512],
                                  SCRF[:, 0:512], MAGIC)
                        b.wait("dve", "dve", t2v)
                        t3v = DVE("tensor_scalar_sub", SCRF[:, 0:512],
                                  SCRF[:, 0:512], MAGIC)
                        b.wait("dve", "dve", t3v)
                        bb["qh%d" % half] = DVE(
                            "tensor_scalar", WB[:, half * 512:half * 512 + 512],
                            SCRF[:, 0:512], -8.0, 7.0, ALU.max, ALU.min)
                        scrf_st["free"] = ("dve", bb["qh%d" % half])
                    return f

                def s_mm(jp, cg):
                    def f():
                        if cg == 0:
                            jpr = pool.acquire_pair()
                            bb["banks%d" % jp] = [
                                (ps[2 * jpr], 2 * jpr),
                                (ps[2 * jpr + 1], 2 * jpr + 1)]
                        b.wait("pe", "dve", bb["qh%d" % cg])
                        banks = bb["banks%d" % jp]
                        for c in range(4 * cg, 4 * cg + 4):
                            for jj in range(2):
                                j = jp * 2 + jj
                                bb["lastv"] = MM(
                                    banks[jj][0][0:128, 0:TT],
                                    WB[:, c * 128:c * 128 + 128],
                                    XN[:, c * T + j * TT: c * T + j * TT + TT],
                                    start=(c == 0), stop=(c == NCH - 1),
                                    inc=(c == NCH - 1 and jj == 1))
                    return f

                def s_ev(jp):
                    def f():
                        for jj in range(2):
                            j = jp * 2 + jj
                            bk, ib = bb["banks%d" % jp][jj]
                            b.wait("dve", "pe", bb["lastv"])
                            b.wait("dve", "dve", qsc["sc"])
                            if ffn_st["x_last"] is not None:
                                b.wait("dve", "dve", ffn_st["x_last"])
                            ffn_st["x_last"] = DVE(
                                "scalar_tensor_tensor",
                                X[:, mb * T + j * TT: mb * T + j * TT + TT],
                                bk[0:128, 0:TT], STAT[:, col_sc:col_sc + 1],
                                X[:, mb * T + j * TT: mb * T + j * TT + TT],
                                ALU.mult, ALU.add)
                            pool.release(ib, "dve", ffn_st["x_last"])
                        if jp == 1:
                            ffn_st["cons"] = ("pe", bb["lastv"])
                    return f

                def s_bias():
                    b.wait("dve", "d_pre", pre_tot)
                    b.wait("dve", "dve", ffn_st["x_last"])
                    ffn_st["x_last"] = DVE(
                        "tensor_scalar_add", X[:, mb * T:(mb + 1) * T],
                        X[:, mb * T:(mb + 1) * T],
                        STAT[:, col_bias0 + mb:col_bias0 + mb + 1])

                return [s_q(0), s_q(1), s_mm(0, 0), s_mm(0, 1), s_ev(0),
                        s_mm(1, 0), s_mm(1, 1), s_ev(1), s_bias]

            for mb in range(8):
                ffnq.extend(ffn0_band_substeps(mb))

            wslot["par"] = 0
            pt_setup([(SCR, PT[0]), (SCR, PT[1]), (WB, 1024), (WB, 1536)],
                     init_cons=wslot["cons"][1])
            ocell = [None]
            pend = None
            for h in range(_NHEADS):
                for qt in range(NTT):
                    pend = attn_l0(h, qt, negB0_v, pend, ocell,
                                   allow_ffn=(h * NTT + qt >= 6))
            for st_ in pend:
                st_()
            while ffnq:
                ffnq.pop(0)()
            ocv = ocell[0]
            ptc = [c for c in pt_state["cons"][2:] if c is not None]
            if ptc:
                wslot["cons"][1] = ("pe", max(v for _, v in ptc))
            wslot["cons"][0] = ffn_st["cons"]
            x_last = ffn_st["x_last"]

            cA = wslot["cons"][0]
            if cA is not None:
                b.wait("gp", cA[0], cA[1])
            ta = DMA("gp", wslot["sem"][0], WB[:, 0:128], dr["a0"][:, :])
            cB = wslot["cons"][1]
            if cB is not None:
                b.wait("gp", cB[0], cB[1])
            tb = DMA("gp", wslot["sem"][1], WB[0:16, 1024:2048], dr["b0"][:, :])
            b.wait("pe", wslot["sem"][0], ta)
            xa_banks = [pool.acquire() for _ in range(NTT)]
            lastxa = None
            for c in range(NCH):
                for j in range(NTT):
                    lastxa = MM(xa_banks[j][0][0:16, 0:TT],
                                WB[:, c * 16:c * 16 + 16],
                                XN[:, c * T + j * TT: c * T + j * TT + TT],
                                start=(c == 0), stop=(c == NCH - 1),
                                inc=(c == NCH - 1 and j == NTT - 1))
            wslot["cons"][0] = ("pe", lastxa)
            xa_evs = []
            for j in range(NTT):
                bk, ib = xa_banks[j]
                b.wait("act", "pe", lastxa)
                ev = ACT(SCR[0:16, j * TT:(j + 1) * TT], bk[0:16, 0:TT], AF.Copy)
                pool.release(ib, "act", ev)
                xa_evs.append(ev)
            b.wait("pe", wslot["sem"][1], tb)
            lora_last = None
            for mb in range(8):
                banks = [pool.acquire() for _ in range(NTT)]
                lastv = None
                for j in range(NTT):
                    b.wait("pe", "act", xa_evs[j])
                    lastv = MM(banks[j][0][0:128, 0:TT],
                               WB[0:16, 1024 + mb * 128:1024 + mb * 128 + 128],
                               SCR[0:16, j * TT:(j + 1) * TT],
                               start=True, stop=True, inc=(j == NTT - 1))
                if mb == 7:
                    wslot["cons"][1] = ("pe", lastv)
                for j in range(NTT):
                    bk, ib = banks[j]
                    b.wait("dve", "pe", lastv)
                    b.wait("dve", "dve", x_last)
                    x_last = DVE("tensor_add",
                                 X[:, mb * T + j * TT: mb * T + j * TT + TT],
                                 X[:, mb * T + j * TT: mb * T + j * TT + TT],
                                 bk[0:128, 0:TT])
                    pool.release(ib, "dve", x_last)
            if phase_end("attn0"):
                break

            x_last = out_proj("wp0", ("dve", ocv), L0Q)
            if phase_end("l0"):
                break

            ln1_v, kept, _ = layernorm(1, ("dve", x_last))
            bk_m, i_m, bk_r, i_r = kept

            xl_ap = X.ap().rearrange("p (c t) -> p c t", t=T)[:, :, T - 1:T]
            xnl_v = DVE("tensor_scalar", STAT[:, col_b1s:col_b1s + 8], xl_ap,
                        bk_m[0:128, 511:512], bk_r[0:128, 511:512],
                        ALU.subtract, ALU.mult)
            b.wait("dve", "dve", xnl_v)
            dbgc = DVE("tensor_copy", STAT[:, 796:804], STAT[:, col_b1s:col_b1s + 8])
            dbgc = DVE("tensor_copy", STAT[:, 804:805], bk_m[0:128, 511:512])
            dbgc = DVE("tensor_copy", STAT[:, 805:806], bk_r[0:128, 511:512])
            pool.release(i_m, "dve", dbgc)
            pool.release(i_r, "dve", dbgc)
            bz1, iz1 = pool.acquire()
            lastz1 = None
            for half in range(2):
                b.wait("sp", scrf_st["free"][0], scrf_st["free"][1])
                qs_tot = DMA("sp", "d_qs", SCRF[:, 0:512],
                             dr["wp1"][:, half * 512:half * 512 + 512])
                b.wait("pe", "d_qs", qs_tot)
                b.wait("pe", "dve", xnl_v)
                for cc in range(4):
                    c = half * 4 + cc
                    lastz1 = MM(bz1[0:128, 0:1], SCRF[:, cc * 128:cc * 128 + 128],
                                STAT[:, col_b1s + c:col_b1s + c + 1],
                                start=(c == 0), stop=(c == 7),
                                inc=(cc == 3))
                scrf_st["free"] = ("pe", lastz1)
            b.wait("act", "pe", lastz1)
            b.wait("act", "d_pre", pre_tot)
            g_v = ACT(STAT[:, col_g:col_g + 1], bz1[0:128, 0:1], AF.Gelu,
                      bias=STAT[:, col_bp1:col_bp1 + 1], scale=1.0)
            pool.release(iz1, "act", g_v)
            bz2, iz2 = pool.acquire()
            b.wait("pe", "act", g_v)
            z2_v = MM(bz2[0:1, 0:8], STAT[:, col_g:col_g + 1],
                      STAT[:, col_wp2:col_wp2 + 8], start=True, stop=True, inc=True)
            b.wait("dve", "pe", z2_v)
            lg_v = DVE("tensor_add", STAT[0:1, p_log:p_log + 8], bz2[0:1, 0:8],
                       STAT[0:1, p_bp2:p_bp2 + 8])
            pool.release(iz2, "dve", lg_v)
            b.wait("dve", "dve", lg_v)
            m_v = DVE("tensor_reduce", STAT[0:1, p_m:p_m + 1],
                      STAT[0:1, p_log:p_log + 8], AX.X, ALU.max)
            b.wait("dve", "dve", m_v)
            eq_v = DVE("tensor_scalar", STAT[0:1, p_eq:p_eq + 8],
                       STAT[0:1, p_log:p_log + 8], STAT[0:1, p_m:p_m + 1], None,
                       ALU.is_equal)
            b.wait("dve", "dve", eq_v)
            c_v = DVE("tensor_mul", STAT[0:1, p_cand:p_cand + 8],
                      STAT[0:1, p_eq:p_eq + 8], STAT[0:1, p_io99:p_io99 + 8])
            b.wait("dve", "dve", c_v)
            c2_v = DVE("tensor_scalar_add", STAT[0:1, p_cand:p_cand + 8],
                       STAT[0:1, p_cand:p_cand + 8], 99.0)
            b.wait("dve", "dve", c2_v)
            ix_v = DVE("tensor_reduce", STAT[0:1, p_idx:p_idx + 1],
                       STAT[0:1, p_cand:p_cand + 8], AX.X, ALU.min)
            b.wait("dve", "dve", ix_v)
            ixi_v = DVE("tensor_copy", IDXI[0:1, 0:1], STAT[0:1, p_idx:p_idx + 1])

            selcell = {}

            def selfn(e):
                r0 = e.alloc_register("selreg")
                e.reg_load(r0, IDXI[0:1, 0:1])
                selcell["r0"] = r0
                selcell["v"] = e.snap(r0, min_val=0, max_val=7)
                selcell["e"] = e
                return None

            def fresh_sel():
                return selcell["e"].snap(selcell["r0"], min_val=0, max_val=7)
            b.wait("gp", "dve", ixi_v)
            b.op("gp", selfn, inc=False)

            def dynfn(out_ap, src, tag=""):
                def fn(e, o=out_ap, s=src, t=tag):
                    try:
                        return e.dma_start(out=o, in_=s())
                    except Exception:
                        print("DYN DMA FAIL tag=", t)
                        raise
                return fn
            b.wait("gp", "pe", lastz1)
            sm_tot = b.dma("gp", "d_sm", dynfn(
                STAT[:, col_b1s:col_b1s + 16],
                lambda: dr["b1e"][bass.ds(selcell["v"], 1), :].rearrange(
                    "e (p q) -> (e p) q", p=128)))
            sm_tot = b.dma("gp", "d_sm", dynfn(
                STAT[:, col_b2s:col_b2s + 8],
                lambda: dr["b2e"][bass.ds(selcell["v"], 1), :].rearrange(
                    "e (p q) -> (e p) q", p=128)))

            b.wait("act", "d_pre", pre_tot)
            lam_v = ACT(STAT[64:65, 160:176], STAT[64:65, 176:192], AF.Sigmoid)

            for c in range(NCH):
                b.wait("dve", "dve", ln1_v)
                b.wait("dve", "dve", x_last)
                x_last = DVE("tensor_add", X[:, c * T:(c + 1) * T],
                             X[:, c * T:(c + 1) * T], XN[:, c * T:(c + 1) * T])
            if phase_end("router"):
                break

            b.wait("act", "d_sm", sm_tot)
            h_done = None
            for grp_i in range(8):
                parA, cA = slot_begin()
                parB, cB = slot_begin()
                for cs in (cA, cB):
                    if cs is not None:
                        b.wait("gp", cs[0], cs[1])
                w1tot = b.dma("gp", wslot["sem"][0], dynfn(
                    WB[:, 0:2048],
                    lambda g=grp_i: dr["w1e"][bass.ds(selcell["v"], 1), :].rearrange(
                        "e (m p x) -> (e p) m x", m=16, p=128)[:, 2 * g:2 * g + 2, :],
                    tag=f"w1 grp={grp_i}"))
                b.wait("pe", wslot["sem"][0], w1tot)
                lastv = None
                for sub in range(2):
                    mb = 2 * grp_i + sub
                    banks = [pool.acquire() for _ in range(NTT)]
                    for c in range(NCH):
                        for j in range(NTT):
                            lastv = MM(banks[j][0][0:128, 0:TT],
                                       WB[:, sub * 1024 + c * 128:
                                          sub * 1024 + c * 128 + 128],
                                       XN[:, c * T + j * TT: c * T + j * TT + TT],
                                       start=(c == 0), stop=(c == NCH - 1),
                                       inc=(c == NCH - 1 and j == NTT - 1))
                    for j in range(NTT):
                        bk, ib = banks[j]
                        b.wait("act", "pe", lastv)
                        h_done = ACT(BIG[:, L1H + mb * T + j * TT:
                                         L1H + mb * T + j * TT + TT],
                                     bk[0:128, 0:TT], AF.Gelu,
                                     bias=STAT[:, col_b1s + mb:col_b1s + mb + 1],
                                     scale=1.0)
                        pool.release(ib, "act", h_done)
                band_release(parA, "pe", lastv)
                band_release(parB, "pe", lastv)
            rs_done = None
            for mb in range(8):
                parA, cA = slot_begin()
                parB, cB = slot_begin()
                for cs in (cA, cB):
                    if cs is not None:
                        b.wait("gp", cs[0], cs[1])
                w2tot = b.dma("gp", wslot["sem"][1], dynfn(
                    WB[:, 0:2048],
                    lambda mb=mb: dr["w2e"][
                        bass.ds(selcell["v"], 1),
                        mb * 262144:(mb + 1) * 262144].rearrange(
                        "e (p q) -> (e p) q", p=128),
                    tag=f"w2 mb={mb}"))
                banks = [pool.acquire() for _ in range(NTT)]
                lastv = None
                b.wait("pe", "act", h_done)
                b.wait("pe", wslot["sem"][1], w2tot)
                for c in range(16):
                    for j in range(NTT):
                        lastv = MM(banks[j][0][0:128, 0:TT],
                                   WB[:, c * 128:c * 128 + 128],
                                   BIG[:, L1H + c * T + j * TT:
                                       L1H + c * T + j * TT + TT],
                                   start=(c == 0), stop=(c == 15),
                                   inc=(c == 15 and j == NTT - 1))
                band_release(parA, "pe", lastv)
                band_release(parB, "pe", lastv)
                for j in range(NTT):
                    bk, ib = banks[j]
                    b.wait("dve", "pe", lastv)
                    b.wait("dve", "d_sm", sm_tot)
                    rs_done = DVE("tensor_single_scalar",
                                  BIG[:, L1RS + mb * T + j * TT:
                                      L1RS + mb * T + j * TT + TT],
                                  bk[0:128, 0:TT],
                                  STAT[:, col_b2s + mb:col_b2s + mb + 1], ALU.add)
                    pool.release(ib, "dve", rs_done)
            bo3_v = DVE("tensor_scalar_mul", STAT[:, col_bo3:col_bo3 + 8],
                        STAT[:, col_bo:col_bo + 8], 0.3)
            for mb in range(8):
                par, sem, tot = band_load(dr["wof"][mb:mb + 1, :, :], cols=1024)
                b.wait("pe", sem, tot)
                b.wait("pe", "dve", rs_done)
                banks = [pool.acquire() for _ in range(NTT)]
                lastv = None
                for c in range(NCH):
                    for j in range(NTT):
                        lastv = MM(banks[j][0][0:128, 0:TT],
                                   WB[:, par * 1024 + c * 128:
                                      par * 1024 + c * 128 + 128],
                                   BIG[:, L1RS + c * T + j * TT:
                                       L1RS + c * T + j * TT + TT],
                                   start=(c == 0), stop=(c == NCH - 1),
                                   inc=(c == NCH - 1 and j == NTT - 1))
                band_release(par, "pe", lastv)
                for j in range(NTT):
                    bk, ib = banks[j]
                    b.wait("dve", "pe", lastv)
                    b.wait("dve", "dve", x_last)
                    x_last = DVE("scalar_tensor_tensor",
                                 X[:, mb * T + j * TT: mb * T + j * TT + TT],
                                 bk[0:128, 0:TT], 0.3,
                                 X[:, mb * T + j * TT: mb * T + j * TT + TT],
                                 ALU.mult, ALU.add)
                    pool.release(ib, "dve", x_last)
                b.wait("dve", "dve", bo3_v)
                b.wait("dve", "dve", x_last)
                x_last = DVE("tensor_scalar_add", X[:, mb * T:(mb + 1) * T],
                             X[:, mb * T:(mb + 1) * T],
                             STAT[:, col_bo3 + mb:col_bo3 + mb + 1])
            if phase_end("ffn1"):
                break

            b.wait("sp", "pe", b.ctr["pe"])
            cs_tot = DMA("sp", "d_cs", BIG[:, L1CS:L1CS + 2 * T], dr["cs"][:, :])
            b.wait("dve", "pe", b.ctr["pe"])
            vt1_ones = vt_ones(L1VT, 260)

            shuf = []
            for i2 in range(0, 32, 2):
                shuf += [i2 + 1, i2]

            def rope(bases, ready):
                done = None
                b.wait("dve", "d_cs", cs_tot)
                if ready is not None:
                    b.wait("dve", ready[0], ready[1])
                for base in bases:
                    for hf in range(2):
                        o = base + hf * 1024
                        co = L1CS + hf * 1024
                        so = L1CS + T + hf * 1024
                        sh_v = DVE("stream_shuffle", SCR[:, SC_Y:SC_Y + 1024],
                                   BIG[:, o:o + 1024], shuf)
                        b.wait("dve", "dve", sh_v)
                        s_v = DVE("tensor_mul", SCR[:, SC_Y:SC_Y + 1024],
                                  SCR[:, SC_Y:SC_Y + 1024],
                                  BIG[:, so:so + 1024])
                        c_v2 = DVE("tensor_mul", BIG[:, o:o + 1024],
                                   BIG[:, o:o + 1024], BIG[:, co:co + 1024])
                        b.wait("dve", "dve", s_v)
                        b.wait("dve", "dve", c_v2)
                        done = DVE("tensor_add", BIG[:, o:o + 1024],
                                   BIG[:, o:o + 1024], SCR[:, SC_Y:SC_Y + 1024])
                return done

            lq1 = lk1 = None
            for mb in range(8):
                qd = projection("wq1", [mb], evac_to(L1Q))
                ropeq_v = rope([L1Q + mb * T], qd)
                lq1 = bscan(L1Q + mb * T, 1, HN_Q, ("dve", ropeq_v),
                            init=(mb == 0))
            ropek_v = None
            for mb in range(2):
                kd = projection("wk1", [mb], evac_to(L1KA))
                ropek_v = rope([L1KA + mb * T], kd)
                lk1 = bscan(L1KA + mb * T, 1, HN_K, ("dve", ropek_v),
                            init=(mb == 0))
            b.wait("sp", "dve", ropek_v)
            kb_tot = 0
            for cc in range(2):
                kb_tot = DMA("sp", "d_kb",
                             BIG[0:64, L1KB + cc * T:L1KB + (cc + 1) * T],
                             BIG[64:128, L1KA + cc * T:L1KA + (cc + 1) * T])
                kb_tot = DMA("sp", "d_kb",
                             BIG[64:128, L1KB + cc * T:L1KB + (cc + 1) * T],
                             BIG[0:64, L1KA + cc * T:L1KA + (cc + 1) * T])
            b.wait("act", "dve", vt1_ones)
            v_project("wv1", list(range(2)), L1VT, 2, 260)
            negB1_v = bfinal(1, 32 ** -0.5, lq1, lk1)
            if phase_end("qkv1"):
                break

            pt_setup([(BIG, PT1_BASE), (BIG, PT1_BASE + 1024),
                      (BIG, PT1_BASE + 2048), (BIG, PT1_BASE + 3072)])
            ocell = [None]
            pend = None
            for h in range(_NHEADS1):
                for qt in range(NTT):
                    pend = attn_l1(h, qt, negB1_v, kb_tot, lam_v, pend, ocell)
            for st_ in pend:
                st_()
            ocv = ocell[0]
            if phase_end("attn1"):
                break

            x_last = out_proj("wo1", ("dve", ocv), L1Q)
            finish()
            break

        def run_stream(eng_name):
            def runner(e):
                for idx_it, item in enumerate(b.stream[eng_name]):
                    kind = item[0]
                    try:
                        if kind == "op":
                            _, fn, inc_eng = item
                            inst = fn(e)
                            if inc_eng is not None:
                                inst.then_inc(sems["c_" + inc_eng], 1)
                        elif kind == "wait":
                            _, sem, val = item
                            key = sems[sem] if sem in sems else sems["c_" + sem]
                            e.wait_ge(key, val)
                        else:
                            _, fn, sem = item
                            inst = fn(e)
                            inst.then_inc(sems[sem], 16)
                    except Exception:
                        import traceback
                        print(f"REPLAY FAIL {eng_name}[{idx_it}] kind={kind}")
                        traceback.print_exc()
                        raise
            return runner

        with nc.allow_low_precision(reason="bf16 elementwise on bf16 matmul operands"), \
             nc.Block() as block:
            block.tensor(run_stream("pe"))
            block.scalar(run_stream("act"))
            block.vector(run_stream("dve"))
            block.sync(run_stream("sp"))
            block.gpsimd(run_stream("gp"))

    nc.finalize()
    return nc



def _img(A):
    K, F = A.shape
    kc, mb = K // 128, F // 128
    return np.ascontiguousarray(
        A.reshape(kc, 128, mb, 128).transpose(2, 1, 0, 3)).reshape(mb, 128, kc * 128)


def _rope_tables():
    j = np.arange(16, dtype=np.float64)
    invf = 10000.0 ** (-j / 16.0)
    t = np.arange(T, dtype=np.float64)
    ang = np.outer(invf, t)
    rows_j = (np.arange(128) % 32) // 2
    C = np.cos(ang)[rows_j]
    S = np.sin(ang)[rows_j]
    sign = np.where(np.arange(128) % 2 == 0, -1.0, 1.0)[:, None]
    return np.concatenate([C, S * sign], axis=1).astype(ml_dtypes.bfloat16)


def prep_inputs(L0_attn_x, L0_attn_w_attn, L0_attn_w_proj, L0_ffn_kernel,
                L0_ffn_bias, L0_ffn_a, L0_ffn_b, L1_attn_Wq, L1_attn_Wkv,
                L1_attn_Wout, L1_attn_lambda_init, L1_attn_rms_weight,
                L1_ffn_Wp1, L1_ffn_bp1, L1_ffn_Wp2, L1_ffn_bp2, L1_ffn_W1,
                L1_ffn_b1, L1_ffn_W2, L1_ffn_b2, L1_ffn_Wo, L1_ffn_bo):
    f = np.asarray
    x = f(L0_attn_x, dtype=np.float32)
    Aqkv = np.ascontiguousarray(f(L0_attn_w_attn, dtype=np.float32).T)
    w = {
        "wq0": _img(Aqkv[:, 0:1024]),
        "wk0": _img(Aqkv[:, 1024:2048]),
        "wv0": _img(Aqkv[:, 2048:3072]),
        "wp0": _img(np.ascontiguousarray(f(L0_attn_w_proj, dtype=np.float32).T)),
        "ker0": _img(f(L0_ffn_kernel, dtype=np.float32)),
        "b0": np.ascontiguousarray(f(L0_ffn_b, dtype=np.float32)),
        "bias0": np.ascontiguousarray(
            f(L0_ffn_bias, dtype=np.float32).reshape(8, 128).T),
        "wq1": _img(np.ascontiguousarray(f(L1_attn_Wq, dtype=np.float32).T)),
        "wo1": _img(np.ascontiguousarray(f(L1_attn_Wout, dtype=np.float32).T)),
        "lam": f(L1_attn_lambda_init, dtype=np.float32).reshape(1, 16),
        "rmsw": f(L1_attn_rms_weight, dtype=np.float32).reshape(64, 1),
        "wp1": _img(f(L1_ffn_Wp1, dtype=np.float32))[0],
        "wp2": f(L1_ffn_Wp2, dtype=np.float32),
        "bp1": f(L1_ffn_bp1, dtype=np.float32).reshape(128, 1),
        "bp2": f(L1_ffn_bp2, dtype=np.float32).reshape(1, 8),
        "b1e": np.stack([np.ascontiguousarray(
            f(L1_ffn_b1, dtype=np.float32)[e].reshape(16, 128).T).reshape(-1)
            for e in range(8)]),
        "b2e": np.stack([np.ascontiguousarray(
            f(L1_ffn_b2, dtype=np.float32)[e].reshape(8, 128).T).reshape(-1)
            for e in range(8)]),
        "wof": _img(f(L1_ffn_Wo, dtype=np.float32)),
        "bof": np.ascontiguousarray(f(L1_ffn_bo, dtype=np.float32).reshape(8, 128).T),
    }
    a = f(L0_ffn_a, dtype=np.float32)
    w["a0"] = np.ascontiguousarray(
        a.reshape(8, 128, 16).transpose(1, 0, 2)).reshape(128, 128)
    Akv = np.ascontiguousarray(f(L1_attn_Wkv, dtype=np.float32).T)
    w["wk1"] = _img(Akv[:, 0:256])
    w["wv1"] = _img(Akv[:, 256:512])
    W1 = f(L1_ffn_W1, dtype=np.float32)
    w["w1e"] = np.stack([_img(W1[e]).reshape(-1) for e in range(8)])
    W2 = f(L1_ffn_W2, dtype=np.float32)
    w["w2e"] = np.stack([_img(W2[e]).reshape(-1) for e in range(8)])
    w["cs"] = _rope_tables()
    kk, qq = np.meshgrid(np.arange(128), np.arange(128), indexing="ij")
    w["mask"] = (qq >= kk).astype(ml_dtypes.bfloat16)
    w["maskn"] = np.where(qq > kk, np.float32(-1e9),
                          np.float32(0)).astype(ml_dtypes.bfloat16)
    w["ident"] = np.eye(128, dtype=ml_dtypes.bfloat16)
    w["identf"] = np.eye(128, dtype=np.float32)
    w["grp2"] = (np.arange(128)[:, None] // 64 ==
                 np.arange(2)[None, :]).astype(ml_dtypes.bfloat16)
    w["grp4"] = (np.arange(128)[:, None] // 32 ==
                 np.arange(4)[None, :]).astype(ml_dtypes.bfloat16)
    w["io99"] = (np.arange(8, dtype=np.float32) - 99.0).reshape(1, 8)
    in_maps = []
    for core in range(8):
        m = {"xT": np.ascontiguousarray(x[core].T)}
        m.update(w)
        in_maps.append(m)
    return in_maps


def kernel(**inputs):
    in_maps = prep_inputs(**inputs)
    nc = build_nc(upto=_UPTO, debug=_DEBUG)
    res = run_bass_kernel_spmd(nc, in_maps, core_ids=list(range(8)))
    out = np.stack([np.ascontiguousarray(r["yT"].T) for r in res.results])
    return out.astype(np.float32)


# version 22
# speedup vs baseline: 1.0124x; 1.0124x over previous
import contextlib

import numpy as np
import ml_dtypes

import concourse.bass as bass
import concourse.mybir as mybir
from concourse.bass_utils import run_bass_kernel_spmd

F32 = mybir.dt.float32
BF16 = mybir.dt.bfloat16
I32 = mybir.dt.int32
AF = mybir.ActivationFunctionType
ALU = mybir.AluOpType
AX = mybir.AxisListType

D = 1024
T = 2048
NCH = 8
NTT = 4
TT = 512
MAGIC = 12582912.0

_UPTO = "all"
_DEBUG = False
_NHEADS = 16
_NHEADS1 = 16

ENGS = ("pe", "act", "dve", "sp", "gp")


class Builder:
    def __init__(self):
        self.stream = {e: [] for e in ENGS}
        self.ctr = {e: 0 for e in ENGS}
        self.dmatot = {}
        self.waited = {}

    def op(self, eng, fn, inc=False):
        if inc:
            self.ctr[eng] += 1
            self.stream[eng].append(("op", fn, eng))
            return self.ctr[eng]
        self.stream[eng].append(("op", fn, None))
        return None

    def wait(self, eng, sem, val):
        if val is None or val <= 0:
            return
        key = (eng, sem)
        if self.waited.get(key, 0) >= val:
            return
        self.waited[key] = val
        self.stream[eng].append(("wait", sem, val))

    def dma(self, eng, sem, fn):
        self.dmatot[sem] = self.dmatot.get(sem, 0) + 16
        self.stream[eng].append(("dma", fn, sem))
        return self.dmatot[sem]


class BankPool:
    def __init__(self, b, tensors):
        self.b = b
        self.tensors = tensors
        self.free_q = list(range(len(tensors)))
        self.cons = {i: None for i in range(len(tensors))}

    def acquire(self):
        i = self.free_q.pop(0)
        c = self.cons[i]
        if c is not None:
            self.b.wait("pe", c[0], c[1])
        return self.tensors[i], i

    def acquire_pair(self):
        order = [i // 2 for i in self.free_q]
        cand = [j for k, j in enumerate(order) if j in order[k + 1:]]
        for j in cand + list(range(len(self.tensors) // 2)):
            if 2 * j in self.free_q and 2 * j + 1 in self.free_q:
                self.free_q.remove(2 * j)
                self.free_q.remove(2 * j + 1)
                for i in (2 * j, 2 * j + 1):
                    c = self.cons[i]
                    if c is not None:
                        self.b.wait("pe", c[0], c[1])
                return j
        raise RuntimeError("no aligned psum pair free")

    def acquire_deferred(self):
        i = self.free_q.pop(0)
        return self.tensors[i], i, self.cons[i]

    def acquire_pair_deferred(self):
        order = [i // 2 for i in self.free_q]
        cand = [j for k, j in enumerate(order) if j in order[k + 1:]]
        for j in cand + list(range(len(self.tensors) // 2)):
            if 2 * j in self.free_q and 2 * j + 1 in self.free_q:
                self.free_q.remove(2 * j)
                self.free_q.remove(2 * j + 1)
                return j, [self.cons[2 * j], self.cons[2 * j + 1]]
        raise RuntimeError("no aligned psum pair free")

    def release(self, i, eng, val):
        self.cons[i] = (eng, val)
        self.free_q.append(i)


def build_nc(upto="all", debug=False):
    _NHEADS = globals()["_NHEADS"]
    _NHEADS1 = globals()["_NHEADS1"]
    nc = bass.Bass("TRN2", target_bir_lowering=False)
    dr = {}

    def din(name, shape, dt=F32):
        dr[name] = nc.dram_tensor(name, list(shape), dt, kind="ExternalInput").ap()

    din("xT", (D, T))
    din("wq0", (8, 128, 1024)); din("wk0", (8, 128, 1024)); din("wv0", (8, 128, 1024))
    din("wp0", (8, 128, 1024))
    din("ker0", (8, 128, 1024))
    din("a0", (128, 128)); din("b0", (16, 1024)); din("bias0", (128, 8))
    din("wq1", (8, 128, 1024)); din("wk1", (2, 128, 1024)); din("wv1", (2, 128, 1024))
    din("wo1", (8, 128, 1024))
    din("lam", (1, 16)); din("rmsw", (64, 1))
    din("wp1", (128, 1024)); din("wp2", (128, 8)); din("bp1", (128, 1)); din("bp2", (1, 8))
    din("w1e", (8, 16 * 128 * 1024)); din("w2e", (8, 8 * 128 * 2048))
    din("b1e", (8, 2048)); din("b2e", (8, 1024))
    din("wof", (8, 128, 1024)); din("bof", (128, 8))
    din("cs", (128, 2 * T), BF16)
    din("mask", (128, 128), BF16)
    din("ident", (128, 128), BF16)
    din("identf", (128, 128))
    din("grp2", (128, 2), BF16); din("grp4", (128, 4), BF16)
    din("io99", (1, 8))

    yT = nc.dram_tensor("yT", [D, T], F32, kind="ExternalOutput").ap()
    dbg_xn = dbg_big = dbg_stat = None
    if debug:
        dbg_xn = nc.dram_tensor("dbg_xn", [128, NCH * T], BF16, kind="ExternalOutput").ap()
        dbg_big = nc.dram_tensor("dbg_big", [128, 49408], BF16, kind="ExternalOutput").ap()
        dbg_stat = nc.dram_tensor("dbg_stat", [128, 832], F32, kind="ExternalOutput").ap()

    b = Builder()

    with contextlib.ExitStack() as ctx:
        def sbuf(name, shape, dt):
            return ctx.enter_context(nc.sbuf_tensor(name, list(shape), dt))

        X = sbuf("X", (128, NCH * T), F32)
        XN = sbuf("XN", (128, NCH * T), BF16)
        BIG = sbuf("BIGB", (128, 49408), BF16)
        WB = sbuf("WB", (128, 2048), BF16)
        SCR = sbuf("SCR", (128, 2048), BF16)
        SCRF = sbuf("SCRF", (128, 512), F32)
        STAT = sbuf("STAT", (128, 832), F32)
        SMALL = sbuf("SMALL", (128, 72), BF16)
        MASKT = sbuf("MASKT", (128, 128), BF16)
        IDT = sbuf("IDT", (128, 128), BF16)
        IDF = sbuf("IDF", (128, 128), F32)
        IDXI = sbuf("IDXI", (1, 1), I32)
        ps_big = [ctx.enter_context(nc.psum_tensor(f"psb{i}", [128, 1024], F32))
                  for i in range(4)]
        ps = []
        for i in range(4):
            ps.append(ps_big[i][0:128, 0:512])
            ps.append(ps_big[i][0:128, 512:1024])

        sems = {}
        for e in ENGS:
            sems["c_" + e] = ctx.enter_context(nc.semaphore("c_" + e))
        for d in ("d_pre", "d_x", "d_wa", "d_wb", "d_qs", "d_cs", "d_kb",
                  "d_sm", "d_y", "d_mx", "d_dbg"):
            sems[d] = ctx.enter_context(nc.semaphore(d))

        HN_Q, HN_QT, HN_K, HN_KT = 0, 8, 16, 24
        ONES_F = 32
        RSTD = 160
        _statc = [672]

        def stat_col(n=1):
            c = _statc[0]; _statc[0] += n
            assert _statc[0] <= 744
            return c

        col_negB = [stat_col(), stat_col()]
        col_isc = stat_col(); col_sc = stat_col()
        col_bias0 = stat_col(8)
        col_bo = stat_col(8); col_bo3 = stat_col(8)
        col_b1s = stat_col(16); col_b2s = stat_col(8)
        col_bp1 = stat_col(); col_rmsw = stat_col()
        col_g = stat_col(); col_wp2 = stat_col(8)
        col_mx = stat_col(); col_mxt = stat_col()
        col_eps5 = stat_col(); col_eps6 = stat_col()
        _p0c = [744]

        def p0_col(n=1):
            c = _p0c[0]; _p0c[0] += n
            assert _p0c[0] <= 832
            return c

        p_io99 = p0_col(8); p_bp2 = p0_col(8); p_log = p0_col(8)
        p_eq = p0_col(8); p_cand = p0_col(8)
        p_m = p0_col(); p_idx = p0_col(); p_prod = p0_col(); p_t1 = p0_col()

        SM_G2, SM_G4, ONES_B = 0, 2, 6

        L0Q, L0K, L0VT = 0, 16384, 32768
        L1Q, L1KA, L1KB, L1VT, L1CS = 0, 16384, 20480, 24576, 28736
        L1H, L1RS = 0, 32768
        PT = (0, 512)
        SC_Y, SC_U = 1024, 1536

        pool = BankPool(b, ps)

        def MM(out_ap, lhsT, rhs, start, stop, inc=False, tp=None):
            def fn(e, o=out_ap, l=lhsT, r=rhs, st=start, sp=stop, t=tp):
                return e.matmul(o, l, r, start=st, stop=sp, tile_position=t)
            return b.op("pe", fn, inc=inc)

        def ACT(out_ap, in_ap, func, bias=0.0, scale=1.0, inc=True):
            def fn(e, o=out_ap, i=in_ap, f=func, bi=bias, sc=scale):
                return e.activation(o, i, f, bias=bi, scale=sc)
            return b.op("act", fn, inc=inc)

        def DVE(meth, *args, inc=True, **kw):
            def fn(e, m=meth, a=args, k=kw):
                return getattr(e, m)(*a, **k)
            return b.op("dve", fn, inc=inc)

        def DMA(eng, sem, out_ap, in_ap, **kw):
            def fn(e, o=out_ap, i=in_ap, k=kw):
                return e.dma_start(out=o, in_=i, **k)
            return b.dma(eng, sem, fn)

        xload = 0
        for c in range(NCH):
            xload = DMA("sp", "d_x", X[:, c * T:(c + 1) * T],
                        dr["xT"][c * 128:(c + 1) * 128, :])
        pre = 0
        pre = DMA("sp", "d_pre", MASKT[:, :], dr["mask"][:, :])
        pre = DMA("sp", "d_pre", IDT[:, :], dr["ident"][:, :])
        pre = DMA("sp", "d_pre", IDF[:, :], dr["identf"][:, :])
        pre = DMA("sp", "d_pre", SMALL[:, SM_G2:SM_G2 + 2], dr["grp2"][:, :])
        pre = DMA("sp", "d_pre", SMALL[:, SM_G4:SM_G4 + 4], dr["grp4"][:, :])
        pre = DMA("sp", "d_pre", STAT[0:1, p_io99:p_io99 + 8], dr["io99"][:, :])
        pre = DMA("sp", "d_pre", STAT[0:1, p_bp2:p_bp2 + 8], dr["bp2"][:, :])
        pre = DMA("sp", "d_pre", STAT[:, col_bias0:col_bias0 + 8], dr["bias0"][:, :])
        pre = DMA("sp", "d_pre", STAT[:, col_bo:col_bo + 8], dr["bof"][:, :])
        pre = DMA("sp", "d_pre", STAT[:, col_bp1:col_bp1 + 1], dr["bp1"][:, :])
        pre = DMA("sp", "d_pre", STAT[0:64, col_rmsw:col_rmsw + 1], dr["rmsw"][:, :])
        pre = DMA("sp", "d_pre", STAT[64:65, 176:192], dr["lam"][:, :])
        pre = DMA("sp", "d_pre", STAT[:, col_wp2:col_wp2 + 8], dr["wp2"][:, :])
        pre_tot = pre

        e5_v = DVE("memset", STAT[:, col_eps5:col_eps5 + 1], 1e-5, inc=True)
        e6_v = DVE("memset", STAT[:, col_eps6:col_eps6 + 1], 1e-6, inc=True)
        ov1 = DVE("memset", STAT[:, ONES_F:ONES_F + 128], 1.0, inc=True)
        b.wait("dve", "dve", ov1)
        ones_v = DVE("memset", SMALL[:, ONES_B:ONES_B + 64], 1.0, inc=True)

        def vt_ones(base, per_tt):
            ap3 = BIG[:, base:base + 16 * per_tt].rearrange(
                "p (t x) -> p t x", x=per_tt).rearrange(
                "p t (h o) -> p (t h) o", o=65)[:, :, 64:65]
            return DVE("memset", ap3, 1.0, inc=True)

        wslot = {"par": 0, "cons": {0: None, 1: None}, "sem": ("d_wa", "d_wb")}
        scrf_st = {"free": None}

        def slot_begin():
            par = wslot["par"]
            wslot["par"] ^= 1
            return par, wslot["cons"][par]

        def band_load(in_ap, cols=1024):
            par, c = slot_begin()
            if c is not None:
                b.wait("gp", c[0], c[1])
            tot = DMA("gp", wslot["sem"][par],
                      WB[:, par * 1024:par * 1024 + cols], in_ap)
            return par, wslot["sem"][par], tot

        def band_release(par, eng, val):
            wslot["cons"][par] = (eng, val)

        def layernorm(layer, x_ready):
            b.wait("act", "d_x", xload)
            if x_ready is not None:
                b.wait("act", x_ready[0], x_ready[1])
            xb = {}
            for c in range(NCH):
                xb[c] = ACT(XN[:, c * T:(c + 1) * T], X[:, c * T:(c + 1) * T], AF.Copy)
            kept = None
            last_n = None
            prev_br = None
            tevs = []
            slot_user = {0: None, 1: None}
            for j in range(NTT):
                bk_sx, i_sx = pool.acquire()
                bk_s2, i_s2 = pool.acquire()
                sqv = {}
                vx = v2 = None
                for c in range(NCH):
                    sl = (c % 2) * 512
                    if slot_user[c % 2] is not None:
                        b.wait("act", "pe", slot_user[c % 2])
                    sqv[c] = ACT(SCR[:, sl:sl + 512],
                                 XN[:, c * T + j * TT: c * T + j * TT + TT],
                                 AF.Square)
                    b.wait("pe", "act", xb[c])
                    vx = MM(bk_sx[0:1, 0:TT], SMALL[0:128, ONES_B:ONES_B + 1],
                            XN[:, c * T + j * TT: c * T + j * TT + TT],
                            start=(c == 0), stop=(c == NCH - 1), inc=(c == NCH - 1))
                    b.wait("pe", "act", sqv[c])
                    v2 = MM(bk_s2[0:1, 0:TT], SMALL[0:128, ONES_B:ONES_B + 1],
                            SCR[:, sl:sl + 512],
                            start=(c == 0), stop=(c == NCH - 1), inc=True)
                    slot_user[c % 2] = v2
                b.wait("dve", "pe", vx)
                if scrf_st["free"] is not None:
                    b.wait("dve", scrf_st["free"][0], scrf_st["free"][1])
                m_v = DVE("tensor_scalar_mul", SCRF[0:1, 0:TT], bk_sx[0:1, 0:TT],
                          1.0 / 1024.0)
                pool.release(i_sx, "dve", m_v)
                b.wait("dve", "dve", m_v)
                b.wait("dve", "pe", prev_br)
                t_v = DVE("tensor_mul", STAT[0:1, RSTD:RSTD + TT],
                          SCRF[0:1, 0:TT], SCRF[0:1, 0:TT])
                b.wait("dve", "pe", v2)
                b.wait("dve", "dve", t_v)
                var_v = DVE("scalar_tensor_tensor", bk_s2[0:1, 0:TT],
                            bk_s2[0:1, 0:TT], 1.0 / 1024.0,
                            STAT[0:1, RSTD:RSTD + TT], ALU.mult, ALU.subtract)
                b.wait("act", "dve", var_v)
                b.wait("act", "dve", e5_v)
                ln_v = ACT(STAT[0:1, RSTD:RSTD + TT], bk_s2[0:1, 0:TT],
                           AF.Ln, bias=STAT[0:1, col_eps5:col_eps5 + 1],
                           scale=1.0)
                pool.release(i_s2, "act", ln_v)
                b.wait("act", "act", ln_v)
                ex_v = ACT(STAT[0:1, RSTD:RSTD + TT], STAT[0:1, RSTD:RSTD + TT],
                           AF.Exp, scale=-0.5)
                bk_m, i_m = pool.acquire()
                bk_r, i_r = pool.acquire()
                b.wait("pe", "dve", ov1)
                b.wait("pe", "dve", m_v)
                bm = MM(bk_m[0:128, 0:TT], STAT[0:1, ONES_F:ONES_F + 128],
                        SCRF[0:1, 0:TT], start=True, stop=True, inc=True)
                b.wait("pe", "act", ex_v)
                br = MM(bk_r[0:128, 0:TT], STAT[0:1, ONES_F:ONES_F + 128],
                        STAT[0:1, RSTD:RSTD + TT], start=True, stop=True, inc=True)
                prev_br = br
                for c in range(NCH):
                    b.wait("dve", "pe", bm)
                    tv = DVE("tensor_sub", SCRF[:, 0:TT],
                             X[:, c * T + j * TT: c * T + j * TT + TT],
                             bk_m[0:128, 0:TT])
                    b.wait("dve", "pe", br)
                    b.wait("dve", "dve", tv)
                    last_n = DVE("tensor_mul",
                                 XN[:, c * T + j * TT: c * T + j * TT + TT],
                                 SCRF[:, 0:TT], bk_r[0:128, 0:TT])
                if layer == 1 and j == NTT - 1:
                    kept = (bk_m, i_m, bk_r, i_r)
                else:
                    pool.release(i_m, "dve", last_n)
                    pool.release(i_r, "dve", last_n)
                scrf_st["free"] = ("dve", last_n)
                tevs.append(("dve", last_n))
            return last_n, kept, tevs

        def evac_to(base):
            def dest(mb, j, bk):
                ev = ACT(BIG[:, base + mb * T + j * TT: base + mb * T + j * TT + TT],
                         bk[0:128, 0:TT], AF.Copy)
                return ("act", ev)
            return dest

        def projection(img, mbs, dest_fn, nkc=NCH, rhs=None, rhs_base=0,
                       rhs_ready=None, rhs_tevs=None):
            src = rhs if rhs is not None else XN
            last_dest = None
            for mb in mbs:
                par, sem, tot = band_load(dr[img][mb:mb + 1, :, :nkc * 128],
                                          cols=nkc * 128)
                b.wait("pe", sem, tot)
                if rhs_ready is not None:
                    b.wait("pe", rhs_ready[0], rhs_ready[1])
                banks = [pool.acquire() for _ in range(NTT)]
                lastv = None
                if rhs_tevs is not None:
                    for j in range(NTT):
                        b.wait("pe", rhs_tevs[j][0], rhs_tevs[j][1])
                        for c in range(nkc):
                            lastv = MM(
                                banks[j][0][0:128, 0:TT],
                                WB[:, par * 1024 + c * 128:
                                   par * 1024 + c * 128 + 128],
                                src[:, rhs_base + c * T + j * TT:
                                    rhs_base + c * T + j * TT + TT],
                                start=(c == 0), stop=(c == nkc - 1),
                                inc=(c == nkc - 1 and j == NTT - 1))
                else:
                    for c in range(nkc):
                        for j in range(NTT):
                            lastv = MM(
                                banks[j][0][0:128, 0:TT],
                                WB[:, par * 1024 + c * 128:
                                   par * 1024 + c * 128 + 128],
                                src[:, rhs_base + c * T + j * TT:
                                    rhs_base + c * T + j * TT + TT],
                                start=(c == 0), stop=(c == nkc - 1),
                                inc=(c == nkc - 1 and j == NTT - 1))
                band_release(par, "pe", lastv)
                for j in range(NTT):
                    bk, ib = banks[j]
                    b.wait("act", "pe", lastv)
                    b.wait("dve", "pe", lastv)
                    eng, val = dest_fn(mb, j, bk)
                    pool.release(ib, eng, val)
                    last_dest = (eng, val)
            return last_dest

        def v_project(img, mbs, vt_base, heads_per_mb, per_tt):
            for mi, mb in enumerate(mbs):
                par, sem, tot = band_load(dr[img][mb:mb + 1, :, :], cols=1024)
                b.wait("pe", sem, tot)
                banks = [pool.acquire() for _ in range(NTT)]
                lastv = None
                for c in range(NCH):
                    for j in range(NTT):
                        lastv = MM(
                            banks[j][0][0:128, 0:TT],
                            WB[:, par * 1024 + c * 128: par * 1024 + c * 128 + 128],
                            XN[:, c * T + j * TT: c * T + j * TT + TT],
                            start=(c == 0), stop=(c == NCH - 1),
                            inc=(c == NCH - 1 and j == NTT - 1))
                band_release(par, "pe", lastv)
                tp_prev = None
                for wave in range(2):
                    evs = {}
                    for jj in range(2):
                        j = wave * 2 + jj
                        bk, ib = banks[j]
                        b.wait("act", "pe", lastv)
                        b.wait("act", "pe", tp_prev)
                        evs[jj] = ACT(SCR[:, jj * TT:(jj + 1) * TT],
                                      bk[0:128, 0:TT], AF.Copy)
                        pool.release(ib, "act", evs[jj])
                    for ttw in range(8):
                        tt = wave * 8 + ttw
                        bk, ib = pool.acquire()
                        b.wait("pe", "act", evs[ttw // 4])
                        b.wait("pe", "d_pre", pre_tot)
                        tp_out = bk[0:128, 0:64].bitcast(BF16)

                        def tpfn(e, o=tp_out,
                                 i=SCR[:, ttw * 128:ttw * 128 + 128],
                                 idn=IDT[:, :]):
                            return e.transpose(o, i, idn)
                        tv = b.op("pe", tpfn, inc=True)
                        tp_prev = tv
                        base = vt_base + tt * per_tt + mi * heads_per_mb * 65
                        o_ap = BIG[:, base:base + heads_per_mb * 65].rearrange(
                            "p (h o) -> p h o", o=65)[:, :, 0:64]
                        i_ap = tp_out.rearrange("p (h o) -> p h o", o=64)[
                            :, 0:heads_per_mb, :]
                        b.wait("act", "pe", tv)
                        ev2 = ACT(o_ap, i_ap, AF.Copy)
                        pool.release(ib, "act", ev2)
            return

        def out_proj(img, o_ready, o_base):
            lastadd = None
            for mb in range(8):
                par, sem, tot = band_load(dr[img][mb:mb + 1, :, :], cols=1024)
                b.wait("pe", sem, tot)
                b.wait("pe", o_ready[0], o_ready[1])
                banks = [pool.acquire() for _ in range(NTT)]
                lastv = None
                for c in range(NCH):
                    for j in range(NTT):
                        lastv = MM(banks[j][0][0:128, 0:TT],
                                   WB[:, par * 1024 + c * 128: par * 1024 + c * 128 + 128],
                                   BIG[:, o_base + c * T + j * TT:
                                       o_base + c * T + j * TT + TT],
                                   start=(c == 0), stop=(c == NCH - 1),
                                   inc=(c == NCH - 1 and j == NTT - 1))
                band_release(par, "pe", lastv)
                for j in range(NTT):
                    bk, ib = banks[j]
                    b.wait("dve", "pe", lastv)
                    av = DVE("tensor_add", X[:, mb * T + j * TT: mb * T + j * TT + TT],
                             X[:, mb * T + j * TT: mb * T + j * TT + TT],
                             bk[0:128, 0:TT])
                    pool.release(ib, "dve", av)
                    lastadd = av
            return lastadd

        def bscan(base_col, nch, acc_col, ready, init=True):
            if init:
                DVE("memset", STAT[:, acc_col:acc_col + 1], 0.0)
            last = None
            for c in range(nch):
                for j in range(NTT):
                    if ready is not None:
                        b.wait("dve", ready[0], ready[1])
                    rv = DVE("tensor_reduce", STAT[:, HN_QT:HN_QT + 1],
                             BIG[:, base_col + c * T + j * TT:
                                 base_col + c * T + j * TT + TT],
                             AX.X, ALU.max, apply_absolute_value=True)
                    b.wait("dve", "dve", rv)
                    last = DVE("tensor_max", STAT[:, acc_col:acc_col + 1],
                               STAT[:, acc_col:acc_col + 1],
                               STAT[:, HN_QT:HN_QT + 1])
            return last

        def bfinal(layer, scale, lq, lk):
            bq_, ibq = pool.acquire()
            bk_, ibk = pool.acquire()
            b.wait("pe", "dve", max(lq, lk))
            b.wait("pe", "d_pre", pre_tot)
            tq = MM(bq_[0:1, 0:128], STAT[:, HN_Q:HN_Q + 1], IDF[:, :],
                    start=True, stop=True, inc=True)
            tk = MM(bk_[0:1, 0:128], STAT[:, HN_K:HN_K + 1], IDF[:, :],
                    start=True, stop=True, inc=True)
            b.wait("dve", "pe", tk)
            rq = DVE("tensor_reduce", STAT[0:1, HN_Q:HN_Q + 1],
                     bq_[0:1, 0:128], AX.X, ALU.max)
            rk = DVE("tensor_reduce", STAT[0:1, HN_K:HN_K + 1],
                     bk_[0:1, 0:128], AX.X, ALU.max)
            pool.release(ibq, "dve", rq)
            pool.release(ibk, "dve", rk)
            b.wait("dve", "dve", rk)
            pv = DVE("tensor_mul", STAT[0:1, p_prod:p_prod + 1],
                     STAT[0:1, HN_Q:HN_Q + 1], STAT[0:1, HN_K:HN_K + 1])
            b.wait("dve", "dve", pv)
            nv = DVE("tensor_scalar_mul", STAT[0:1, p_t1:p_t1 + 1],
                     STAT[0:1, p_prod:p_prod + 1], -1.02 * scale)
            bk, ib = pool.acquire()
            b.wait("pe", "dve", nv)
            bv = MM(bk[0:128, 0:1], STAT[0:1, ONES_F:ONES_F + 128],
                    STAT[0:1, p_t1:p_t1 + 1], start=True, stop=True, inc=True)
            b.wait("dve", "pe", bv)
            cv = DVE("tensor_copy", STAT[:, col_negB[layer]:col_negB[layer] + 1],
                     bk[0:128, 0:1])
            pool.release(ib, "dve", cv)
            return cv

        PT1_BASE = 33344
        pt_state = {"i": 0, "cons": [], "slots": []}
        fin1_cell = [None]
        oev1_cell = [None]
        comb1_cell = [None]
        ffnq = []

        def pt_setup(slots, init_cons=None):
            pt_state["i"] = 0
            pt_state["slots"] = slots
            pt_state["cons"] = [init_cons] * len(slots)

        def pt_begin():
            i = pt_state["i"]
            pt_state["i"] = (i + 1) % len(pt_state["slots"])
            c = pt_state["cons"][i]
            if c is not None:
                b.wait("act", c[0], c[1])
            return i

        def pt_ap(i, N):
            tens, base = pt_state["slots"][i]
            return tens[:, base:base + N]

        def attn_l0(h, qt, negB_v, pend, ocell, allow_ffn=True):
            bh = 64 * (h % 2)
            ch = h // 2
            bkO, iO, bkO_cons = pool.acquire_deferred()
            bkO_wait = [bkO_cons]
            stb = []
            for _ in range(2):
                t_, i_, c_ = pool.acquire_deferred()
                stb.append([t_, i_, None, c_])
            n = 4 * qt + 4
            stage = []
            pv_last = [None]
            pend = list(pend) if pend else []

            def drain(k):
                while k > 0:
                    if pend:
                        pend.pop(0)()
                    elif ffnq:
                        ffnq.pop(0)()
                    else:
                        return
                    k -= 1

            def emit_A(idx):
                i = idx
                r = i - 4 * qt
                col0 = 128 * r if r >= 0 else 0
                N = 512 - col0
                bkS, iS, prev_ev, first_c = stb[idx % 2]
                if first_c is not None:
                    b.wait("pe", first_c[0], first_c[1])
                    stb[idx % 2][3] = None
                if prev_ev is not None:
                    b.wait("pe", "act", prev_ev)
                sv = MM(bkS[0:128, 0:N],
                        BIG[bh:bh + 64,
                            L0K + ch * T + i * 128: L0K + ch * T + i * 128 + 128],
                        BIG[bh:bh + 64,
                            L0Q + ch * T + qt * TT + col0: L0Q + ch * T + qt * TT + 512],
                        start=True, stop=True, inc=True)
                par = pt_begin()
                b.wait("act", "pe", sv)
                b.wait("act", "dve", negB_v)
                ev = ACT(pt_ap(par, N), bkS[0:128, 0:N], AF.Exp,
                         bias=STAT[:, col_negB[0]:col_negB[0] + 1], scale=0.125)
                stb[idx % 2][2] = ev
                if r >= 0:
                    b.wait("dve", "act", ev)
                    mv = DVE("tensor_mul", pt_ap(par, 128),
                             pt_ap(par, 128), MASKT[:, :])
                    pw = ("dve", mv)
                else:
                    pw = ("act", ev)
                stage.append((idx, par, pw, col0, N))

            def emit_B():
                idx, par, pw, col0, N = stage.pop(0)
                if bkO_wait[0] is not None:
                    b.wait("pe", bkO_wait[0][0], bkO_wait[0][1])
                    bkO_wait[0] = None
                b.wait("pe", pw[0], pw[1])
                padm = min(128, 16640 - (idx * 1040 + h * 65))
                pv = MM(bkO[0:padm, col0:512],
                        BIG[:, L0VT + idx * 1040 + h * 65:
                            L0VT + idx * 1040 + h * 65 + padm],
                        pt_ap(par, N),
                        start=(idx == 0), stop=(idx == n - 1), inc=True)
                pt_state["cons"][par] = ("pe", pv)
                pv_last[0] = pv

            LAG = 3
            for idx in range(n):
                emit_A(idx)
                if idx >= 3 and pend:
                    pend.pop(0)()
                elif allow_ffn and ffnq:
                    ffnq.pop(0)()
                if idx >= LAG:
                    emit_B()
            while stage:
                emit_B()
            drain(len(pend))
            for bkS, iS, prev_ev, _fc in stb:
                pool.release(iS, "act", prev_ev)

            env = {}

            b.wait("dve", "pe", pv_last[0])
            env["rv"] = DVE("reciprocal", SCR[64:65, SC_Y:SC_Y + 512],
                            bkO[64:65, 0:512])

            def s_bcast():
                bkB, iB = pool.acquire()
                env["bkB"], env["iB"] = bkB, iB
                b.wait("pe", "dve", env["rv"])
                env["bv"] = MM(bkB[0:64, 0:512],
                               SMALL[64:65, ONES_B:ONES_B + 64],
                               SCR[64:65, SC_Y:SC_Y + 512],
                               start=True, stop=True, inc=True)

            def s_bev():
                b.wait("act", "pe", env["bv"])
                bev = ACT(SCR[0:64, SC_U:SC_U + 512],
                          env["bkB"][0:64, 0:512], AF.Copy)
                env["bev"] = bev
                pool.release(env["iB"], "act", bev)

            def s_final():
                ocol = L0Q + ch * T + qt * TT
                b.wait("dve", "act", env["bev"])
                if h % 2 == 0:
                    cv = DVE("tensor_mul", BIG[0:64, ocol:ocol + TT],
                             bkO[0:64, 0:512], SCR[0:64, SC_U:SC_U + 512])
                    pool.release(iO, "dve", cv)
                else:
                    yv = DVE("tensor_mul", SCR[0:64, SC_U:SC_U + 512],
                             bkO[0:64, 0:512], SCR[0:64, SC_U:SC_U + 512])
                    pool.release(iO, "dve", yv)
                    bkR, iR = pool.acquire()
                    b.wait("pe", "dve", yv)
                    rrv = MM(bkR[64:128, 0:512], IDT[0:64, 0:64],
                             SCR[0:64, SC_U:SC_U + 512], start=True, stop=True,
                             inc=True, tp=(0, 64))
                    b.wait("dve", "pe", rrv)
                    cv = DVE("tensor_copy", BIG[64:128, ocol:ocol + TT],
                             bkR[64:128, 0:512])
                    pool.release(iR, "dve", cv)
                ocell[0] = cv

            return [s_bcast, s_bev, s_final]

        def attn_l1(h, qt, negB_v, kb_tot, lam_v, pend, ocell):
            g = h // 4
            bh = 64 * (h % 2)
            ksrc = L1KA if 64 * (g % 2) == bh else L1KB
            kcol = ksrc + (g // 2) * T
            jO, jO_cons = pool.acquire_pair_deferred()
            bkO1, iO1 = ps[2 * jO], 2 * jO
            bkO2, iO2 = ps[2 * jO + 1], 2 * jO + 1
            jS_d = [pool.acquire_pair_deferred(), pool.acquire_pair_deferred()]
            jSs = [jS_d[0][0], jS_d[1][0]]
            st_wait = [jS_d[0][1], jS_d[1][1]]
            jO_wait = [jO_cons]
            st_ev = [None, None]
            n = 4 * qt + 4
            stage = []
            pv_last = [None]
            b.wait("pe", "d_kb", kb_tot)
            pend = list(pend) if pend else []

            def drain(k):
                while pend and k > 0:
                    pend.pop(0)()
                    k -= 1

            def emit_A(idx):
                i = idx
                r = i - 4 * qt
                col0 = 128 * r if r >= 0 else 0
                N = 512 - col0
                w = idx % 2
                jS = jSs[w]
                if st_wait[w] is not None:
                    for c_ in st_wait[w]:
                        if c_ is not None:
                            b.wait("pe", c_[0], c_[1])
                    st_wait[w] = None
                if st_ev[w] is not None:
                    b.wait("pe", "act", st_ev[w])
                svs = None
                for s in range(2):
                    rb = bh + 32 * s
                    svs = MM(ps[2 * jS + s][0:128, 0:N],
                             BIG[rb:rb + 32, kcol + i * 128: kcol + i * 128 + 128],
                             BIG[rb:rb + 32,
                                 L1Q + (h // 2) * T + qt * TT + col0:
                                 L1Q + (h // 2) * T + qt * TT + 512],
                             start=True, stop=True, inc=(s == 1), tp=(rb, 0))
                par = pt_begin()
                ptt, ptb = pt_state["slots"][par]
                in_ap = ps_big[jS][0:128, 0:1024].rearrange(
                    "p (s x) -> p s x", s=2)[:, :, 0:N]
                out_ap = ptt[:, ptb:ptb + 1024].rearrange(
                    "p (s x) -> p s x", s=2)[:, :, 0:N]
                b.wait("act", "pe", svs)
                b.wait("act", "dve", negB_v)
                ev = ACT(out_ap, in_ap, AF.Exp,
                         bias=STAT[:, col_negB[1]:col_negB[1] + 1],
                         scale=0.17677669529663687)
                st_ev[w] = ev
                if r >= 0:
                    b.wait("dve", "act", ev)
                    for s2_ in range(2):
                        mo = ptb + 512 * s2_
                        mv = DVE("tensor_mul", ptt[:, mo:mo + 128],
                                 ptt[:, mo:mo + 128], MASKT[:, :])
                    pw = ("dve", mv)
                else:
                    pw = ("act", ev)
                stage.append((idx, par, pw, col0, N))

            def emit_B():
                idx, par, pw, col0, N = stage.pop(0)
                ptt, ptb = pt_state["slots"][par]
                pv = None
                if jO_wait[0] is not None:
                    for c_ in jO_wait[0]:
                        if c_ is not None:
                            b.wait("pe", c_[0], c_[1])
                    jO_wait[0] = None
                padm = min(128, 4160 + 2 * T - (idx * 260 + g * 65))
                for s in range(2):
                    b.wait("pe", pw[0], pw[1])
                    bkT = bkO1 if s == 0 else bkO2
                    pv = MM(bkT[0:padm, col0:512],
                            BIG[:, L1VT + idx * 260 + g * 65:
                                L1VT + idx * 260 + g * 65 + padm],
                            ptt[:, ptb + 512 * s: ptb + 512 * s + N],
                            start=(idx == 0), stop=(idx == n - 1), inc=(s == 1))
                pt_state["cons"][par] = ("pe", pv)
                pv_last[0] = pv

            LAG = 3
            for idx in range(n):
                emit_A(idx)
                if idx >= 2:
                    drain(1)
                if idx >= LAG:
                    emit_B()
            while stage:
                emit_B()
            while len(pend) > 4:
                pend.pop(0)()
            for w in range(2):
                for s in range(2):
                    pool.release(2 * jSs[w] + s, "act", st_ev[w])

            env = {}

            b.wait("act", "pe", pv_last[0])
            if comb1_cell[0] is not None:
                b.wait("act", "dve", comb1_cell[0])
            oe1 = ACT(SCR[0:65, 0:512], bkO1[0:65, 0:512], AF.Copy)
            oe2 = ACT(SCR[0:65, 512:1024], bkO2[0:65, 0:512], AF.Copy)
            env["oe"] = oe2
            pool.release(iO1, "act", oe2)
            pool.release(iO2, "act", oe2)
            b.wait("dve", "act", oe2)
            env["r1"] = DVE("reciprocal", SCR[64:65, SC_Y:SC_Y + 512],
                            SCR[64:65, 0:512])
            env["r2"] = DVE("reciprocal", SCR[64:65, SC_U:SC_U + 512],
                            SCR[64:65, 512:1024])
            b.wait("dve", "dve", env["r2"])
            b.wait("dve", "act", lam_v)
            env["r2l"] = DVE("tensor_scalar_mul",
                             SCR[64:65, SC_U:SC_U + 512],
                             SCR[64:65, SC_U:SC_U + 512],
                             STAT[64:65, 160 + h:161 + h])

            def s_bc1():
                jB = pool.acquire_pair()
                env["bkB1"], env["iB1"] = ps[2 * jB], 2 * jB
                env["bkB2"], env["iB2"] = ps[2 * jB + 1], 2 * jB + 1
                b.wait("pe", "dve", env["r1"])
                env["bv1"] = MM(env["bkB1"][0:64, 0:512],
                                SMALL[64:65, ONES_B:ONES_B + 64],
                                SCR[64:65, SC_Y:SC_Y + 512],
                                start=True, stop=True, inc=True)

            def s_bc2():
                b.wait("pe", "dve", env["r2l"])
                env["bv2"] = MM(env["bkB2"][0:64, 0:512],
                                SMALL[64:65, ONES_B:ONES_B + 64],
                                SCR[64:65, SC_U:SC_U + 512],
                                start=True, stop=True, inc=True)

            def s_t1():
                b.wait("dve", "pe", env["bv1"])
                b.wait("dve", "act", env["oe"])
                env["t1"] = DVE("tensor_mul", SCR[0:64, SC_Y:SC_Y + 512],
                                SCR[0:64, 0:512], env["bkB1"][0:64, 0:512])

            def s_u():
                b.wait("dve", "pe", env["bv2"])
                u = DVE("tensor_mul", SCR[0:64, SC_U:SC_U + 512],
                        SCR[0:64, 512:1024], env["bkB2"][0:64, 0:512])
                comb1_cell[0] = u
                b.wait("dve", "dve", u)
                yv = DVE("tensor_sub", SCR[0:64, SC_Y:SC_Y + 512],
                         SCR[0:64, SC_Y:SC_Y + 512],
                         SCR[0:64, SC_U:SC_U + 512])
                b.wait("dve", "dve", yv)
                env["sqv"] = DVE("tensor_mul", SCR[0:64, SC_U:SC_U + 512],
                                 SCR[0:64, SC_Y:SC_Y + 512],
                                 SCR[0:64, SC_Y:SC_Y + 512])
                env["uv"] = u

            def s_ss():
                b.wait("pe", "dve", env["sqv"])
                env["ssv"] = MM(env["bkB1"][0:1, 0:512],
                                SMALL[0:64, ONES_B:ONES_B + 1],
                                SCR[0:64, SC_U:SC_U + 512],
                                start=True, stop=True, inc=True)

            def s_ln():
                b.wait("act", "pe", env["ssv"])
                b.wait("act", "dve", e6_v)
                c = scrf_st["free"]
                if c is not None:
                    b.wait("act", c[0], c[1])
                lnv = ACT(SCRF[0:1, 0:512], env["bkB1"][0:1, 0:512], AF.Ln,
                          bias=STAT[0:1, col_eps6:col_eps6 + 1], scale=1.0 / 64.0)
                pool.release(env["iB1"], "act", lnv)
                env["lnv"] = lnv

            def s_exp():
                b.wait("act", "act", env["lnv"])
                env["xev"] = ACT(BIG[0:1, 32832:32832 + 512],
                                 SCRF[0:1, 0:512], AF.Exp, scale=-0.5)

            def s_rb():
                b.wait("pe", "dve", env["uv"])
                b.wait("pe", "act", env["xev"])
                env["rbv"] = MM(env["bkB2"][0:64, 0:512],
                                SMALL[0:1, ONES_B:ONES_B + 64],
                                BIG[0:1, 32832:32832 + 512],
                                start=True, stop=True, inc=True)

            def s_final():
                ocol = L1Q + (h // 2) * T + qt * TT
                b.wait("dve", "pe", env["rbv"])
                if h % 2 == 0:
                    cv = DVE("scalar_tensor_tensor", BIG[0:64, ocol:ocol + TT],
                             SCR[0:64, SC_Y:SC_Y + 512],
                             STAT[0:64, col_rmsw:col_rmsw + 1],
                             env["bkB2"][0:64, 0:512], ALU.mult, ALU.mult)
                    pool.release(env["iB2"], "dve", cv)
                else:
                    yv2 = DVE("scalar_tensor_tensor", SCR[0:64, SC_U:SC_U + 512],
                              SCR[0:64, SC_Y:SC_Y + 512],
                              STAT[0:64, col_rmsw:col_rmsw + 1],
                              env["bkB2"][0:64, 0:512], ALU.mult, ALU.mult)
                    pool.release(env["iB2"], "dve", yv2)
                    bkR, iR = pool.acquire()
                    b.wait("pe", "dve", yv2)
                    rrv = MM(bkR[64:128, 0:512], IDT[0:64, 0:64],
                             SCR[0:64, SC_U:SC_U + 512], start=True, stop=True,
                             inc=True, tp=(0, 64))
                    b.wait("dve", "pe", rrv)
                    cv = DVE("tensor_copy", BIG[64:128, ocol:ocol + TT],
                             bkR[64:128, 0:512])
                    pool.release(iR, "dve", cv)
                ocell[0] = cv
                fin1_cell[0] = cv

            return pend + [s_bc1, s_bc2, s_t1, s_u, s_ss,
                           s_ln, s_exp, s_rb, s_final]

        def finish():
            for e in ("pe", "act", "dve", "gp"):
                b.wait("sp", e, b.ctr[e])
            ytot = 0
            for c in range(NCH):
                ytot = DMA("sp", "d_y", yT[c * 128:(c + 1) * 128, :],
                           X[:, c * T:(c + 1) * T])
            if debug:
                dtot = DMA("sp", "d_dbg", dbg_xn[:, :], XN[:, :])
                dtot = DMA("sp", "d_dbg", dbg_big[:, :], BIG[:, :])
                dtot = DMA("sp", "d_dbg", dbg_stat[:, :], STAT[:, :])
                b.wait("sp", "d_dbg", dtot)
            b.wait("sp", "d_y", ytot)

        done = [False]

        def phase_end(name):
            if upto == name:
                finish()
                done[0] = True
            return done[0]

        while True:
            ln0_v, _, ln0_tevs = layernorm(0, None)
            if phase_end("ln0"):
                break

            vt0_ones = vt_ones(L0VT, 1040)
            qd = projection("wq0", list(range(8)), evac_to(L0Q),
                            rhs_tevs=ln0_tevs)
            lq0 = bscan(L0Q, NCH, HN_Q, qd)
            kd = projection("wk0", list(range(8)), evac_to(L0K))
            lk0 = bscan(L0K, NCH, HN_K, kd)
            b.wait("act", "dve", vt0_ones)
            v_project("wv0", list(range(8)), L0VT, 2, 1040)
            negB0_v = bfinal(0, 0.125, lq0, lk0)

            qst = {"mx_last": None}

            def quant_scan_step(hb):
                def f():
                    if hb == 0:
                        qst["mx_last"] = DVE("memset",
                                             STAT[:, col_mx:col_mx + 1], 0.0)
                    b.wait("sp", scrf_st["free"][0], scrf_st["free"][1])
                    qs_tot = DMA("sp", "d_qs", SCRF[:, 0:512],
                                 dr["ker0"][hb // 2, :,
                                            (hb % 2) * 512:(hb % 2) * 512 + 512])
                    b.wait("dve", "d_qs", qs_tot)
                    b.wait("dve", "dve", qst["mx_last"])
                    rv = DVE("tensor_reduce", STAT[:, col_mxt:col_mxt + 1],
                             SCRF[:, 0:512], AX.X, ALU.max,
                             apply_absolute_value=True)
                    b.wait("dve", "dve", rv)
                    qst["mx_last"] = DVE("tensor_max",
                                         STAT[:, col_mx:col_mx + 1],
                                         STAT[:, col_mx:col_mx + 1],
                                         STAT[:, col_mxt:col_mxt + 1])
                    scrf_st["free"] = ("dve", rv)
                return f

            for hb in range(16):
                ffnq.append(quant_scan_step(hb))
            qsc = {}

            def quant_tail():
                bmx, imx = pool.acquire()
                b.wait("pe", "dve", qst["mx_last"])
                b.wait("pe", "d_pre", pre_tot)
                tmx = MM(bmx[0:1, 0:128], STAT[:, col_mx:col_mx + 1], IDF[:, :],
                         start=True, stop=True, inc=True)
                b.wait("dve", "pe", tmx)
                mxr = DVE("tensor_reduce", STAT[0:1, p_t1:p_t1 + 1],
                          bmx[0:1, 0:128], AX.X, ALU.max)
                pool.release(imx, "dve", mxr)
                b.wait("act", "dve", mxr)
                sv2 = ACT(STAT[0:1, p_idx:p_idx + 1], STAT[0:1, p_t1:p_t1 + 1],
                          AF.Copy, scale=1.0 / 7.0)
                b.wait("dve", "act", sv2)
                iv = DVE("reciprocal", STAT[0:1, p_m:p_m + 1],
                         STAT[0:1, p_idx:p_idx + 1])
                bki, ii = pool.acquire()
                bks, is_ = pool.acquire()
                b.wait("pe", "dve", iv)
                b.wait("pe", "dve", ov1)
                v1 = MM(bki[0:128, 0:1], STAT[0:1, ONES_F:ONES_F + 128],
                        STAT[0:1, p_m:p_m + 1], start=True, stop=True, inc=True)
                b.wait("pe", "act", sv2)
                v2 = MM(bks[0:128, 0:1], STAT[0:1, ONES_F:ONES_F + 128],
                        STAT[0:1, p_idx:p_idx + 1], start=True, stop=True,
                        inc=True)
                b.wait("dve", "pe", v1)
                qsc["isc"] = DVE("tensor_copy", STAT[:, col_isc:col_isc + 1],
                                 bki[0:128, 0:1])
                b.wait("dve", "pe", v2)
                qsc["sc"] = DVE("tensor_copy", STAT[:, col_sc:col_sc + 1],
                                bks[0:128, 0:1])
                pool.release(ii, "dve", qsc["isc"])
                pool.release(is_, "dve", qsc["sc"])

            ffnq.append(quant_tail)

            if phase_end("qkv0"):
                break

            ffn_st = {"cons": wslot["cons"][0], "x_last": None}

            def ffn0_band_substeps(mb):
                bb = {}

                def s_q(half):
                    def f():
                        b.wait("sp", scrf_st["free"][0], scrf_st["free"][1])
                        qs_tot = DMA("sp", "d_qs", SCRF[:, 0:512],
                                     dr["ker0"][mb, :, half * 512:half * 512 + 512])
                        b.wait("dve", "d_qs", qs_tot)
                        b.wait("dve", "dve", qsc["isc"])
                        if ffn_st["cons"] is not None:
                            b.wait("dve", ffn_st["cons"][0], ffn_st["cons"][1])
                        t1v = DVE("tensor_scalar_mul", SCRF[:, 0:512],
                                  SCRF[:, 0:512], STAT[:, col_isc:col_isc + 1])
                        b.wait("dve", "dve", t1v)
                        t2v = DVE("tensor_scalar_add", SCRF[:, 0:512],
                                  SCRF[:, 0:512], MAGIC)
                        b.wait("dve", "dve", t2v)
                        t3v = DVE("tensor_scalar_sub", SCRF[:, 0:512],
                                  SCRF[:, 0:512], MAGIC)
                        b.wait("dve", "dve", t3v)
                        bb["qh%d" % half] = DVE(
                            "tensor_scalar", WB[:, half * 512:half * 512 + 512],
                            SCRF[:, 0:512], -8.0, 7.0, ALU.max, ALU.min)
                        scrf_st["free"] = ("dve", bb["qh%d" % half])
                    return f

                def s_mm(jp, cg):
                    def f():
                        if cg == 0:
                            jpr = pool.acquire_pair()
                            bb["banks%d" % jp] = [
                                (ps[2 * jpr], 2 * jpr),
                                (ps[2 * jpr + 1], 2 * jpr + 1)]
                        b.wait("pe", "dve", bb["qh%d" % cg])
                        banks = bb["banks%d" % jp]
                        for c in range(4 * cg, 4 * cg + 4):
                            for jj in range(2):
                                j = jp * 2 + jj
                                bb["lastv"] = MM(
                                    banks[jj][0][0:128, 0:TT],
                                    WB[:, c * 128:c * 128 + 128],
                                    XN[:, c * T + j * TT: c * T + j * TT + TT],
                                    start=(c == 0), stop=(c == NCH - 1),
                                    inc=(c == NCH - 1 and jj == 1))
                    return f

                def s_ev(jp):
                    def f():
                        for jj in range(2):
                            j = jp * 2 + jj
                            bk, ib = bb["banks%d" % jp][jj]
                            b.wait("dve", "pe", bb["lastv"])
                            b.wait("dve", "dve", qsc["sc"])
                            if ffn_st["x_last"] is not None:
                                b.wait("dve", "dve", ffn_st["x_last"])
                            ffn_st["x_last"] = DVE(
                                "scalar_tensor_tensor",
                                X[:, mb * T + j * TT: mb * T + j * TT + TT],
                                bk[0:128, 0:TT], STAT[:, col_sc:col_sc + 1],
                                X[:, mb * T + j * TT: mb * T + j * TT + TT],
                                ALU.mult, ALU.add)
                            pool.release(ib, "dve", ffn_st["x_last"])
                        if jp == 1:
                            ffn_st["cons"] = ("pe", bb["lastv"])
                    return f

                def s_bias():
                    b.wait("dve", "d_pre", pre_tot)
                    b.wait("dve", "dve", ffn_st["x_last"])
                    ffn_st["x_last"] = DVE(
                        "tensor_scalar_add", X[:, mb * T:(mb + 1) * T],
                        X[:, mb * T:(mb + 1) * T],
                        STAT[:, col_bias0 + mb:col_bias0 + mb + 1])

                return [s_q(0), s_q(1), s_mm(0, 0), s_mm(0, 1), s_ev(0),
                        s_mm(1, 0), s_mm(1, 1), s_ev(1), s_bias]

            for mb in range(8):
                ffnq.extend(ffn0_band_substeps(mb))

            wslot["par"] = 0
            pt_setup([(SCR, PT[0]), (SCR, PT[1]), (WB, 1024), (WB, 1536)],
                     init_cons=wslot["cons"][1])
            ocell = [None]
            pend = None
            for h in range(_NHEADS):
                for qt in range(NTT):
                    pend = attn_l0(h, qt, negB0_v, pend, ocell,
                                   allow_ffn=(h * NTT + qt >= 6))
            for st_ in pend:
                st_()
            while ffnq:
                ffnq.pop(0)()
            ocv = ocell[0]
            ptc = [c for c in pt_state["cons"][2:] if c is not None]
            if ptc:
                wslot["cons"][1] = ("pe", max(v for _, v in ptc))
            wslot["cons"][0] = ffn_st["cons"]
            x_last = ffn_st["x_last"]

            cA = wslot["cons"][0]
            if cA is not None:
                b.wait("gp", cA[0], cA[1])
            ta = DMA("gp", wslot["sem"][0], WB[:, 0:128], dr["a0"][:, :])
            cB = wslot["cons"][1]
            if cB is not None:
                b.wait("gp", cB[0], cB[1])
            tb = DMA("gp", wslot["sem"][1], WB[0:16, 1024:2048], dr["b0"][:, :])
            b.wait("pe", wslot["sem"][0], ta)
            xa_banks = [pool.acquire() for _ in range(NTT)]
            lastxa = None
            for c in range(NCH):
                for j in range(NTT):
                    lastxa = MM(xa_banks[j][0][0:16, 0:TT],
                                WB[:, c * 16:c * 16 + 16],
                                XN[:, c * T + j * TT: c * T + j * TT + TT],
                                start=(c == 0), stop=(c == NCH - 1),
                                inc=(c == NCH - 1 and j == NTT - 1))
            wslot["cons"][0] = ("pe", lastxa)
            xa_evs = []
            for j in range(NTT):
                bk, ib = xa_banks[j]
                b.wait("act", "pe", lastxa)
                ev = ACT(SCR[0:16, j * TT:(j + 1) * TT], bk[0:16, 0:TT], AF.Copy)
                pool.release(ib, "act", ev)
                xa_evs.append(ev)
            b.wait("pe", wslot["sem"][1], tb)
            lora_last = None
            for mb in range(8):
                banks = [pool.acquire() for _ in range(NTT)]
                lastv = None
                for j in range(NTT):
                    b.wait("pe", "act", xa_evs[j])
                    lastv = MM(banks[j][0][0:128, 0:TT],
                               WB[0:16, 1024 + mb * 128:1024 + mb * 128 + 128],
                               SCR[0:16, j * TT:(j + 1) * TT],
                               start=True, stop=True, inc=(j == NTT - 1))
                if mb == 7:
                    wslot["cons"][1] = ("pe", lastv)
                for j in range(NTT):
                    bk, ib = banks[j]
                    b.wait("dve", "pe", lastv)
                    b.wait("dve", "dve", x_last)
                    x_last = DVE("tensor_add",
                                 X[:, mb * T + j * TT: mb * T + j * TT + TT],
                                 X[:, mb * T + j * TT: mb * T + j * TT + TT],
                                 bk[0:128, 0:TT])
                    pool.release(ib, "dve", x_last)
            if phase_end("attn0"):
                break

            x_last = out_proj("wp0", ("dve", ocv), L0Q)
            if phase_end("l0"):
                break

            ln1_v, kept, _ = layernorm(1, ("dve", x_last))
            bk_m, i_m, bk_r, i_r = kept

            xl_ap = X.ap().rearrange("p (c t) -> p c t", t=T)[:, :, T - 1:T]
            xnl_v = DVE("tensor_scalar", STAT[:, col_b1s:col_b1s + 8], xl_ap,
                        bk_m[0:128, 511:512], bk_r[0:128, 511:512],
                        ALU.subtract, ALU.mult)
            b.wait("dve", "dve", xnl_v)
            dbgc = DVE("tensor_copy", STAT[:, 796:804], STAT[:, col_b1s:col_b1s + 8])
            dbgc = DVE("tensor_copy", STAT[:, 804:805], bk_m[0:128, 511:512])
            dbgc = DVE("tensor_copy", STAT[:, 805:806], bk_r[0:128, 511:512])
            pool.release(i_m, "dve", dbgc)
            pool.release(i_r, "dve", dbgc)
            bz1, iz1 = pool.acquire()
            lastz1 = None
            for half in range(2):
                b.wait("sp", scrf_st["free"][0], scrf_st["free"][1])
                qs_tot = DMA("sp", "d_qs", SCRF[:, 0:512],
                             dr["wp1"][:, half * 512:half * 512 + 512])
                b.wait("pe", "d_qs", qs_tot)
                b.wait("pe", "dve", xnl_v)
                for cc in range(4):
                    c = half * 4 + cc
                    lastz1 = MM(bz1[0:128, 0:1], SCRF[:, cc * 128:cc * 128 + 128],
                                STAT[:, col_b1s + c:col_b1s + c + 1],
                                start=(c == 0), stop=(c == 7),
                                inc=(cc == 3))
                scrf_st["free"] = ("pe", lastz1)
            b.wait("act", "pe", lastz1)
            b.wait("act", "d_pre", pre_tot)
            g_v = ACT(STAT[:, col_g:col_g + 1], bz1[0:128, 0:1], AF.Gelu,
                      bias=STAT[:, col_bp1:col_bp1 + 1], scale=1.0)
            pool.release(iz1, "act", g_v)
            bz2, iz2 = pool.acquire()
            b.wait("pe", "act", g_v)
            z2_v = MM(bz2[0:1, 0:8], STAT[:, col_g:col_g + 1],
                      STAT[:, col_wp2:col_wp2 + 8], start=True, stop=True, inc=True)
            b.wait("dve", "pe", z2_v)
            lg_v = DVE("tensor_add", STAT[0:1, p_log:p_log + 8], bz2[0:1, 0:8],
                       STAT[0:1, p_bp2:p_bp2 + 8])
            pool.release(iz2, "dve", lg_v)
            b.wait("dve", "dve", lg_v)
            m_v = DVE("tensor_reduce", STAT[0:1, p_m:p_m + 1],
                      STAT[0:1, p_log:p_log + 8], AX.X, ALU.max)
            b.wait("dve", "dve", m_v)
            eq_v = DVE("tensor_scalar", STAT[0:1, p_eq:p_eq + 8],
                       STAT[0:1, p_log:p_log + 8], STAT[0:1, p_m:p_m + 1], None,
                       ALU.is_equal)
            b.wait("dve", "dve", eq_v)
            c_v = DVE("tensor_mul", STAT[0:1, p_cand:p_cand + 8],
                      STAT[0:1, p_eq:p_eq + 8], STAT[0:1, p_io99:p_io99 + 8])
            b.wait("dve", "dve", c_v)
            c2_v = DVE("tensor_scalar_add", STAT[0:1, p_cand:p_cand + 8],
                       STAT[0:1, p_cand:p_cand + 8], 99.0)
            b.wait("dve", "dve", c2_v)
            ix_v = DVE("tensor_reduce", STAT[0:1, p_idx:p_idx + 1],
                       STAT[0:1, p_cand:p_cand + 8], AX.X, ALU.min)
            b.wait("dve", "dve", ix_v)
            ixi_v = DVE("tensor_copy", IDXI[0:1, 0:1], STAT[0:1, p_idx:p_idx + 1])

            selcell = {}

            def selfn(e):
                r0 = e.alloc_register("selreg")
                e.reg_load(r0, IDXI[0:1, 0:1])
                selcell["r0"] = r0
                selcell["v"] = e.snap(r0, min_val=0, max_val=7)
                selcell["e"] = e
                return None

            def fresh_sel():
                return selcell["e"].snap(selcell["r0"], min_val=0, max_val=7)
            b.wait("gp", "dve", ixi_v)
            b.op("gp", selfn, inc=False)

            def dynfn(out_ap, src, tag=""):
                def fn(e, o=out_ap, s=src, t=tag):
                    try:
                        return e.dma_start(out=o, in_=s())
                    except Exception:
                        print("DYN DMA FAIL tag=", t)
                        raise
                return fn
            b.wait("gp", "pe", lastz1)
            sm_tot = b.dma("gp", "d_sm", dynfn(
                STAT[:, col_b1s:col_b1s + 16],
                lambda: dr["b1e"][bass.ds(selcell["v"], 1), :].rearrange(
                    "e (p q) -> (e p) q", p=128)))
            sm_tot = b.dma("gp", "d_sm", dynfn(
                STAT[:, col_b2s:col_b2s + 8],
                lambda: dr["b2e"][bass.ds(selcell["v"], 1), :].rearrange(
                    "e (p q) -> (e p) q", p=128)))

            b.wait("act", "d_pre", pre_tot)
            lam_v = ACT(STAT[64:65, 160:176], STAT[64:65, 176:192], AF.Sigmoid)

            for c in range(NCH):
                b.wait("dve", "dve", ln1_v)
                b.wait("dve", "dve", x_last)
                x_last = DVE("tensor_add", X[:, c * T:(c + 1) * T],
                             X[:, c * T:(c + 1) * T], XN[:, c * T:(c + 1) * T])
            if phase_end("router"):
                break

            b.wait("act", "d_sm", sm_tot)
            h_done = None
            for grp_i in range(8):
                parA, cA = slot_begin()
                parB, cB = slot_begin()
                for cs in (cA, cB):
                    if cs is not None:
                        b.wait("gp", cs[0], cs[1])
                w1tot = b.dma("gp", wslot["sem"][0], dynfn(
                    WB[:, 0:2048],
                    lambda g=grp_i: dr["w1e"][bass.ds(selcell["v"], 1), :].rearrange(
                        "e (m p x) -> (e p) m x", m=16, p=128)[:, 2 * g:2 * g + 2, :],
                    tag=f"w1 grp={grp_i}"))
                b.wait("pe", wslot["sem"][0], w1tot)
                lastv = None
                for sub in range(2):
                    mb = 2 * grp_i + sub
                    banks = [pool.acquire() for _ in range(NTT)]
                    for c in range(NCH):
                        for j in range(NTT):
                            lastv = MM(banks[j][0][0:128, 0:TT],
                                       WB[:, sub * 1024 + c * 128:
                                          sub * 1024 + c * 128 + 128],
                                       XN[:, c * T + j * TT: c * T + j * TT + TT],
                                       start=(c == 0), stop=(c == NCH - 1),
                                       inc=(c == NCH - 1 and j == NTT - 1))
                    for j in range(NTT):
                        bk, ib = banks[j]
                        b.wait("act", "pe", lastv)
                        h_done = ACT(BIG[:, L1H + mb * T + j * TT:
                                         L1H + mb * T + j * TT + TT],
                                     bk[0:128, 0:TT], AF.Gelu,
                                     bias=STAT[:, col_b1s + mb:col_b1s + mb + 1],
                                     scale=1.0)
                        pool.release(ib, "act", h_done)
                band_release(parA, "pe", lastv)
                band_release(parB, "pe", lastv)
            rs_done = None
            for mb in range(8):
                parA, cA = slot_begin()
                parB, cB = slot_begin()
                for cs in (cA, cB):
                    if cs is not None:
                        b.wait("gp", cs[0], cs[1])
                w2tot = b.dma("gp", wslot["sem"][1], dynfn(
                    WB[:, 0:2048],
                    lambda mb=mb: dr["w2e"][
                        bass.ds(selcell["v"], 1),
                        mb * 262144:(mb + 1) * 262144].rearrange(
                        "e (p q) -> (e p) q", p=128),
                    tag=f"w2 mb={mb}"))
                banks = [pool.acquire() for _ in range(NTT)]
                lastv = None
                b.wait("pe", "act", h_done)
                b.wait("pe", wslot["sem"][1], w2tot)
                for c in range(16):
                    for j in range(NTT):
                        lastv = MM(banks[j][0][0:128, 0:TT],
                                   WB[:, c * 128:c * 128 + 128],
                                   BIG[:, L1H + c * T + j * TT:
                                       L1H + c * T + j * TT + TT],
                                   start=(c == 0), stop=(c == 15),
                                   inc=(c == 15 and j == NTT - 1))
                band_release(parA, "pe", lastv)
                band_release(parB, "pe", lastv)
                for j in range(NTT):
                    bk, ib = banks[j]
                    b.wait("dve", "pe", lastv)
                    b.wait("dve", "d_sm", sm_tot)
                    rs_done = DVE("tensor_single_scalar",
                                  BIG[:, L1RS + mb * T + j * TT:
                                      L1RS + mb * T + j * TT + TT],
                                  bk[0:128, 0:TT],
                                  STAT[:, col_b2s + mb:col_b2s + mb + 1], ALU.add)
                    pool.release(ib, "dve", rs_done)
            bo3_v = DVE("tensor_scalar_mul", STAT[:, col_bo3:col_bo3 + 8],
                        STAT[:, col_bo:col_bo + 8], 0.3)
            for mb in range(8):
                par, sem, tot = band_load(dr["wof"][mb:mb + 1, :, :], cols=1024)
                b.wait("pe", sem, tot)
                b.wait("pe", "dve", rs_done)
                banks = [pool.acquire() for _ in range(NTT)]
                lastv = None
                for c in range(NCH):
                    for j in range(NTT):
                        lastv = MM(banks[j][0][0:128, 0:TT],
                                   WB[:, par * 1024 + c * 128:
                                      par * 1024 + c * 128 + 128],
                                   BIG[:, L1RS + c * T + j * TT:
                                       L1RS + c * T + j * TT + TT],
                                   start=(c == 0), stop=(c == NCH - 1),
                                   inc=(c == NCH - 1 and j == NTT - 1))
                band_release(par, "pe", lastv)
                for j in range(NTT):
                    bk, ib = banks[j]
                    b.wait("dve", "pe", lastv)
                    b.wait("dve", "dve", x_last)
                    x_last = DVE("scalar_tensor_tensor",
                                 X[:, mb * T + j * TT: mb * T + j * TT + TT],
                                 bk[0:128, 0:TT], 0.3,
                                 X[:, mb * T + j * TT: mb * T + j * TT + TT],
                                 ALU.mult, ALU.add)
                    pool.release(ib, "dve", x_last)
                b.wait("dve", "dve", bo3_v)
                b.wait("dve", "dve", x_last)
                x_last = DVE("tensor_scalar_add", X[:, mb * T:(mb + 1) * T],
                             X[:, mb * T:(mb + 1) * T],
                             STAT[:, col_bo3 + mb:col_bo3 + mb + 1])
            if phase_end("ffn1"):
                break

            b.wait("sp", "pe", b.ctr["pe"])
            cs_tot = DMA("sp", "d_cs", BIG[:, L1CS:L1CS + 2 * T], dr["cs"][:, :])
            b.wait("dve", "pe", b.ctr["pe"])
            vt1_ones = vt_ones(L1VT, 260)

            shuf = []
            for i2 in range(0, 32, 2):
                shuf += [i2 + 1, i2]

            def rope(bases, ready):
                done = None
                b.wait("dve", "d_cs", cs_tot)
                if ready is not None:
                    b.wait("dve", ready[0], ready[1])
                for base in bases:
                    for hf in range(2):
                        o = base + hf * 1024
                        co = L1CS + hf * 1024
                        so = L1CS + T + hf * 1024
                        sh_v = DVE("stream_shuffle", SCR[:, SC_Y:SC_Y + 1024],
                                   BIG[:, o:o + 1024], shuf)
                        b.wait("dve", "dve", sh_v)
                        s_v = DVE("tensor_mul", SCR[:, SC_Y:SC_Y + 1024],
                                  SCR[:, SC_Y:SC_Y + 1024],
                                  BIG[:, so:so + 1024])
                        c_v2 = DVE("tensor_mul", BIG[:, o:o + 1024],
                                   BIG[:, o:o + 1024], BIG[:, co:co + 1024])
                        b.wait("dve", "dve", s_v)
                        b.wait("dve", "dve", c_v2)
                        done = DVE("tensor_add", BIG[:, o:o + 1024],
                                   BIG[:, o:o + 1024], SCR[:, SC_Y:SC_Y + 1024])
                return done

            lq1 = lk1 = None
            for mb in range(8):
                qd = projection("wq1", [mb], evac_to(L1Q))
                ropeq_v = rope([L1Q + mb * T], qd)
                lq1 = bscan(L1Q + mb * T, 1, HN_Q, ("dve", ropeq_v),
                            init=(mb == 0))
            ropek_v = None
            for mb in range(2):
                kd = projection("wk1", [mb], evac_to(L1KA))
                ropek_v = rope([L1KA + mb * T], kd)
                lk1 = bscan(L1KA + mb * T, 1, HN_K, ("dve", ropek_v),
                            init=(mb == 0))
            b.wait("sp", "dve", ropek_v)
            kb_tot = 0
            for cc in range(2):
                kb_tot = DMA("sp", "d_kb",
                             BIG[0:64, L1KB + cc * T:L1KB + (cc + 1) * T],
                             BIG[64:128, L1KA + cc * T:L1KA + (cc + 1) * T])
                kb_tot = DMA("sp", "d_kb",
                             BIG[64:128, L1KB + cc * T:L1KB + (cc + 1) * T],
                             BIG[0:64, L1KA + cc * T:L1KA + (cc + 1) * T])
            b.wait("act", "dve", vt1_ones)
            v_project("wv1", list(range(2)), L1VT, 2, 260)
            negB1_v = bfinal(1, 32 ** -0.5, lq1, lk1)
            if phase_end("qkv1"):
                break

            pt_setup([(BIG, PT1_BASE), (BIG, PT1_BASE + 1024),
                      (BIG, PT1_BASE + 2048), (BIG, PT1_BASE + 3072)])
            ocell = [None]
            pend = None
            for h in range(_NHEADS1):
                for qt in range(NTT):
                    pend = attn_l1(h, qt, negB1_v, kb_tot, lam_v, pend, ocell)
            for st_ in pend:
                st_()
            ocv = ocell[0]
            if phase_end("attn1"):
                break

            x_last = out_proj("wo1", ("dve", ocv), L1Q)
            finish()
            break

        def run_stream(eng_name):
            def runner(e):
                for idx_it, item in enumerate(b.stream[eng_name]):
                    kind = item[0]
                    try:
                        if kind == "op":
                            _, fn, inc_eng = item
                            inst = fn(e)
                            if inc_eng is not None:
                                inst.then_inc(sems["c_" + inc_eng], 1)
                        elif kind == "wait":
                            _, sem, val = item
                            key = sems[sem] if sem in sems else sems["c_" + sem]
                            e.wait_ge(key, val)
                        else:
                            _, fn, sem = item
                            inst = fn(e)
                            inst.then_inc(sems[sem], 16)
                    except Exception:
                        import traceback
                        print(f"REPLAY FAIL {eng_name}[{idx_it}] kind={kind}")
                        traceback.print_exc()
                        raise
            return runner

        with nc.allow_low_precision(reason="bf16 elementwise on bf16 matmul operands"), \
             nc.Block() as block:
            block.tensor(run_stream("pe"))
            block.scalar(run_stream("act"))
            block.vector(run_stream("dve"))
            block.sync(run_stream("sp"))
            block.gpsimd(run_stream("gp"))

    nc.finalize()
    return nc



def _img(A):
    K, F = A.shape
    kc, mb = K // 128, F // 128
    return np.ascontiguousarray(
        A.reshape(kc, 128, mb, 128).transpose(2, 1, 0, 3)).reshape(mb, 128, kc * 128)


def _rope_tables():
    j = np.arange(16, dtype=np.float64)
    invf = 10000.0 ** (-j / 16.0)
    t = np.arange(T, dtype=np.float64)
    ang = np.outer(invf, t)
    rows_j = (np.arange(128) % 32) // 2
    C = np.cos(ang)[rows_j]
    S = np.sin(ang)[rows_j]
    sign = np.where(np.arange(128) % 2 == 0, -1.0, 1.0)[:, None]
    return np.concatenate([C, S * sign], axis=1).astype(ml_dtypes.bfloat16)


def prep_inputs(L0_attn_x, L0_attn_w_attn, L0_attn_w_proj, L0_ffn_kernel,
                L0_ffn_bias, L0_ffn_a, L0_ffn_b, L1_attn_Wq, L1_attn_Wkv,
                L1_attn_Wout, L1_attn_lambda_init, L1_attn_rms_weight,
                L1_ffn_Wp1, L1_ffn_bp1, L1_ffn_Wp2, L1_ffn_bp2, L1_ffn_W1,
                L1_ffn_b1, L1_ffn_W2, L1_ffn_b2, L1_ffn_Wo, L1_ffn_bo):
    f = np.asarray
    x = f(L0_attn_x, dtype=np.float32)
    Aqkv = np.ascontiguousarray(f(L0_attn_w_attn, dtype=np.float32).T)
    w = {
        "wq0": _img(Aqkv[:, 0:1024]),
        "wk0": _img(Aqkv[:, 1024:2048]),
        "wv0": _img(Aqkv[:, 2048:3072]),
        "wp0": _img(np.ascontiguousarray(f(L0_attn_w_proj, dtype=np.float32).T)),
        "ker0": _img(f(L0_ffn_kernel, dtype=np.float32)),
        "b0": np.ascontiguousarray(f(L0_ffn_b, dtype=np.float32)),
        "bias0": np.ascontiguousarray(
            f(L0_ffn_bias, dtype=np.float32).reshape(8, 128).T),
        "wq1": _img(np.ascontiguousarray(f(L1_attn_Wq, dtype=np.float32).T)),
        "wo1": _img(np.ascontiguousarray(f(L1_attn_Wout, dtype=np.float32).T)),
        "lam": f(L1_attn_lambda_init, dtype=np.float32).reshape(1, 16),
        "rmsw": f(L1_attn_rms_weight, dtype=np.float32).reshape(64, 1),
        "wp1": _img(f(L1_ffn_Wp1, dtype=np.float32))[0],
        "wp2": f(L1_ffn_Wp2, dtype=np.float32),
        "bp1": f(L1_ffn_bp1, dtype=np.float32).reshape(128, 1),
        "bp2": f(L1_ffn_bp2, dtype=np.float32).reshape(1, 8),
        "b1e": np.stack([np.ascontiguousarray(
            f(L1_ffn_b1, dtype=np.float32)[e].reshape(16, 128).T).reshape(-1)
            for e in range(8)]),
        "b2e": np.stack([np.ascontiguousarray(
            f(L1_ffn_b2, dtype=np.float32)[e].reshape(8, 128).T).reshape(-1)
            for e in range(8)]),
        "wof": _img(f(L1_ffn_Wo, dtype=np.float32)),
        "bof": np.ascontiguousarray(f(L1_ffn_bo, dtype=np.float32).reshape(8, 128).T),
    }
    a = f(L0_ffn_a, dtype=np.float32)
    w["a0"] = np.ascontiguousarray(
        a.reshape(8, 128, 16).transpose(1, 0, 2)).reshape(128, 128)
    Akv = np.ascontiguousarray(f(L1_attn_Wkv, dtype=np.float32).T)
    w["wk1"] = _img(Akv[:, 0:256])
    w["wv1"] = _img(Akv[:, 256:512])
    W1 = f(L1_ffn_W1, dtype=np.float32)
    w["w1e"] = np.stack([_img(W1[e]).reshape(-1) for e in range(8)])
    W2 = f(L1_ffn_W2, dtype=np.float32)
    w["w2e"] = np.stack([_img(W2[e]).reshape(-1) for e in range(8)])
    w["cs"] = _rope_tables()
    kk, qq = np.meshgrid(np.arange(128), np.arange(128), indexing="ij")
    w["mask"] = (qq >= kk).astype(ml_dtypes.bfloat16)
    w["ident"] = np.eye(128, dtype=ml_dtypes.bfloat16)
    w["identf"] = np.eye(128, dtype=np.float32)
    w["grp2"] = (np.arange(128)[:, None] // 64 ==
                 np.arange(2)[None, :]).astype(ml_dtypes.bfloat16)
    w["grp4"] = (np.arange(128)[:, None] // 32 ==
                 np.arange(4)[None, :]).astype(ml_dtypes.bfloat16)
    w["io99"] = (np.arange(8, dtype=np.float32) - 99.0).reshape(1, 8)
    in_maps = []
    for core in range(8):
        m = {"xT": np.ascontiguousarray(x[core].T)}
        m.update(w)
        in_maps.append(m)
    return in_maps


def kernel(**inputs):
    in_maps = prep_inputs(**inputs)
    nc = build_nc(upto=_UPTO, debug=_DEBUG)
    res = run_bass_kernel_spmd(nc, in_maps, core_ids=list(range(8)))
    out = np.stack([np.ascontiguousarray(r["yT"].T) for r in res.results])
    return out.astype(np.float32)


# version 23
# speedup vs baseline: 1.0177x; 1.0053x over previous
import contextlib

import numpy as np
import ml_dtypes

import concourse.bass as bass
import concourse.mybir as mybir
from concourse.bass_utils import run_bass_kernel_spmd

F32 = mybir.dt.float32
BF16 = mybir.dt.bfloat16
I32 = mybir.dt.int32
AF = mybir.ActivationFunctionType
ALU = mybir.AluOpType
AX = mybir.AxisListType

D = 1024
T = 2048
NCH = 8
NTT = 4
TT = 512
MAGIC = 12582912.0

_UPTO = "all"
_DEBUG = False
_NHEADS = 16
_NHEADS1 = 16

ENGS = ("pe", "act", "dve", "sp", "gp")


class Builder:
    def __init__(self):
        self.stream = {e: [] for e in ENGS}
        self.ctr = {e: 0 for e in ENGS}
        self.dmatot = {}
        self.waited = {}

    def op(self, eng, fn, inc=False):
        if inc:
            self.ctr[eng] += 1
            self.stream[eng].append(("op", fn, eng))
            return self.ctr[eng]
        self.stream[eng].append(("op", fn, None))
        return None

    def wait(self, eng, sem, val):
        if val is None or val <= 0:
            return
        key = (eng, sem)
        if self.waited.get(key, 0) >= val:
            return
        self.waited[key] = val
        self.stream[eng].append(("wait", sem, val))

    def dma(self, eng, sem, fn):
        self.dmatot[sem] = self.dmatot.get(sem, 0) + 16
        self.stream[eng].append(("dma", fn, sem))
        return self.dmatot[sem]


class BankPool:
    def __init__(self, b, tensors):
        self.b = b
        self.tensors = tensors
        self.free_q = list(range(len(tensors)))
        self.cons = {i: None for i in range(len(tensors))}

    def acquire(self):
        i = self.free_q.pop(0)
        c = self.cons[i]
        if c is not None:
            self.b.wait("pe", c[0], c[1])
        return self.tensors[i], i

    def acquire_pair(self):
        order = [i // 2 for i in self.free_q]
        cand = [j for k, j in enumerate(order) if j in order[k + 1:]]
        for j in cand + list(range(len(self.tensors) // 2)):
            if 2 * j in self.free_q and 2 * j + 1 in self.free_q:
                self.free_q.remove(2 * j)
                self.free_q.remove(2 * j + 1)
                for i in (2 * j, 2 * j + 1):
                    c = self.cons[i]
                    if c is not None:
                        self.b.wait("pe", c[0], c[1])
                return j
        raise RuntimeError("no aligned psum pair free")

    def acquire_deferred(self):
        i = self.free_q.pop(0)
        return self.tensors[i], i, self.cons[i]

    def acquire_pair_deferred(self):
        order = [i // 2 for i in self.free_q]
        cand = [j for k, j in enumerate(order) if j in order[k + 1:]]
        for j in cand + list(range(len(self.tensors) // 2)):
            if 2 * j in self.free_q and 2 * j + 1 in self.free_q:
                self.free_q.remove(2 * j)
                self.free_q.remove(2 * j + 1)
                return j, [self.cons[2 * j], self.cons[2 * j + 1]]
        raise RuntimeError("no aligned psum pair free")

    def release(self, i, eng, val):
        self.cons[i] = (eng, val)
        self.free_q.append(i)


def build_nc(upto="all", debug=False):
    _NHEADS = globals()["_NHEADS"]
    _NHEADS1 = globals()["_NHEADS1"]
    nc = bass.Bass("TRN2", target_bir_lowering=False)
    dr = {}

    def din(name, shape, dt=F32):
        dr[name] = nc.dram_tensor(name, list(shape), dt, kind="ExternalInput").ap()

    din("xT", (D, T))
    din("wq0", (8, 128, 1024)); din("wk0", (8, 128, 1024)); din("wv0", (8, 128, 1024))
    din("wp0", (8, 128, 1024))
    din("ker0", (8, 128, 1024))
    din("a0", (128, 128)); din("b0", (16, 1024)); din("bias0", (128, 8))
    din("wq1", (8, 128, 1024)); din("wk1", (2, 128, 1024)); din("wv1", (2, 128, 1024))
    din("wo1", (8, 128, 1024))
    din("lam", (1, 16)); din("rmsw", (64, 1))
    din("wp1", (128, 1024)); din("wp2", (128, 8)); din("bp1", (128, 1)); din("bp2", (1, 8))
    din("w1e", (8, 16 * 128 * 1024)); din("w2e", (8, 8 * 128 * 2048))
    din("b1e", (8, 2048)); din("b2e", (8, 1024))
    din("wof", (8, 128, 1024)); din("bof", (128, 8))
    din("cs", (128, 2 * T), BF16)
    din("mask", (128, 128), BF16)
    din("ident", (128, 128), BF16)
    din("identf", (128, 128))
    din("grp2", (128, 2), BF16); din("grp4", (128, 4), BF16)
    din("io99", (1, 8))

    yT = nc.dram_tensor("yT", [D, T], F32, kind="ExternalOutput").ap()
    dbg_xn = dbg_big = dbg_stat = None
    if debug:
        dbg_xn = nc.dram_tensor("dbg_xn", [128, NCH * T], BF16, kind="ExternalOutput").ap()
        dbg_big = nc.dram_tensor("dbg_big", [128, 49408], BF16, kind="ExternalOutput").ap()
        dbg_stat = nc.dram_tensor("dbg_stat", [128, 832], F32, kind="ExternalOutput").ap()

    b = Builder()

    with contextlib.ExitStack() as ctx:
        def sbuf(name, shape, dt):
            return ctx.enter_context(nc.sbuf_tensor(name, list(shape), dt))

        X = sbuf("X", (128, NCH * T), F32)
        XN = sbuf("XN", (128, NCH * T), BF16)
        BIG = sbuf("BIGB", (128, 49408), BF16)
        WB = sbuf("WB", (128, 2048), BF16)
        SCR = sbuf("SCR", (128, 2048), BF16)
        SCRF = sbuf("SCRF", (128, 512), F32)
        STAT = sbuf("STAT", (128, 832), F32)
        SMALL = sbuf("SMALL", (128, 72), BF16)
        MASKT = sbuf("MASKT", (128, 128), BF16)
        IDT = sbuf("IDT", (128, 128), BF16)
        IDF = sbuf("IDF", (128, 128), F32)
        IDXI = sbuf("IDXI", (1, 1), I32)
        ps_big = [ctx.enter_context(nc.psum_tensor(f"psb{i}", [128, 1024], F32))
                  for i in range(4)]
        ps = []
        for i in range(4):
            ps.append(ps_big[i][0:128, 0:512])
            ps.append(ps_big[i][0:128, 512:1024])

        sems = {}
        for e in ENGS:
            sems["c_" + e] = ctx.enter_context(nc.semaphore("c_" + e))
        for d in ("d_pre", "d_x", "d_wa", "d_wb", "d_qs", "d_cs", "d_kb",
                  "d_sm", "d_y", "d_mx", "d_dbg"):
            sems[d] = ctx.enter_context(nc.semaphore(d))

        HN_Q, HN_QT, HN_K, HN_KT = 0, 8, 16, 24
        ONES_F = 32
        RSTD = 160
        _statc = [672]

        def stat_col(n=1):
            c = _statc[0]; _statc[0] += n
            assert _statc[0] <= 744
            return c

        col_negB = [stat_col(), stat_col()]
        col_isc = stat_col(); col_sc = stat_col()
        col_bias0 = stat_col(8)
        col_bo = stat_col(8); col_bo3 = stat_col(8)
        col_b1s = stat_col(16); col_b2s = stat_col(8)
        col_bp1 = stat_col(); col_rmsw = stat_col()
        col_g = stat_col(); col_wp2 = stat_col(8)
        col_mx = stat_col(); col_mxt = stat_col()
        col_eps5 = stat_col(); col_eps6 = stat_col()
        _p0c = [744]

        def p0_col(n=1):
            c = _p0c[0]; _p0c[0] += n
            assert _p0c[0] <= 832
            return c

        p_io99 = p0_col(8); p_bp2 = p0_col(8); p_log = p0_col(8)
        p_eq = p0_col(8); p_cand = p0_col(8)
        p_m = p0_col(); p_idx = p0_col(); p_prod = p0_col(); p_t1 = p0_col()

        SM_G2, SM_G4, ONES_B = 0, 2, 6

        L0Q, L0K, L0VT = 0, 16384, 32768
        L1Q, L1KA, L1KB, L1VT, L1CS = 0, 16384, 20480, 24576, 28736
        L1H, L1RS = 0, 32768
        PT = (0, 512)
        SC_Y, SC_U = 1024, 1536

        pool = BankPool(b, ps)

        def MM(out_ap, lhsT, rhs, start, stop, inc=False, tp=None):
            def fn(e, o=out_ap, l=lhsT, r=rhs, st=start, sp=stop, t=tp):
                return e.matmul(o, l, r, start=st, stop=sp, tile_position=t)
            return b.op("pe", fn, inc=inc)

        def ACT(out_ap, in_ap, func, bias=0.0, scale=1.0, inc=True):
            def fn(e, o=out_ap, i=in_ap, f=func, bi=bias, sc=scale):
                return e.activation(o, i, f, bias=bi, scale=sc)
            return b.op("act", fn, inc=inc)

        def DVE(meth, *args, inc=True, **kw):
            def fn(e, m=meth, a=args, k=kw):
                return getattr(e, m)(*a, **k)
            return b.op("dve", fn, inc=inc)

        def DMA(eng, sem, out_ap, in_ap, **kw):
            def fn(e, o=out_ap, i=in_ap, k=kw):
                return e.dma_start(out=o, in_=i, **k)
            return b.dma(eng, sem, fn)

        xloads = []
        for c in range(NCH):
            xloads.append(DMA("sp", "d_x", X[:, c * T:(c + 1) * T],
                              dr["xT"][c * 128:(c + 1) * 128, :]))
        pre = 0
        pre = DMA("sp", "d_pre", MASKT[:, :], dr["mask"][:, :])
        pre = DMA("sp", "d_pre", IDT[:, :], dr["ident"][:, :])
        pre = DMA("sp", "d_pre", IDF[:, :], dr["identf"][:, :])
        pre = DMA("sp", "d_pre", SMALL[:, SM_G2:SM_G2 + 2], dr["grp2"][:, :])
        pre = DMA("sp", "d_pre", SMALL[:, SM_G4:SM_G4 + 4], dr["grp4"][:, :])
        pre = DMA("sp", "d_pre", STAT[0:1, p_io99:p_io99 + 8], dr["io99"][:, :])
        pre = DMA("sp", "d_pre", STAT[0:1, p_bp2:p_bp2 + 8], dr["bp2"][:, :])
        pre = DMA("sp", "d_pre", STAT[:, col_bias0:col_bias0 + 8], dr["bias0"][:, :])
        pre = DMA("sp", "d_pre", STAT[:, col_bo:col_bo + 8], dr["bof"][:, :])
        pre = DMA("sp", "d_pre", STAT[:, col_bp1:col_bp1 + 1], dr["bp1"][:, :])
        pre = DMA("sp", "d_pre", STAT[0:64, col_rmsw:col_rmsw + 1], dr["rmsw"][:, :])
        pre = DMA("sp", "d_pre", STAT[64:65, 176:192], dr["lam"][:, :])
        pre = DMA("sp", "d_pre", STAT[:, col_wp2:col_wp2 + 8], dr["wp2"][:, :])
        pre_tot = pre

        e5_v = DVE("memset", STAT[:, col_eps5:col_eps5 + 1], 1e-5, inc=True)
        e6_v = DVE("memset", STAT[:, col_eps6:col_eps6 + 1], 1e-6, inc=True)
        ov1 = DVE("memset", STAT[:, ONES_F:ONES_F + 128], 1.0, inc=True)
        b.wait("dve", "dve", ov1)
        ones_v = DVE("memset", SMALL[:, ONES_B:ONES_B + 64], 1.0, inc=True)

        def vt_ones(base, per_tt):
            ap3 = BIG[:, base:base + 16 * per_tt].rearrange(
                "p (t x) -> p t x", x=per_tt).rearrange(
                "p t (h o) -> p (t h) o", o=65)[:, :, 64:65]
            return DVE("memset", ap3, 1.0, inc=True)

        wslot = {"par": 0, "cons": {0: None, 1: None}, "sem": ("d_wa", "d_wb")}
        scrf_st = {"free": None}

        def slot_begin():
            par = wslot["par"]
            wslot["par"] ^= 1
            return par, wslot["cons"][par]

        def band_load(in_ap, cols=1024):
            par, c = slot_begin()
            if c is not None:
                b.wait("gp", c[0], c[1])
            tot = DMA("gp", wslot["sem"][par],
                      WB[:, par * 1024:par * 1024 + cols], in_ap)
            return par, wslot["sem"][par], tot

        def band_release(par, eng, val):
            wslot["cons"][par] = (eng, val)

        def layernorm(layer, x_ready, xb_pre=None):
            if xb_pre is not None:
                xb = xb_pre
            else:
                if x_ready is not None:
                    b.wait("act", x_ready[0], x_ready[1])
                xb = {}
                for c in range(NCH):
                    b.wait("act", "d_x", xloads[c])
                    xb[c] = ACT(XN[:, c * T:(c + 1) * T],
                                X[:, c * T:(c + 1) * T], AF.Copy)
            kept = None
            last_n = None
            prev_br = None
            tevs = []
            slot_user = {0: None, 1: None}
            for j in range(NTT):
                bk_sx, i_sx = pool.acquire()
                bk_s2, i_s2 = pool.acquire()
                sqv = {}
                vx = v2 = None
                for c in range(NCH):
                    sl = (c % 2) * 512
                    if slot_user[c % 2] is not None:
                        b.wait("act", "pe", slot_user[c % 2])
                    sqv[c] = ACT(SCR[:, sl:sl + 512],
                                 XN[:, c * T + j * TT: c * T + j * TT + TT],
                                 AF.Square)
                    b.wait("pe", "act", xb[c])
                    vx = MM(bk_sx[0:1, 0:TT], SMALL[0:128, ONES_B:ONES_B + 1],
                            XN[:, c * T + j * TT: c * T + j * TT + TT],
                            start=(c == 0), stop=(c == NCH - 1), inc=(c == NCH - 1))
                    b.wait("pe", "act", sqv[c])
                    v2 = MM(bk_s2[0:1, 0:TT], SMALL[0:128, ONES_B:ONES_B + 1],
                            SCR[:, sl:sl + 512],
                            start=(c == 0), stop=(c == NCH - 1), inc=True)
                    slot_user[c % 2] = v2
                b.wait("dve", "pe", vx)
                if scrf_st["free"] is not None:
                    b.wait("dve", scrf_st["free"][0], scrf_st["free"][1])
                m_v = DVE("tensor_scalar_mul", SCRF[0:1, 0:TT], bk_sx[0:1, 0:TT],
                          1.0 / 1024.0)
                pool.release(i_sx, "dve", m_v)
                b.wait("dve", "dve", m_v)
                b.wait("dve", "pe", prev_br)
                t_v = DVE("tensor_mul", STAT[0:1, RSTD:RSTD + TT],
                          SCRF[0:1, 0:TT], SCRF[0:1, 0:TT])
                b.wait("dve", "pe", v2)
                b.wait("dve", "dve", t_v)
                var_v = DVE("scalar_tensor_tensor", bk_s2[0:1, 0:TT],
                            bk_s2[0:1, 0:TT], 1.0 / 1024.0,
                            STAT[0:1, RSTD:RSTD + TT], ALU.mult, ALU.subtract)
                b.wait("act", "dve", var_v)
                b.wait("act", "dve", e5_v)
                ln_v = ACT(STAT[0:1, RSTD:RSTD + TT], bk_s2[0:1, 0:TT],
                           AF.Ln, bias=STAT[0:1, col_eps5:col_eps5 + 1],
                           scale=1.0)
                pool.release(i_s2, "act", ln_v)
                b.wait("act", "act", ln_v)
                ex_v = ACT(STAT[0:1, RSTD:RSTD + TT], STAT[0:1, RSTD:RSTD + TT],
                           AF.Exp, scale=-0.5)
                bk_m, i_m = pool.acquire()
                bk_r, i_r = pool.acquire()
                b.wait("pe", "dve", ov1)
                b.wait("pe", "dve", m_v)
                bm = MM(bk_m[0:128, 0:TT], STAT[0:1, ONES_F:ONES_F + 128],
                        SCRF[0:1, 0:TT], start=True, stop=True, inc=True)
                b.wait("pe", "act", ex_v)
                br = MM(bk_r[0:128, 0:TT], STAT[0:1, ONES_F:ONES_F + 128],
                        STAT[0:1, RSTD:RSTD + TT], start=True, stop=True, inc=True)
                prev_br = br
                for c in range(NCH):
                    b.wait("dve", "pe", bm)
                    tv = DVE("tensor_sub", SCRF[:, 0:TT],
                             X[:, c * T + j * TT: c * T + j * TT + TT],
                             bk_m[0:128, 0:TT])
                    b.wait("dve", "pe", br)
                    b.wait("dve", "dve", tv)
                    last_n = DVE("tensor_mul",
                                 XN[:, c * T + j * TT: c * T + j * TT + TT],
                                 SCRF[:, 0:TT], bk_r[0:128, 0:TT])
                if layer == 1 and j == NTT - 1:
                    kept = (bk_m, i_m, bk_r, i_r)
                else:
                    pool.release(i_m, "dve", last_n)
                    pool.release(i_r, "dve", last_n)
                scrf_st["free"] = ("dve", last_n)
                tevs.append(("dve", last_n))
            return last_n, kept, tevs

        def evac_to(base):
            def dest(mb, j, bk):
                ev = ACT(BIG[:, base + mb * T + j * TT: base + mb * T + j * TT + TT],
                         bk[0:128, 0:TT], AF.Copy)
                return ("act", ev)
            return dest

        def projection(img, mbs, dest_fn, nkc=NCH, rhs=None, rhs_base=0,
                       rhs_ready=None, rhs_tevs=None):
            src = rhs if rhs is not None else XN
            last_dest = None
            for mb in mbs:
                par, sem, tot = band_load(dr[img][mb:mb + 1, :, :nkc * 128],
                                          cols=nkc * 128)
                b.wait("pe", sem, tot)
                if rhs_ready is not None:
                    b.wait("pe", rhs_ready[0], rhs_ready[1])
                banks = [pool.acquire() for _ in range(NTT)]
                lastv = None
                if rhs_tevs is not None:
                    for j in range(NTT):
                        b.wait("pe", rhs_tevs[j][0], rhs_tevs[j][1])
                        for c in range(nkc):
                            lastv = MM(
                                banks[j][0][0:128, 0:TT],
                                WB[:, par * 1024 + c * 128:
                                   par * 1024 + c * 128 + 128],
                                src[:, rhs_base + c * T + j * TT:
                                    rhs_base + c * T + j * TT + TT],
                                start=(c == 0), stop=(c == nkc - 1),
                                inc=(c == nkc - 1 and j == NTT - 1))
                else:
                    for c in range(nkc):
                        for j in range(NTT):
                            lastv = MM(
                                banks[j][0][0:128, 0:TT],
                                WB[:, par * 1024 + c * 128:
                                   par * 1024 + c * 128 + 128],
                                src[:, rhs_base + c * T + j * TT:
                                    rhs_base + c * T + j * TT + TT],
                                start=(c == 0), stop=(c == nkc - 1),
                                inc=(c == nkc - 1 and j == NTT - 1))
                band_release(par, "pe", lastv)
                for j in range(NTT):
                    bk, ib = banks[j]
                    b.wait("act", "pe", lastv)
                    b.wait("dve", "pe", lastv)
                    eng, val = dest_fn(mb, j, bk)
                    pool.release(ib, eng, val)
                    last_dest = (eng, val)
            return last_dest

        def v_project(img, mbs, vt_base, heads_per_mb, per_tt):
            for mi, mb in enumerate(mbs):
                par, sem, tot = band_load(dr[img][mb:mb + 1, :, :], cols=1024)
                b.wait("pe", sem, tot)
                banks = [pool.acquire() for _ in range(NTT)]
                lastv = None
                for c in range(NCH):
                    for j in range(NTT):
                        lastv = MM(
                            banks[j][0][0:128, 0:TT],
                            WB[:, par * 1024 + c * 128: par * 1024 + c * 128 + 128],
                            XN[:, c * T + j * TT: c * T + j * TT + TT],
                            start=(c == 0), stop=(c == NCH - 1),
                            inc=(c == NCH - 1 and j == NTT - 1))
                band_release(par, "pe", lastv)
                tp_prev = None
                for wave in range(2):
                    evs = {}
                    for jj in range(2):
                        j = wave * 2 + jj
                        bk, ib = banks[j]
                        b.wait("act", "pe", lastv)
                        b.wait("act", "pe", tp_prev)
                        evs[jj] = ACT(SCR[:, jj * TT:(jj + 1) * TT],
                                      bk[0:128, 0:TT], AF.Copy)
                        pool.release(ib, "act", evs[jj])
                    for ttw in range(8):
                        tt = wave * 8 + ttw
                        bk, ib = pool.acquire()
                        b.wait("pe", "act", evs[ttw // 4])
                        b.wait("pe", "d_pre", pre_tot)
                        tp_out = bk[0:128, 0:64].bitcast(BF16)

                        def tpfn(e, o=tp_out,
                                 i=SCR[:, ttw * 128:ttw * 128 + 128],
                                 idn=IDT[:, :]):
                            return e.transpose(o, i, idn)
                        tv = b.op("pe", tpfn, inc=True)
                        tp_prev = tv
                        base = vt_base + tt * per_tt + mi * heads_per_mb * 65
                        o_ap = BIG[:, base:base + heads_per_mb * 65].rearrange(
                            "p (h o) -> p h o", o=65)[:, :, 0:64]
                        i_ap = tp_out.rearrange("p (h o) -> p h o", o=64)[
                            :, 0:heads_per_mb, :]
                        b.wait("act", "pe", tv)
                        ev2 = ACT(o_ap, i_ap, AF.Copy)
                        pool.release(ib, "act", ev2)
            return

        def out_proj(img, o_ready, o_base, make_xn=False):
            lastadd = None
            xb = {}
            for mb in range(8):
                par, sem, tot = band_load(dr[img][mb:mb + 1, :, :], cols=1024)
                b.wait("pe", sem, tot)
                b.wait("pe", o_ready[0], o_ready[1])
                banks = [pool.acquire() for _ in range(NTT)]
                lastv = None
                for c in range(NCH):
                    for j in range(NTT):
                        lastv = MM(banks[j][0][0:128, 0:TT],
                                   WB[:, par * 1024 + c * 128: par * 1024 + c * 128 + 128],
                                   BIG[:, o_base + c * T + j * TT:
                                       o_base + c * T + j * TT + TT],
                                   start=(c == 0), stop=(c == NCH - 1),
                                   inc=(c == NCH - 1 and j == NTT - 1))
                band_release(par, "pe", lastv)
                for j in range(NTT):
                    bk, ib = banks[j]
                    b.wait("dve", "pe", lastv)
                    av = DVE("tensor_add", X[:, mb * T + j * TT: mb * T + j * TT + TT],
                             X[:, mb * T + j * TT: mb * T + j * TT + TT],
                             bk[0:128, 0:TT])
                    pool.release(ib, "dve", av)
                    lastadd = av
                if make_xn:
                    b.wait("act", "dve", lastadd)
                    xb[mb] = ACT(XN[:, mb * T:(mb + 1) * T],
                                 X[:, mb * T:(mb + 1) * T], AF.Copy)
            if make_xn:
                return lastadd, xb
            return lastadd

        def bscan(base_col, nch, acc_col, ready, init=True):
            if init:
                DVE("memset", STAT[:, acc_col:acc_col + 1], 0.0)
            last = None
            for c in range(nch):
                for j in range(NTT):
                    if ready is not None:
                        b.wait("dve", ready[0], ready[1])
                    rv = DVE("tensor_reduce", STAT[:, HN_QT:HN_QT + 1],
                             BIG[:, base_col + c * T + j * TT:
                                 base_col + c * T + j * TT + TT],
                             AX.X, ALU.max, apply_absolute_value=True)
                    b.wait("dve", "dve", rv)
                    last = DVE("tensor_max", STAT[:, acc_col:acc_col + 1],
                               STAT[:, acc_col:acc_col + 1],
                               STAT[:, HN_QT:HN_QT + 1])
            return last

        def bfinal(layer, scale, lq, lk):
            bq_, ibq = pool.acquire()
            bk_, ibk = pool.acquire()
            b.wait("pe", "dve", max(lq, lk))
            b.wait("pe", "d_pre", pre_tot)
            tq = MM(bq_[0:1, 0:128], STAT[:, HN_Q:HN_Q + 1], IDF[:, :],
                    start=True, stop=True, inc=True)
            tk = MM(bk_[0:1, 0:128], STAT[:, HN_K:HN_K + 1], IDF[:, :],
                    start=True, stop=True, inc=True)
            b.wait("dve", "pe", tk)
            rq = DVE("tensor_reduce", STAT[0:1, HN_Q:HN_Q + 1],
                     bq_[0:1, 0:128], AX.X, ALU.max)
            rk = DVE("tensor_reduce", STAT[0:1, HN_K:HN_K + 1],
                     bk_[0:1, 0:128], AX.X, ALU.max)
            pool.release(ibq, "dve", rq)
            pool.release(ibk, "dve", rk)
            b.wait("dve", "dve", rk)
            pv = DVE("tensor_mul", STAT[0:1, p_prod:p_prod + 1],
                     STAT[0:1, HN_Q:HN_Q + 1], STAT[0:1, HN_K:HN_K + 1])
            b.wait("dve", "dve", pv)
            nv = DVE("tensor_scalar_mul", STAT[0:1, p_t1:p_t1 + 1],
                     STAT[0:1, p_prod:p_prod + 1], -1.02 * scale)
            bk, ib = pool.acquire()
            b.wait("pe", "dve", nv)
            bv = MM(bk[0:128, 0:1], STAT[0:1, ONES_F:ONES_F + 128],
                    STAT[0:1, p_t1:p_t1 + 1], start=True, stop=True, inc=True)
            b.wait("dve", "pe", bv)
            cv = DVE("tensor_copy", STAT[:, col_negB[layer]:col_negB[layer] + 1],
                     bk[0:128, 0:1])
            pool.release(ib, "dve", cv)
            return cv

        PT1_BASE = 33344
        pt_state = {"i": 0, "cons": [], "slots": []}
        fin1_cell = [None]
        oev1_cell = [None]
        comb1_cell = [None]
        ffnq = []

        def pt_setup(slots, init_cons=None):
            pt_state["i"] = 0
            pt_state["slots"] = slots
            pt_state["cons"] = [init_cons] * len(slots)

        def pt_begin():
            i = pt_state["i"]
            pt_state["i"] = (i + 1) % len(pt_state["slots"])
            c = pt_state["cons"][i]
            if c is not None:
                b.wait("act", c[0], c[1])
            return i

        def pt_ap(i, N):
            tens, base = pt_state["slots"][i]
            return tens[:, base:base + N]

        def attn_l0(h, qt, negB_v, pend, ocell, allow_ffn=True):
            bh = 64 * (h % 2)
            ch = h // 2
            bkO, iO, bkO_cons = pool.acquire_deferred()
            bkO_wait = [bkO_cons]
            stb = []
            for _ in range(2):
                t_, i_, c_ = pool.acquire_deferred()
                stb.append([t_, i_, None, c_])
            n = 4 * qt + 4
            stage = []
            pv_last = [None]
            pend = list(pend) if pend else []

            def drain(k):
                while k > 0:
                    if pend:
                        pend.pop(0)()
                    elif ffnq:
                        ffnq.pop(0)()
                    else:
                        return
                    k -= 1

            def emit_A(idx):
                i = idx
                r = i - 4 * qt
                col0 = 128 * r if r >= 0 else 0
                N = 512 - col0
                bkS, iS, prev_ev, first_c = stb[idx % 2]
                if first_c is not None:
                    b.wait("pe", first_c[0], first_c[1])
                    stb[idx % 2][3] = None
                if prev_ev is not None:
                    b.wait("pe", "act", prev_ev)
                sv = MM(bkS[0:128, 0:N],
                        BIG[bh:bh + 64,
                            L0K + ch * T + i * 128: L0K + ch * T + i * 128 + 128],
                        BIG[bh:bh + 64,
                            L0Q + ch * T + qt * TT + col0: L0Q + ch * T + qt * TT + 512],
                        start=True, stop=True, inc=True)
                par = pt_begin()
                b.wait("act", "pe", sv)
                b.wait("act", "dve", negB_v)
                ev = ACT(pt_ap(par, N), bkS[0:128, 0:N], AF.Exp,
                         bias=STAT[:, col_negB[0]:col_negB[0] + 1], scale=0.125)
                stb[idx % 2][2] = ev
                if r >= 0:
                    b.wait("dve", "act", ev)
                    mv = DVE("tensor_mul", pt_ap(par, 128),
                             pt_ap(par, 128), MASKT[:, :])
                    pw = ("dve", mv)
                else:
                    pw = ("act", ev)
                stage.append((idx, par, pw, col0, N))

            def emit_B():
                idx, par, pw, col0, N = stage.pop(0)
                if bkO_wait[0] is not None:
                    b.wait("pe", bkO_wait[0][0], bkO_wait[0][1])
                    bkO_wait[0] = None
                b.wait("pe", pw[0], pw[1])
                padm = min(128, 16640 - (idx * 1040 + h * 65))
                pv = MM(bkO[0:padm, col0:512],
                        BIG[:, L0VT + idx * 1040 + h * 65:
                            L0VT + idx * 1040 + h * 65 + padm],
                        pt_ap(par, N),
                        start=(idx == 0), stop=(idx == n - 1), inc=True)
                pt_state["cons"][par] = ("pe", pv)
                pv_last[0] = pv

            LAG = 3
            for idx in range(n):
                emit_A(idx)
                if idx >= 3 and pend:
                    pend.pop(0)()
                elif allow_ffn and ffnq:
                    ffnq.pop(0)()
                if idx >= LAG:
                    emit_B()
            while stage:
                emit_B()
            drain(len(pend))
            for bkS, iS, prev_ev, _fc in stb:
                pool.release(iS, "act", prev_ev)

            env = {}

            b.wait("dve", "pe", pv_last[0])
            env["rv"] = DVE("reciprocal", SCR[64:65, SC_Y:SC_Y + 512],
                            bkO[64:65, 0:512])

            def s_bcast():
                bkB, iB = pool.acquire()
                env["bkB"], env["iB"] = bkB, iB
                b.wait("pe", "dve", env["rv"])
                env["bv"] = MM(bkB[0:64, 0:512],
                               SMALL[64:65, ONES_B:ONES_B + 64],
                               SCR[64:65, SC_Y:SC_Y + 512],
                               start=True, stop=True, inc=True)

            def s_bev():
                b.wait("act", "pe", env["bv"])
                bev = ACT(SCR[0:64, SC_U:SC_U + 512],
                          env["bkB"][0:64, 0:512], AF.Copy)
                env["bev"] = bev
                pool.release(env["iB"], "act", bev)

            def s_final():
                ocol = L0Q + ch * T + qt * TT
                b.wait("dve", "act", env["bev"])
                if h % 2 == 0:
                    cv = DVE("tensor_mul", BIG[0:64, ocol:ocol + TT],
                             bkO[0:64, 0:512], SCR[0:64, SC_U:SC_U + 512])
                    pool.release(iO, "dve", cv)
                else:
                    yv = DVE("tensor_mul", SCR[0:64, SC_U:SC_U + 512],
                             bkO[0:64, 0:512], SCR[0:64, SC_U:SC_U + 512])
                    pool.release(iO, "dve", yv)
                    bkR, iR = pool.acquire()
                    b.wait("pe", "dve", yv)
                    rrv = MM(bkR[64:128, 0:512], IDT[0:64, 0:64],
                             SCR[0:64, SC_U:SC_U + 512], start=True, stop=True,
                             inc=True, tp=(0, 64))
                    b.wait("dve", "pe", rrv)
                    cv = DVE("tensor_copy", BIG[64:128, ocol:ocol + TT],
                             bkR[64:128, 0:512])
                    pool.release(iR, "dve", cv)
                ocell[0] = cv

            return [s_bcast, s_bev, s_final]

        def attn_l1(h, qt, negB_v, kb_tot, lam_v, pend, ocell):
            g = h // 4
            bh = 64 * (h % 2)
            ksrc = L1KA if 64 * (g % 2) == bh else L1KB
            kcol = ksrc + (g // 2) * T
            jO, jO_cons = pool.acquire_pair_deferred()
            bkO1, iO1 = ps[2 * jO], 2 * jO
            bkO2, iO2 = ps[2 * jO + 1], 2 * jO + 1
            jS_d = [pool.acquire_pair_deferred(), pool.acquire_pair_deferred()]
            jSs = [jS_d[0][0], jS_d[1][0]]
            st_wait = [jS_d[0][1], jS_d[1][1]]
            jO_wait = [jO_cons]
            st_ev = [None, None]
            n = 4 * qt + 4
            stage = []
            pv_last = [None]
            b.wait("pe", "d_kb", kb_tot)
            pend = list(pend) if pend else []

            def drain(k):
                while pend and k > 0:
                    pend.pop(0)()
                    k -= 1

            def emit_A(idx):
                i = idx
                r = i - 4 * qt
                col0 = 128 * r if r >= 0 else 0
                N = 512 - col0
                w = idx % 2
                jS = jSs[w]
                if st_wait[w] is not None:
                    for c_ in st_wait[w]:
                        if c_ is not None:
                            b.wait("pe", c_[0], c_[1])
                    st_wait[w] = None
                if st_ev[w] is not None:
                    b.wait("pe", "act", st_ev[w])
                svs = None
                for s in range(2):
                    rb = bh + 32 * s
                    svs = MM(ps[2 * jS + s][0:128, 0:N],
                             BIG[rb:rb + 32, kcol + i * 128: kcol + i * 128 + 128],
                             BIG[rb:rb + 32,
                                 L1Q + (h // 2) * T + qt * TT + col0:
                                 L1Q + (h // 2) * T + qt * TT + 512],
                             start=True, stop=True, inc=(s == 1), tp=(rb, 0))
                par = pt_begin()
                ptt, ptb = pt_state["slots"][par]
                in_ap = ps_big[jS][0:128, 0:1024].rearrange(
                    "p (s x) -> p s x", s=2)[:, :, 0:N]
                out_ap = ptt[:, ptb:ptb + 1024].rearrange(
                    "p (s x) -> p s x", s=2)[:, :, 0:N]
                b.wait("act", "pe", svs)
                b.wait("act", "dve", negB_v)
                ev = ACT(out_ap, in_ap, AF.Exp,
                         bias=STAT[:, col_negB[1]:col_negB[1] + 1],
                         scale=0.17677669529663687)
                st_ev[w] = ev
                if r >= 0:
                    b.wait("dve", "act", ev)
                    for s2_ in range(2):
                        mo = ptb + 512 * s2_
                        mv = DVE("tensor_mul", ptt[:, mo:mo + 128],
                                 ptt[:, mo:mo + 128], MASKT[:, :])
                    pw = ("dve", mv)
                else:
                    pw = ("act", ev)
                stage.append((idx, par, pw, col0, N))

            def emit_B():
                idx, par, pw, col0, N = stage.pop(0)
                ptt, ptb = pt_state["slots"][par]
                pv = None
                if jO_wait[0] is not None:
                    for c_ in jO_wait[0]:
                        if c_ is not None:
                            b.wait("pe", c_[0], c_[1])
                    jO_wait[0] = None
                padm = min(128, 4160 + 2 * T - (idx * 260 + g * 65))
                for s in range(2):
                    b.wait("pe", pw[0], pw[1])
                    bkT = bkO1 if s == 0 else bkO2
                    pv = MM(bkT[0:padm, col0:512],
                            BIG[:, L1VT + idx * 260 + g * 65:
                                L1VT + idx * 260 + g * 65 + padm],
                            ptt[:, ptb + 512 * s: ptb + 512 * s + N],
                            start=(idx == 0), stop=(idx == n - 1), inc=(s == 1))
                pt_state["cons"][par] = ("pe", pv)
                pv_last[0] = pv

            LAG = 3
            for idx in range(n):
                emit_A(idx)
                if idx >= 2:
                    drain(1)
                if idx >= LAG:
                    emit_B()
            while stage:
                emit_B()
            while len(pend) > 4:
                pend.pop(0)()
            for w in range(2):
                for s in range(2):
                    pool.release(2 * jSs[w] + s, "act", st_ev[w])

            env = {}

            b.wait("act", "pe", pv_last[0])
            if comb1_cell[0] is not None:
                b.wait("act", "dve", comb1_cell[0])
            oe1 = ACT(SCR[0:65, 0:512], bkO1[0:65, 0:512], AF.Copy)
            oe2 = ACT(SCR[0:65, 512:1024], bkO2[0:65, 0:512], AF.Copy)
            env["oe"] = oe2
            pool.release(iO1, "act", oe2)
            pool.release(iO2, "act", oe2)
            b.wait("dve", "act", oe2)
            env["r1"] = DVE("reciprocal", SCR[64:65, SC_Y:SC_Y + 512],
                            SCR[64:65, 0:512])
            env["r2"] = DVE("reciprocal", SCR[64:65, SC_U:SC_U + 512],
                            SCR[64:65, 512:1024])
            b.wait("dve", "dve", env["r2"])
            b.wait("dve", "act", lam_v)
            env["r2l"] = DVE("tensor_scalar_mul",
                             SCR[64:65, SC_U:SC_U + 512],
                             SCR[64:65, SC_U:SC_U + 512],
                             STAT[64:65, 160 + h:161 + h])

            def s_bc1():
                jB = pool.acquire_pair()
                env["bkB1"], env["iB1"] = ps[2 * jB], 2 * jB
                env["bkB2"], env["iB2"] = ps[2 * jB + 1], 2 * jB + 1
                b.wait("pe", "dve", env["r1"])
                env["bv1"] = MM(env["bkB1"][0:64, 0:512],
                                SMALL[64:65, ONES_B:ONES_B + 64],
                                SCR[64:65, SC_Y:SC_Y + 512],
                                start=True, stop=True, inc=True)

            def s_bc2():
                b.wait("pe", "dve", env["r2l"])
                env["bv2"] = MM(env["bkB2"][0:64, 0:512],
                                SMALL[64:65, ONES_B:ONES_B + 64],
                                SCR[64:65, SC_U:SC_U + 512],
                                start=True, stop=True, inc=True)

            def s_t1():
                b.wait("dve", "pe", env["bv1"])
                b.wait("dve", "act", env["oe"])
                env["t1"] = DVE("tensor_mul", SCR[0:64, SC_Y:SC_Y + 512],
                                SCR[0:64, 0:512], env["bkB1"][0:64, 0:512])

            def s_u():
                b.wait("dve", "pe", env["bv2"])
                u = DVE("tensor_mul", SCR[0:64, SC_U:SC_U + 512],
                        SCR[0:64, 512:1024], env["bkB2"][0:64, 0:512])
                comb1_cell[0] = u
                b.wait("dve", "dve", u)
                yv = DVE("tensor_sub", SCR[0:64, SC_Y:SC_Y + 512],
                         SCR[0:64, SC_Y:SC_Y + 512],
                         SCR[0:64, SC_U:SC_U + 512])
                b.wait("dve", "dve", yv)
                env["sqv"] = DVE("tensor_mul", SCR[0:64, SC_U:SC_U + 512],
                                 SCR[0:64, SC_Y:SC_Y + 512],
                                 SCR[0:64, SC_Y:SC_Y + 512])
                env["uv"] = u

            def s_ss():
                b.wait("pe", "dve", env["sqv"])
                env["ssv"] = MM(env["bkB1"][0:1, 0:512],
                                SMALL[0:64, ONES_B:ONES_B + 1],
                                SCR[0:64, SC_U:SC_U + 512],
                                start=True, stop=True, inc=True)

            def s_ln():
                b.wait("act", "pe", env["ssv"])
                b.wait("act", "dve", e6_v)
                c = scrf_st["free"]
                if c is not None:
                    b.wait("act", c[0], c[1])
                lnv = ACT(SCRF[0:1, 0:512], env["bkB1"][0:1, 0:512], AF.Ln,
                          bias=STAT[0:1, col_eps6:col_eps6 + 1], scale=1.0 / 64.0)
                pool.release(env["iB1"], "act", lnv)
                env["lnv"] = lnv

            def s_exp():
                b.wait("act", "act", env["lnv"])
                env["xev"] = ACT(BIG[0:1, 32832:32832 + 512],
                                 SCRF[0:1, 0:512], AF.Exp, scale=-0.5)

            def s_rb():
                b.wait("pe", "dve", env["uv"])
                b.wait("pe", "act", env["xev"])
                env["rbv"] = MM(env["bkB2"][0:64, 0:512],
                                SMALL[0:1, ONES_B:ONES_B + 64],
                                BIG[0:1, 32832:32832 + 512],
                                start=True, stop=True, inc=True)

            def s_final():
                ocol = L1Q + (h // 2) * T + qt * TT
                b.wait("dve", "pe", env["rbv"])
                if h % 2 == 0:
                    cv = DVE("scalar_tensor_tensor", BIG[0:64, ocol:ocol + TT],
                             SCR[0:64, SC_Y:SC_Y + 512],
                             STAT[0:64, col_rmsw:col_rmsw + 1],
                             env["bkB2"][0:64, 0:512], ALU.mult, ALU.mult)
                    pool.release(env["iB2"], "dve", cv)
                else:
                    yv2 = DVE("scalar_tensor_tensor", SCR[0:64, SC_U:SC_U + 512],
                              SCR[0:64, SC_Y:SC_Y + 512],
                              STAT[0:64, col_rmsw:col_rmsw + 1],
                              env["bkB2"][0:64, 0:512], ALU.mult, ALU.mult)
                    pool.release(env["iB2"], "dve", yv2)
                    bkR, iR = pool.acquire()
                    b.wait("pe", "dve", yv2)
                    rrv = MM(bkR[64:128, 0:512], IDT[0:64, 0:64],
                             SCR[0:64, SC_U:SC_U + 512], start=True, stop=True,
                             inc=True, tp=(0, 64))
                    b.wait("dve", "pe", rrv)
                    cv = DVE("tensor_copy", BIG[64:128, ocol:ocol + TT],
                             bkR[64:128, 0:512])
                    pool.release(iR, "dve", cv)
                ocell[0] = cv
                fin1_cell[0] = cv

            return pend + [s_bc1, s_bc2, s_t1, s_u, s_ss,
                           s_ln, s_exp, s_rb, s_final]

        def finish():
            for e in ("pe", "act", "dve", "gp"):
                b.wait("sp", e, b.ctr[e])
            ytot = 0
            for c in range(NCH):
                ytot = DMA("sp", "d_y", yT[c * 128:(c + 1) * 128, :],
                           X[:, c * T:(c + 1) * T])
            if debug:
                dtot = DMA("sp", "d_dbg", dbg_xn[:, :], XN[:, :])
                dtot = DMA("sp", "d_dbg", dbg_big[:, :], BIG[:, :])
                dtot = DMA("sp", "d_dbg", dbg_stat[:, :], STAT[:, :])
                b.wait("sp", "d_dbg", dtot)
            b.wait("sp", "d_y", ytot)

        done = [False]

        def phase_end(name):
            if upto == name:
                finish()
                done[0] = True
            return done[0]

        while True:
            ln0_v, _, ln0_tevs = layernorm(0, None)
            if phase_end("ln0"):
                break

            vt0_ones = vt_ones(L0VT, 1040)
            qd = projection("wq0", list(range(8)), evac_to(L0Q),
                            rhs_tevs=ln0_tevs)
            lq0 = bscan(L0Q, NCH, HN_Q, qd)
            kd = projection("wk0", list(range(8)), evac_to(L0K))
            lk0 = bscan(L0K, NCH, HN_K, kd)
            b.wait("act", "dve", vt0_ones)
            v_project("wv0", list(range(8)), L0VT, 2, 1040)
            negB0_v = bfinal(0, 0.125, lq0, lk0)

            qst = {"mx_last": None}

            def quant_scan_step(hb):
                def f():
                    if hb == 0:
                        qst["mx_last"] = DVE("memset",
                                             STAT[:, col_mx:col_mx + 1], 0.0)
                    b.wait("sp", scrf_st["free"][0], scrf_st["free"][1])
                    qs_tot = DMA("sp", "d_qs", SCRF[:, 0:512],
                                 dr["ker0"][hb // 2, :,
                                            (hb % 2) * 512:(hb % 2) * 512 + 512])
                    b.wait("dve", "d_qs", qs_tot)
                    b.wait("dve", "dve", qst["mx_last"])
                    rv = DVE("tensor_reduce", STAT[:, col_mxt:col_mxt + 1],
                             SCRF[:, 0:512], AX.X, ALU.max,
                             apply_absolute_value=True)
                    b.wait("dve", "dve", rv)
                    qst["mx_last"] = DVE("tensor_max",
                                         STAT[:, col_mx:col_mx + 1],
                                         STAT[:, col_mx:col_mx + 1],
                                         STAT[:, col_mxt:col_mxt + 1])
                    scrf_st["free"] = ("dve", rv)
                return f

            for hb in range(16):
                ffnq.append(quant_scan_step(hb))
            qsc = {}

            def quant_tail():
                bmx, imx = pool.acquire()
                b.wait("pe", "dve", qst["mx_last"])
                b.wait("pe", "d_pre", pre_tot)
                tmx = MM(bmx[0:1, 0:128], STAT[:, col_mx:col_mx + 1], IDF[:, :],
                         start=True, stop=True, inc=True)
                b.wait("dve", "pe", tmx)
                mxr = DVE("tensor_reduce", STAT[0:1, p_t1:p_t1 + 1],
                          bmx[0:1, 0:128], AX.X, ALU.max)
                pool.release(imx, "dve", mxr)
                b.wait("act", "dve", mxr)
                sv2 = ACT(STAT[0:1, p_idx:p_idx + 1], STAT[0:1, p_t1:p_t1 + 1],
                          AF.Copy, scale=1.0 / 7.0)
                b.wait("dve", "act", sv2)
                iv = DVE("reciprocal", STAT[0:1, p_m:p_m + 1],
                         STAT[0:1, p_idx:p_idx + 1])
                bki, ii = pool.acquire()
                bks, is_ = pool.acquire()
                b.wait("pe", "dve", iv)
                b.wait("pe", "dve", ov1)
                v1 = MM(bki[0:128, 0:1], STAT[0:1, ONES_F:ONES_F + 128],
                        STAT[0:1, p_m:p_m + 1], start=True, stop=True, inc=True)
                b.wait("pe", "act", sv2)
                v2 = MM(bks[0:128, 0:1], STAT[0:1, ONES_F:ONES_F + 128],
                        STAT[0:1, p_idx:p_idx + 1], start=True, stop=True,
                        inc=True)
                b.wait("dve", "pe", v1)
                qsc["isc"] = DVE("tensor_copy", STAT[:, col_isc:col_isc + 1],
                                 bki[0:128, 0:1])
                b.wait("dve", "pe", v2)
                qsc["sc"] = DVE("tensor_copy", STAT[:, col_sc:col_sc + 1],
                                bks[0:128, 0:1])
                pool.release(ii, "dve", qsc["isc"])
                pool.release(is_, "dve", qsc["sc"])

            ffnq.append(quant_tail)

            if phase_end("qkv0"):
                break

            ffn_st = {"cons": wslot["cons"][0], "x_last": None}

            def ffn0_band_substeps(mb):
                bb = {}

                def s_q(half):
                    def f():
                        b.wait("sp", scrf_st["free"][0], scrf_st["free"][1])
                        qs_tot = DMA("sp", "d_qs", SCRF[:, 0:512],
                                     dr["ker0"][mb, :, half * 512:half * 512 + 512])
                        b.wait("dve", "d_qs", qs_tot)
                        b.wait("dve", "dve", qsc["isc"])
                        if ffn_st["cons"] is not None:
                            b.wait("dve", ffn_st["cons"][0], ffn_st["cons"][1])
                        t1v = DVE("tensor_scalar_mul", SCRF[:, 0:512],
                                  SCRF[:, 0:512], STAT[:, col_isc:col_isc + 1])
                        b.wait("dve", "dve", t1v)
                        t2v = DVE("tensor_scalar_add", SCRF[:, 0:512],
                                  SCRF[:, 0:512], MAGIC)
                        b.wait("dve", "dve", t2v)
                        t3v = DVE("tensor_scalar_sub", SCRF[:, 0:512],
                                  SCRF[:, 0:512], MAGIC)
                        b.wait("dve", "dve", t3v)
                        bb["qh%d" % half] = DVE(
                            "tensor_scalar", WB[:, half * 512:half * 512 + 512],
                            SCRF[:, 0:512], -8.0, 7.0, ALU.max, ALU.min)
                        scrf_st["free"] = ("dve", bb["qh%d" % half])
                    return f

                def s_mm(jp, cg):
                    def f():
                        if cg == 0:
                            jpr = pool.acquire_pair()
                            bb["banks%d" % jp] = [
                                (ps[2 * jpr], 2 * jpr),
                                (ps[2 * jpr + 1], 2 * jpr + 1)]
                        b.wait("pe", "dve", bb["qh%d" % cg])
                        banks = bb["banks%d" % jp]
                        for c in range(4 * cg, 4 * cg + 4):
                            for jj in range(2):
                                j = jp * 2 + jj
                                bb["lastv"] = MM(
                                    banks[jj][0][0:128, 0:TT],
                                    WB[:, c * 128:c * 128 + 128],
                                    XN[:, c * T + j * TT: c * T + j * TT + TT],
                                    start=(c == 0), stop=(c == NCH - 1),
                                    inc=(c == NCH - 1 and jj == 1))
                    return f

                def s_ev(jp):
                    def f():
                        for jj in range(2):
                            j = jp * 2 + jj
                            bk, ib = bb["banks%d" % jp][jj]
                            b.wait("dve", "pe", bb["lastv"])
                            b.wait("dve", "dve", qsc["sc"])
                            if ffn_st["x_last"] is not None:
                                b.wait("dve", "dve", ffn_st["x_last"])
                            ffn_st["x_last"] = DVE(
                                "scalar_tensor_tensor",
                                X[:, mb * T + j * TT: mb * T + j * TT + TT],
                                bk[0:128, 0:TT], STAT[:, col_sc:col_sc + 1],
                                X[:, mb * T + j * TT: mb * T + j * TT + TT],
                                ALU.mult, ALU.add)
                            pool.release(ib, "dve", ffn_st["x_last"])
                        if jp == 1:
                            ffn_st["cons"] = ("pe", bb["lastv"])
                    return f

                def s_bias():
                    b.wait("dve", "d_pre", pre_tot)
                    b.wait("dve", "dve", ffn_st["x_last"])
                    ffn_st["x_last"] = DVE(
                        "tensor_scalar_add", X[:, mb * T:(mb + 1) * T],
                        X[:, mb * T:(mb + 1) * T],
                        STAT[:, col_bias0 + mb:col_bias0 + mb + 1])

                return [s_q(0), s_q(1), s_mm(0, 0), s_mm(0, 1), s_ev(0),
                        s_mm(1, 0), s_mm(1, 1), s_ev(1), s_bias]

            for mb in range(8):
                ffnq.extend(ffn0_band_substeps(mb))

            wslot["par"] = 0
            pt_setup([(SCR, PT[0]), (SCR, PT[1]), (WB, 1024), (WB, 1536)],
                     init_cons=wslot["cons"][1])
            ocell = [None]
            pend = None
            for h in range(_NHEADS):
                for qt in range(NTT):
                    pend = attn_l0(h, qt, negB0_v, pend, ocell,
                                   allow_ffn=(h * NTT + qt >= 6))
            for st_ in pend:
                st_()
            while ffnq:
                ffnq.pop(0)()
            ocv = ocell[0]
            ptc = [c for c in pt_state["cons"][2:] if c is not None]
            if ptc:
                wslot["cons"][1] = ("pe", max(v for _, v in ptc))
            wslot["cons"][0] = ffn_st["cons"]
            x_last = ffn_st["x_last"]

            cA = wslot["cons"][0]
            if cA is not None:
                b.wait("gp", cA[0], cA[1])
            ta = DMA("gp", wslot["sem"][0], WB[:, 0:128], dr["a0"][:, :])
            cB = wslot["cons"][1]
            if cB is not None:
                b.wait("gp", cB[0], cB[1])
            tb = DMA("gp", wslot["sem"][1], WB[0:16, 1024:2048], dr["b0"][:, :])
            b.wait("pe", wslot["sem"][0], ta)
            xa_banks = [pool.acquire() for _ in range(NTT)]
            lastxa = None
            for c in range(NCH):
                for j in range(NTT):
                    lastxa = MM(xa_banks[j][0][0:16, 0:TT],
                                WB[:, c * 16:c * 16 + 16],
                                XN[:, c * T + j * TT: c * T + j * TT + TT],
                                start=(c == 0), stop=(c == NCH - 1),
                                inc=(c == NCH - 1 and j == NTT - 1))
            wslot["cons"][0] = ("pe", lastxa)
            xa_evs = []
            for j in range(NTT):
                bk, ib = xa_banks[j]
                b.wait("act", "pe", lastxa)
                ev = ACT(SCR[0:16, j * TT:(j + 1) * TT], bk[0:16, 0:TT], AF.Copy)
                pool.release(ib, "act", ev)
                xa_evs.append(ev)
            b.wait("pe", wslot["sem"][1], tb)
            lora_last = None
            for mb in range(8):
                banks = [pool.acquire() for _ in range(NTT)]
                lastv = None
                for j in range(NTT):
                    b.wait("pe", "act", xa_evs[j])
                    lastv = MM(banks[j][0][0:128, 0:TT],
                               WB[0:16, 1024 + mb * 128:1024 + mb * 128 + 128],
                               SCR[0:16, j * TT:(j + 1) * TT],
                               start=True, stop=True, inc=(j == NTT - 1))
                if mb == 7:
                    wslot["cons"][1] = ("pe", lastv)
                for j in range(NTT):
                    bk, ib = banks[j]
                    b.wait("dve", "pe", lastv)
                    b.wait("dve", "dve", x_last)
                    x_last = DVE("tensor_add",
                                 X[:, mb * T + j * TT: mb * T + j * TT + TT],
                                 X[:, mb * T + j * TT: mb * T + j * TT + TT],
                                 bk[0:128, 0:TT])
                    pool.release(ib, "dve", x_last)
            if phase_end("attn0"):
                break

            x_last, xb1 = out_proj("wp0", ("dve", ocv), L0Q, make_xn=True)
            if phase_end("l0"):
                break

            ln1_v, kept, _ = layernorm(1, None, xb_pre=xb1)
            bk_m, i_m, bk_r, i_r = kept

            xl_ap = X.ap().rearrange("p (c t) -> p c t", t=T)[:, :, T - 1:T]
            xnl_v = DVE("tensor_scalar", STAT[:, col_b1s:col_b1s + 8], xl_ap,
                        bk_m[0:128, 511:512], bk_r[0:128, 511:512],
                        ALU.subtract, ALU.mult)
            b.wait("dve", "dve", xnl_v)
            dbgc = DVE("tensor_copy", STAT[:, 796:804], STAT[:, col_b1s:col_b1s + 8])
            dbgc = DVE("tensor_copy", STAT[:, 804:805], bk_m[0:128, 511:512])
            dbgc = DVE("tensor_copy", STAT[:, 805:806], bk_r[0:128, 511:512])
            pool.release(i_m, "dve", dbgc)
            pool.release(i_r, "dve", dbgc)
            bz1, iz1 = pool.acquire()
            lastz1 = None
            for half in range(2):
                b.wait("sp", scrf_st["free"][0], scrf_st["free"][1])
                qs_tot = DMA("sp", "d_qs", SCRF[:, 0:512],
                             dr["wp1"][:, half * 512:half * 512 + 512])
                b.wait("pe", "d_qs", qs_tot)
                b.wait("pe", "dve", xnl_v)
                for cc in range(4):
                    c = half * 4 + cc
                    lastz1 = MM(bz1[0:128, 0:1], SCRF[:, cc * 128:cc * 128 + 128],
                                STAT[:, col_b1s + c:col_b1s + c + 1],
                                start=(c == 0), stop=(c == 7),
                                inc=(cc == 3))
                scrf_st["free"] = ("pe", lastz1)
            b.wait("act", "pe", lastz1)
            b.wait("act", "d_pre", pre_tot)
            g_v = ACT(STAT[:, col_g:col_g + 1], bz1[0:128, 0:1], AF.Gelu,
                      bias=STAT[:, col_bp1:col_bp1 + 1], scale=1.0)
            pool.release(iz1, "act", g_v)
            bz2, iz2 = pool.acquire()
            b.wait("pe", "act", g_v)
            z2_v = MM(bz2[0:1, 0:8], STAT[:, col_g:col_g + 1],
                      STAT[:, col_wp2:col_wp2 + 8], start=True, stop=True, inc=True)
            b.wait("dve", "pe", z2_v)
            lg_v = DVE("tensor_add", STAT[0:1, p_log:p_log + 8], bz2[0:1, 0:8],
                       STAT[0:1, p_bp2:p_bp2 + 8])
            pool.release(iz2, "dve", lg_v)
            b.wait("dve", "dve", lg_v)
            m_v = DVE("tensor_reduce", STAT[0:1, p_m:p_m + 1],
                      STAT[0:1, p_log:p_log + 8], AX.X, ALU.max)
            b.wait("dve", "dve", m_v)
            eq_v = DVE("tensor_scalar", STAT[0:1, p_eq:p_eq + 8],
                       STAT[0:1, p_log:p_log + 8], STAT[0:1, p_m:p_m + 1], None,
                       ALU.is_equal)
            b.wait("dve", "dve", eq_v)
            c_v = DVE("tensor_mul", STAT[0:1, p_cand:p_cand + 8],
                      STAT[0:1, p_eq:p_eq + 8], STAT[0:1, p_io99:p_io99 + 8])
            b.wait("dve", "dve", c_v)
            c2_v = DVE("tensor_scalar_add", STAT[0:1, p_cand:p_cand + 8],
                       STAT[0:1, p_cand:p_cand + 8], 99.0)
            b.wait("dve", "dve", c2_v)
            ix_v = DVE("tensor_reduce", STAT[0:1, p_idx:p_idx + 1],
                       STAT[0:1, p_cand:p_cand + 8], AX.X, ALU.min)
            b.wait("dve", "dve", ix_v)
            ixi_v = DVE("tensor_copy", IDXI[0:1, 0:1], STAT[0:1, p_idx:p_idx + 1])

            selcell = {}

            def selfn(e):
                r0 = e.alloc_register("selreg")
                e.reg_load(r0, IDXI[0:1, 0:1])
                selcell["r0"] = r0
                selcell["v"] = e.snap(r0, min_val=0, max_val=7)
                selcell["e"] = e
                return None

            def fresh_sel():
                return selcell["e"].snap(selcell["r0"], min_val=0, max_val=7)
            b.wait("gp", "dve", ixi_v)
            b.op("gp", selfn, inc=False)

            def dynfn(out_ap, src, tag=""):
                def fn(e, o=out_ap, s=src, t=tag):
                    try:
                        return e.dma_start(out=o, in_=s())
                    except Exception:
                        print("DYN DMA FAIL tag=", t)
                        raise
                return fn
            b.wait("gp", "pe", lastz1)
            sm_tot = b.dma("gp", "d_sm", dynfn(
                STAT[:, col_b1s:col_b1s + 16],
                lambda: dr["b1e"][bass.ds(selcell["v"], 1), :].rearrange(
                    "e (p q) -> (e p) q", p=128)))
            sm_tot = b.dma("gp", "d_sm", dynfn(
                STAT[:, col_b2s:col_b2s + 8],
                lambda: dr["b2e"][bass.ds(selcell["v"], 1), :].rearrange(
                    "e (p q) -> (e p) q", p=128)))

            b.wait("act", "d_pre", pre_tot)
            lam_v = ACT(STAT[64:65, 160:176], STAT[64:65, 176:192], AF.Sigmoid)

            for c in range(NCH):
                b.wait("dve", "dve", ln1_v)
                b.wait("dve", "dve", x_last)
                x_last = DVE("tensor_add", X[:, c * T:(c + 1) * T],
                             X[:, c * T:(c + 1) * T], XN[:, c * T:(c + 1) * T])
            if phase_end("router"):
                break

            b.wait("act", "d_sm", sm_tot)
            h_done = None
            for grp_i in range(8):
                parA, cA = slot_begin()
                parB, cB = slot_begin()
                for cs in (cA, cB):
                    if cs is not None:
                        b.wait("gp", cs[0], cs[1])
                w1tot = b.dma("gp", wslot["sem"][0], dynfn(
                    WB[:, 0:2048],
                    lambda g=grp_i: dr["w1e"][bass.ds(selcell["v"], 1), :].rearrange(
                        "e (m p x) -> (e p) m x", m=16, p=128)[:, 2 * g:2 * g + 2, :],
                    tag=f"w1 grp={grp_i}"))
                b.wait("pe", wslot["sem"][0], w1tot)
                lastv = None
                for sub in range(2):
                    mb = 2 * grp_i + sub
                    banks = [pool.acquire() for _ in range(NTT)]
                    for c in range(NCH):
                        for j in range(NTT):
                            lastv = MM(banks[j][0][0:128, 0:TT],
                                       WB[:, sub * 1024 + c * 128:
                                          sub * 1024 + c * 128 + 128],
                                       XN[:, c * T + j * TT: c * T + j * TT + TT],
                                       start=(c == 0), stop=(c == NCH - 1),
                                       inc=(c == NCH - 1 and j == NTT - 1))
                    for j in range(NTT):
                        bk, ib = banks[j]
                        b.wait("act", "pe", lastv)
                        h_done = ACT(BIG[:, L1H + mb * T + j * TT:
                                         L1H + mb * T + j * TT + TT],
                                     bk[0:128, 0:TT], AF.Gelu,
                                     bias=STAT[:, col_b1s + mb:col_b1s + mb + 1],
                                     scale=1.0)
                        pool.release(ib, "act", h_done)
                band_release(parA, "pe", lastv)
                band_release(parB, "pe", lastv)
            rs_done = None
            for mb in range(8):
                parA, cA = slot_begin()
                parB, cB = slot_begin()
                for cs in (cA, cB):
                    if cs is not None:
                        b.wait("gp", cs[0], cs[1])
                w2tot = b.dma("gp", wslot["sem"][1], dynfn(
                    WB[:, 0:2048],
                    lambda mb=mb: dr["w2e"][
                        bass.ds(selcell["v"], 1),
                        mb * 262144:(mb + 1) * 262144].rearrange(
                        "e (p q) -> (e p) q", p=128),
                    tag=f"w2 mb={mb}"))
                banks = [pool.acquire() for _ in range(NTT)]
                lastv = None
                b.wait("pe", "act", h_done)
                b.wait("pe", wslot["sem"][1], w2tot)
                for c in range(16):
                    for j in range(NTT):
                        lastv = MM(banks[j][0][0:128, 0:TT],
                                   WB[:, c * 128:c * 128 + 128],
                                   BIG[:, L1H + c * T + j * TT:
                                       L1H + c * T + j * TT + TT],
                                   start=(c == 0), stop=(c == 15),
                                   inc=(c == 15 and j == NTT - 1))
                band_release(parA, "pe", lastv)
                band_release(parB, "pe", lastv)
                for j in range(NTT):
                    bk, ib = banks[j]
                    b.wait("dve", "pe", lastv)
                    b.wait("dve", "d_sm", sm_tot)
                    rs_done = DVE("tensor_single_scalar",
                                  BIG[:, L1RS + mb * T + j * TT:
                                      L1RS + mb * T + j * TT + TT],
                                  bk[0:128, 0:TT],
                                  STAT[:, col_b2s + mb:col_b2s + mb + 1], ALU.add)
                    pool.release(ib, "dve", rs_done)
            bo3_v = DVE("tensor_scalar_mul", STAT[:, col_bo3:col_bo3 + 8],
                        STAT[:, col_bo:col_bo + 8], 0.3)
            for mb in range(8):
                par, sem, tot = band_load(dr["wof"][mb:mb + 1, :, :], cols=1024)
                b.wait("pe", sem, tot)
                b.wait("pe", "dve", rs_done)
                banks = [pool.acquire() for _ in range(NTT)]
                lastv = None
                for c in range(NCH):
                    for j in range(NTT):
                        lastv = MM(banks[j][0][0:128, 0:TT],
                                   WB[:, par * 1024 + c * 128:
                                      par * 1024 + c * 128 + 128],
                                   BIG[:, L1RS + c * T + j * TT:
                                       L1RS + c * T + j * TT + TT],
                                   start=(c == 0), stop=(c == NCH - 1),
                                   inc=(c == NCH - 1 and j == NTT - 1))
                band_release(par, "pe", lastv)
                for j in range(NTT):
                    bk, ib = banks[j]
                    b.wait("dve", "pe", lastv)
                    b.wait("dve", "dve", x_last)
                    x_last = DVE("scalar_tensor_tensor",
                                 X[:, mb * T + j * TT: mb * T + j * TT + TT],
                                 bk[0:128, 0:TT], 0.3,
                                 X[:, mb * T + j * TT: mb * T + j * TT + TT],
                                 ALU.mult, ALU.add)
                    pool.release(ib, "dve", x_last)
                b.wait("dve", "dve", bo3_v)
                b.wait("dve", "dve", x_last)
                x_last = DVE("tensor_scalar_add", X[:, mb * T:(mb + 1) * T],
                             X[:, mb * T:(mb + 1) * T],
                             STAT[:, col_bo3 + mb:col_bo3 + mb + 1])
            if phase_end("ffn1"):
                break

            b.wait("sp", "pe", b.ctr["pe"])
            cs_tot = DMA("sp", "d_cs", BIG[:, L1CS:L1CS + 2 * T], dr["cs"][:, :])
            b.wait("dve", "pe", b.ctr["pe"])
            vt1_ones = vt_ones(L1VT, 260)

            shuf = []
            for i2 in range(0, 32, 2):
                shuf += [i2 + 1, i2]

            def rope(bases, ready):
                done = None
                b.wait("dve", "d_cs", cs_tot)
                if ready is not None:
                    b.wait("dve", ready[0], ready[1])
                for base in bases:
                    for hf in range(2):
                        o = base + hf * 1024
                        co = L1CS + hf * 1024
                        so = L1CS + T + hf * 1024
                        sh_v = DVE("stream_shuffle", SCR[:, SC_Y:SC_Y + 1024],
                                   BIG[:, o:o + 1024], shuf)
                        b.wait("dve", "dve", sh_v)
                        s_v = DVE("tensor_mul", SCR[:, SC_Y:SC_Y + 1024],
                                  SCR[:, SC_Y:SC_Y + 1024],
                                  BIG[:, so:so + 1024])
                        c_v2 = DVE("tensor_mul", BIG[:, o:o + 1024],
                                   BIG[:, o:o + 1024], BIG[:, co:co + 1024])
                        b.wait("dve", "dve", s_v)
                        b.wait("dve", "dve", c_v2)
                        done = DVE("tensor_add", BIG[:, o:o + 1024],
                                   BIG[:, o:o + 1024], SCR[:, SC_Y:SC_Y + 1024])
                return done

            lq1 = lk1 = None
            for mb in range(8):
                qd = projection("wq1", [mb], evac_to(L1Q))
                ropeq_v = rope([L1Q + mb * T], qd)
                lq1 = bscan(L1Q + mb * T, 1, HN_Q, ("dve", ropeq_v),
                            init=(mb == 0))
            ropek_v = None
            for mb in range(2):
                kd = projection("wk1", [mb], evac_to(L1KA))
                ropek_v = rope([L1KA + mb * T], kd)
                lk1 = bscan(L1KA + mb * T, 1, HN_K, ("dve", ropek_v),
                            init=(mb == 0))
            b.wait("sp", "dve", ropek_v)
            kb_tot = 0
            for cc in range(2):
                kb_tot = DMA("sp", "d_kb",
                             BIG[0:64, L1KB + cc * T:L1KB + (cc + 1) * T],
                             BIG[64:128, L1KA + cc * T:L1KA + (cc + 1) * T])
                kb_tot = DMA("sp", "d_kb",
                             BIG[64:128, L1KB + cc * T:L1KB + (cc + 1) * T],
                             BIG[0:64, L1KA + cc * T:L1KA + (cc + 1) * T])
            b.wait("act", "dve", vt1_ones)
            v_project("wv1", list(range(2)), L1VT, 2, 260)
            negB1_v = bfinal(1, 32 ** -0.5, lq1, lk1)
            if phase_end("qkv1"):
                break

            pt_setup([(BIG, PT1_BASE), (BIG, PT1_BASE + 1024),
                      (BIG, PT1_BASE + 2048), (BIG, PT1_BASE + 3072)])
            ocell = [None]
            pend = None
            for h in range(_NHEADS1):
                for qt in range(NTT):
                    pend = attn_l1(h, qt, negB1_v, kb_tot, lam_v, pend, ocell)
            for st_ in pend:
                st_()
            ocv = ocell[0]
            if phase_end("attn1"):
                break

            x_last = out_proj("wo1", ("dve", ocv), L1Q)
            finish()
            break

        def run_stream(eng_name):
            def runner(e):
                for idx_it, item in enumerate(b.stream[eng_name]):
                    kind = item[0]
                    try:
                        if kind == "op":
                            _, fn, inc_eng = item
                            inst = fn(e)
                            if inc_eng is not None:
                                inst.then_inc(sems["c_" + inc_eng], 1)
                        elif kind == "wait":
                            _, sem, val = item
                            key = sems[sem] if sem in sems else sems["c_" + sem]
                            e.wait_ge(key, val)
                        else:
                            _, fn, sem = item
                            inst = fn(e)
                            inst.then_inc(sems[sem], 16)
                    except Exception:
                        import traceback
                        print(f"REPLAY FAIL {eng_name}[{idx_it}] kind={kind}")
                        traceback.print_exc()
                        raise
            return runner

        with nc.allow_low_precision(reason="bf16 elementwise on bf16 matmul operands"), \
             nc.Block() as block:
            block.tensor(run_stream("pe"))
            block.scalar(run_stream("act"))
            block.vector(run_stream("dve"))
            block.sync(run_stream("sp"))
            block.gpsimd(run_stream("gp"))

    nc.finalize()
    return nc



def _img(A):
    K, F = A.shape
    kc, mb = K // 128, F // 128
    return np.ascontiguousarray(
        A.reshape(kc, 128, mb, 128).transpose(2, 1, 0, 3)).reshape(mb, 128, kc * 128)


def _rope_tables():
    j = np.arange(16, dtype=np.float64)
    invf = 10000.0 ** (-j / 16.0)
    t = np.arange(T, dtype=np.float64)
    ang = np.outer(invf, t)
    rows_j = (np.arange(128) % 32) // 2
    C = np.cos(ang)[rows_j]
    S = np.sin(ang)[rows_j]
    sign = np.where(np.arange(128) % 2 == 0, -1.0, 1.0)[:, None]
    return np.concatenate([C, S * sign], axis=1).astype(ml_dtypes.bfloat16)


def prep_inputs(L0_attn_x, L0_attn_w_attn, L0_attn_w_proj, L0_ffn_kernel,
                L0_ffn_bias, L0_ffn_a, L0_ffn_b, L1_attn_Wq, L1_attn_Wkv,
                L1_attn_Wout, L1_attn_lambda_init, L1_attn_rms_weight,
                L1_ffn_Wp1, L1_ffn_bp1, L1_ffn_Wp2, L1_ffn_bp2, L1_ffn_W1,
                L1_ffn_b1, L1_ffn_W2, L1_ffn_b2, L1_ffn_Wo, L1_ffn_bo):
    f = np.asarray
    x = f(L0_attn_x, dtype=np.float32)
    Aqkv = np.ascontiguousarray(f(L0_attn_w_attn, dtype=np.float32).T)
    w = {
        "wq0": _img(Aqkv[:, 0:1024]),
        "wk0": _img(Aqkv[:, 1024:2048]),
        "wv0": _img(Aqkv[:, 2048:3072]),
        "wp0": _img(np.ascontiguousarray(f(L0_attn_w_proj, dtype=np.float32).T)),
        "ker0": _img(f(L0_ffn_kernel, dtype=np.float32)),
        "b0": np.ascontiguousarray(f(L0_ffn_b, dtype=np.float32)),
        "bias0": np.ascontiguousarray(
            f(L0_ffn_bias, dtype=np.float32).reshape(8, 128).T),
        "wq1": _img(np.ascontiguousarray(f(L1_attn_Wq, dtype=np.float32).T)),
        "wo1": _img(np.ascontiguousarray(f(L1_attn_Wout, dtype=np.float32).T)),
        "lam": f(L1_attn_lambda_init, dtype=np.float32).reshape(1, 16),
        "rmsw": f(L1_attn_rms_weight, dtype=np.float32).reshape(64, 1),
        "wp1": _img(f(L1_ffn_Wp1, dtype=np.float32))[0],
        "wp2": f(L1_ffn_Wp2, dtype=np.float32),
        "bp1": f(L1_ffn_bp1, dtype=np.float32).reshape(128, 1),
        "bp2": f(L1_ffn_bp2, dtype=np.float32).reshape(1, 8),
        "b1e": np.stack([np.ascontiguousarray(
            f(L1_ffn_b1, dtype=np.float32)[e].reshape(16, 128).T).reshape(-1)
            for e in range(8)]),
        "b2e": np.stack([np.ascontiguousarray(
            f(L1_ffn_b2, dtype=np.float32)[e].reshape(8, 128).T).reshape(-1)
            for e in range(8)]),
        "wof": _img(f(L1_ffn_Wo, dtype=np.float32)),
        "bof": np.ascontiguousarray(f(L1_ffn_bo, dtype=np.float32).reshape(8, 128).T),
    }
    a = f(L0_ffn_a, dtype=np.float32)
    w["a0"] = np.ascontiguousarray(
        a.reshape(8, 128, 16).transpose(1, 0, 2)).reshape(128, 128)
    Akv = np.ascontiguousarray(f(L1_attn_Wkv, dtype=np.float32).T)
    w["wk1"] = _img(Akv[:, 0:256])
    w["wv1"] = _img(Akv[:, 256:512])
    W1 = f(L1_ffn_W1, dtype=np.float32)
    w["w1e"] = np.stack([_img(W1[e]).reshape(-1) for e in range(8)])
    W2 = f(L1_ffn_W2, dtype=np.float32)
    w["w2e"] = np.stack([_img(W2[e]).reshape(-1) for e in range(8)])
    w["cs"] = _rope_tables()
    kk, qq = np.meshgrid(np.arange(128), np.arange(128), indexing="ij")
    w["mask"] = (qq >= kk).astype(ml_dtypes.bfloat16)
    w["ident"] = np.eye(128, dtype=ml_dtypes.bfloat16)
    w["identf"] = np.eye(128, dtype=np.float32)
    w["grp2"] = (np.arange(128)[:, None] // 64 ==
                 np.arange(2)[None, :]).astype(ml_dtypes.bfloat16)
    w["grp4"] = (np.arange(128)[:, None] // 32 ==
                 np.arange(4)[None, :]).astype(ml_dtypes.bfloat16)
    w["io99"] = (np.arange(8, dtype=np.float32) - 99.0).reshape(1, 8)
    in_maps = []
    for core in range(8):
        m = {"xT": np.ascontiguousarray(x[core].T)}
        m.update(w)
        in_maps.append(m)
    return in_maps


def kernel(**inputs):
    in_maps = prep_inputs(**inputs)
    nc = build_nc(upto=_UPTO, debug=_DEBUG)
    res = run_bass_kernel_spmd(nc, in_maps, core_ids=list(range(8)))
    out = np.stack([np.ascontiguousarray(r["yT"].T) for r in res.results])
    return out.astype(np.float32)


# version 24
# speedup vs baseline: 1.0221x; 1.0043x over previous
import contextlib

import numpy as np
import ml_dtypes

import concourse.bass as bass
import concourse.mybir as mybir
from concourse.bass_utils import run_bass_kernel_spmd

F32 = mybir.dt.float32
BF16 = mybir.dt.bfloat16
I32 = mybir.dt.int32
AF = mybir.ActivationFunctionType
ALU = mybir.AluOpType
AX = mybir.AxisListType

D = 1024
T = 2048
NCH = 8
NTT = 4
TT = 512
MAGIC = 12582912.0

_UPTO = "all"
_DEBUG = False
_NHEADS = 16
_NHEADS1 = 16

ENGS = ("pe", "act", "dve", "sp", "gp")


class Builder:
    def __init__(self):
        self.stream = {e: [] for e in ENGS}
        self.ctr = {e: 0 for e in ENGS}
        self.dmatot = {}
        self.waited = {}

    def op(self, eng, fn, inc=False):
        if inc:
            self.ctr[eng] += 1
            self.stream[eng].append(("op", fn, eng))
            return self.ctr[eng]
        self.stream[eng].append(("op", fn, None))
        return None

    def wait(self, eng, sem, val):
        if val is None or val <= 0:
            return
        key = (eng, sem)
        if self.waited.get(key, 0) >= val:
            return
        self.waited[key] = val
        self.stream[eng].append(("wait", sem, val))

    def dma(self, eng, sem, fn):
        self.dmatot[sem] = self.dmatot.get(sem, 0) + 16
        self.stream[eng].append(("dma", fn, sem))
        return self.dmatot[sem]


class BankPool:
    def __init__(self, b, tensors):
        self.b = b
        self.tensors = tensors
        self.free_q = list(range(len(tensors)))
        self.cons = {i: None for i in range(len(tensors))}

    def acquire(self):
        i = self.free_q.pop(0)
        c = self.cons[i]
        if c is not None:
            self.b.wait("pe", c[0], c[1])
        return self.tensors[i], i

    def acquire_pair(self):
        order = [i // 2 for i in self.free_q]
        cand = [j for k, j in enumerate(order) if j in order[k + 1:]]
        for j in cand + list(range(len(self.tensors) // 2)):
            if 2 * j in self.free_q and 2 * j + 1 in self.free_q:
                self.free_q.remove(2 * j)
                self.free_q.remove(2 * j + 1)
                for i in (2 * j, 2 * j + 1):
                    c = self.cons[i]
                    if c is not None:
                        self.b.wait("pe", c[0], c[1])
                return j
        raise RuntimeError("no aligned psum pair free")

    def acquire_deferred(self):
        i = self.free_q.pop(0)
        return self.tensors[i], i, self.cons[i]

    def acquire_pair_deferred(self):
        order = [i // 2 for i in self.free_q]
        cand = [j for k, j in enumerate(order) if j in order[k + 1:]]
        for j in cand + list(range(len(self.tensors) // 2)):
            if 2 * j in self.free_q and 2 * j + 1 in self.free_q:
                self.free_q.remove(2 * j)
                self.free_q.remove(2 * j + 1)
                return j, [self.cons[2 * j], self.cons[2 * j + 1]]
        raise RuntimeError("no aligned psum pair free")

    def release(self, i, eng, val):
        self.cons[i] = (eng, val)
        self.free_q.append(i)


def build_nc(upto="all", debug=False):
    _NHEADS = globals()["_NHEADS"]
    _NHEADS1 = globals()["_NHEADS1"]
    nc = bass.Bass("TRN2", target_bir_lowering=False)
    dr = {}

    def din(name, shape, dt=F32):
        dr[name] = nc.dram_tensor(name, list(shape), dt, kind="ExternalInput").ap()

    din("xT", (D, T))
    din("wq0", (8, 128, 1024)); din("wk0", (8, 128, 1024)); din("wv0", (8, 128, 1024))
    din("wp0", (8, 128, 1024))
    din("ker0", (8, 128, 1024))
    din("a0", (128, 128)); din("b0", (16, 1024)); din("bias0", (128, 8))
    din("wq1", (8, 128, 1024)); din("wk1", (2, 128, 1024)); din("wv1", (2, 128, 1024))
    din("wo1", (8, 128, 1024))
    din("lam", (1, 16)); din("rmsw", (64, 1))
    din("wp1", (128, 1024)); din("wp2", (128, 8)); din("bp1", (128, 1)); din("bp2", (1, 8))
    din("w1e", (8, 16 * 128 * 1024)); din("w2e", (8, 8 * 128 * 2048))
    din("b1e", (8, 2048)); din("b2e", (8, 1024))
    din("wof", (8, 128, 1024)); din("bof", (128, 8))
    din("cs", (128, 2 * T), BF16)
    din("mask", (128, 128), BF16)
    din("maskn", (128, 128), BF16)
    din("ident", (128, 128), BF16)
    din("identf", (128, 128))
    din("grp2", (128, 2), BF16); din("grp4", (128, 4), BF16)
    din("io99", (1, 8))

    yT = nc.dram_tensor("yT", [D, T], F32, kind="ExternalOutput").ap()
    dbg_xn = dbg_big = dbg_stat = None
    if debug:
        dbg_xn = nc.dram_tensor("dbg_xn", [128, NCH * T], BF16, kind="ExternalOutput").ap()
        dbg_big = nc.dram_tensor("dbg_big", [128, 49408], BF16, kind="ExternalOutput").ap()
        dbg_stat = nc.dram_tensor("dbg_stat", [128, 832], F32, kind="ExternalOutput").ap()

    b = Builder()

    with contextlib.ExitStack() as ctx:
        def sbuf(name, shape, dt):
            return ctx.enter_context(nc.sbuf_tensor(name, list(shape), dt))

        X = sbuf("X", (128, NCH * T), F32)
        XN = sbuf("XN", (128, NCH * T), BF16)
        BIG = sbuf("BIGB", (128, 49408), BF16)
        WB = sbuf("WB", (128, 2048), BF16)
        SCR = sbuf("SCR", (128, 2048), BF16)
        SCRF = sbuf("SCRF", (128, 512), F32)
        STAT = sbuf("STAT", (128, 832), F32)
        SMALL = sbuf("SMALL", (128, 72), BF16)
        MASKT = sbuf("MASKT", (128, 128), BF16)
        MASKN = sbuf("MASKN", (128, 128), BF16)
        IDT = sbuf("IDT", (128, 128), BF16)
        IDF = sbuf("IDF", (128, 128), F32)
        IDXI = sbuf("IDXI", (1, 1), I32)
        ps_big = [ctx.enter_context(nc.psum_tensor(f"psb{i}", [128, 1024], F32))
                  for i in range(4)]
        ps = []
        for i in range(4):
            ps.append(ps_big[i][0:128, 0:512])
            ps.append(ps_big[i][0:128, 512:1024])

        sems = {}
        for e in ENGS:
            sems["c_" + e] = ctx.enter_context(nc.semaphore("c_" + e))
        for d in ("d_pre", "d_x", "d_wa", "d_wb", "d_qs", "d_cs", "d_kb",
                  "d_sm", "d_y", "d_mx", "d_dbg"):
            sems[d] = ctx.enter_context(nc.semaphore(d))

        HN_Q, HN_QT, HN_K, HN_KT = 0, 8, 16, 24
        ONES_F = 32
        RSTD = 160
        _statc = [672]

        def stat_col(n=1):
            c = _statc[0]; _statc[0] += n
            assert _statc[0] <= 744
            return c

        col_negB = [stat_col(), stat_col()]
        col_isc = stat_col(); col_sc = stat_col()
        col_bias0 = stat_col(8)
        col_bo = stat_col(8); col_bo3 = stat_col(8)
        col_b1s = stat_col(16); col_b2s = stat_col(8)
        col_bp1 = stat_col(); col_rmsw = stat_col()
        col_g = stat_col(); col_wp2 = stat_col(8)
        col_mx = stat_col(); col_mxt = stat_col()
        col_eps5 = stat_col(); col_eps6 = stat_col()
        _p0c = [744]

        def p0_col(n=1):
            c = _p0c[0]; _p0c[0] += n
            assert _p0c[0] <= 832
            return c

        p_io99 = p0_col(8); p_bp2 = p0_col(8); p_log = p0_col(8)
        p_eq = p0_col(8); p_cand = p0_col(8)
        p_m = p0_col(); p_idx = p0_col(); p_prod = p0_col(); p_t1 = p0_col()

        SM_G2, SM_G4, ONES_B = 0, 2, 6

        L0Q, L0K, L0VT = 0, 16384, 32768
        L1Q, L1KA, L1KB, L1VT, L1CS = 0, 16384, 20480, 24576, 28736
        L1H, L1RS = 0, 32768
        PT = (0, 512)
        SC_Y, SC_U = 1024, 1536

        pool = BankPool(b, ps)

        def MM(out_ap, lhsT, rhs, start, stop, inc=False, tp=None):
            def fn(e, o=out_ap, l=lhsT, r=rhs, st=start, sp=stop, t=tp):
                return e.matmul(o, l, r, start=st, stop=sp, tile_position=t)
            return b.op("pe", fn, inc=inc)

        def ACT(out_ap, in_ap, func, bias=0.0, scale=1.0, inc=True):
            def fn(e, o=out_ap, i=in_ap, f=func, bi=bias, sc=scale):
                return e.activation(o, i, f, bias=bi, scale=sc)
            return b.op("act", fn, inc=inc)

        def DVE(meth, *args, inc=True, **kw):
            def fn(e, m=meth, a=args, k=kw):
                return getattr(e, m)(*a, **k)
            return b.op("dve", fn, inc=inc)

        def DMA(eng, sem, out_ap, in_ap, **kw):
            def fn(e, o=out_ap, i=in_ap, k=kw):
                return e.dma_start(out=o, in_=i, **k)
            return b.dma(eng, sem, fn)

        xloads = []
        for c in range(NCH):
            xloads.append(DMA("sp", "d_x", X[:, c * T:(c + 1) * T],
                              dr["xT"][c * 128:(c + 1) * 128, :]))
        pre = 0
        pre = DMA("sp", "d_pre", MASKT[:, :], dr["mask"][:, :])
        pre = DMA("sp", "d_pre", MASKN[:, :], dr["maskn"][:, :])
        pre = DMA("sp", "d_pre", IDT[:, :], dr["ident"][:, :])
        pre = DMA("sp", "d_pre", IDF[:, :], dr["identf"][:, :])
        pre = DMA("sp", "d_pre", SMALL[:, SM_G2:SM_G2 + 2], dr["grp2"][:, :])
        pre = DMA("sp", "d_pre", SMALL[:, SM_G4:SM_G4 + 4], dr["grp4"][:, :])
        pre = DMA("sp", "d_pre", STAT[0:1, p_io99:p_io99 + 8], dr["io99"][:, :])
        pre = DMA("sp", "d_pre", STAT[0:1, p_bp2:p_bp2 + 8], dr["bp2"][:, :])
        pre = DMA("sp", "d_pre", STAT[:, col_bias0:col_bias0 + 8], dr["bias0"][:, :])
        pre = DMA("sp", "d_pre", STAT[:, col_bo:col_bo + 8], dr["bof"][:, :])
        pre = DMA("sp", "d_pre", STAT[:, col_bp1:col_bp1 + 1], dr["bp1"][:, :])
        pre = DMA("sp", "d_pre", STAT[0:64, col_rmsw:col_rmsw + 1], dr["rmsw"][:, :])
        pre = DMA("sp", "d_pre", STAT[64:65, 176:192], dr["lam"][:, :])
        pre = DMA("sp", "d_pre", STAT[:, col_wp2:col_wp2 + 8], dr["wp2"][:, :])
        pre_tot = pre

        e5_v = DVE("memset", STAT[:, col_eps5:col_eps5 + 1], 1e-5, inc=True)
        e6_v = DVE("memset", STAT[:, col_eps6:col_eps6 + 1], 1e-6, inc=True)
        ov1 = DVE("memset", STAT[:, ONES_F:ONES_F + 128], 1.0, inc=True)
        b.wait("dve", "dve", ov1)
        ones_v = DVE("memset", SMALL[:, ONES_B:ONES_B + 64], 1.0, inc=True)

        def vt_ones(base, per_tt):
            ap3 = BIG[:, base:base + 16 * per_tt].rearrange(
                "p (t x) -> p t x", x=per_tt).rearrange(
                "p t (h o) -> p (t h) o", o=65)[:, :, 64:65]
            return DVE("memset", ap3, 1.0, inc=True)

        wslot = {"par": 0, "cons": {0: None, 1: None}, "sem": ("d_wa", "d_wb")}
        scrf_st = {"free": None}

        def slot_begin():
            par = wslot["par"]
            wslot["par"] ^= 1
            return par, wslot["cons"][par]

        def band_load(in_ap, cols=1024):
            par, c = slot_begin()
            if c is not None:
                b.wait("gp", c[0], c[1])
            tot = DMA("gp", wslot["sem"][par],
                      WB[:, par * 1024:par * 1024 + cols], in_ap)
            return par, wslot["sem"][par], tot

        def band_release(par, eng, val):
            wslot["cons"][par] = (eng, val)

        def layernorm(layer, x_ready, xb_pre=None):
            if xb_pre is not None:
                xb = xb_pre
            else:
                if x_ready is not None:
                    b.wait("act", x_ready[0], x_ready[1])
                xb = {}
                for c in range(NCH):
                    b.wait("act", "d_x", xloads[c])
                    xb[c] = ACT(XN[:, c * T:(c + 1) * T],
                                X[:, c * T:(c + 1) * T], AF.Copy)
            kept = None
            last_n = None
            prev_br = None
            tevs = []
            slot_user = {0: None, 1: None}
            for j in range(NTT):
                bk_sx, i_sx = pool.acquire()
                bk_s2, i_s2 = pool.acquire()
                sqv = {}
                vx = v2 = None
                for c in range(NCH):
                    sl = (c % 2) * 512
                    if slot_user[c % 2] is not None:
                        b.wait("act", "pe", slot_user[c % 2])
                    sqv[c] = ACT(SCR[:, sl:sl + 512],
                                 XN[:, c * T + j * TT: c * T + j * TT + TT],
                                 AF.Square)
                    b.wait("pe", "act", xb[c])
                    vx = MM(bk_sx[0:1, 0:TT], SMALL[0:128, ONES_B:ONES_B + 1],
                            XN[:, c * T + j * TT: c * T + j * TT + TT],
                            start=(c == 0), stop=(c == NCH - 1), inc=(c == NCH - 1))
                    b.wait("pe", "act", sqv[c])
                    v2 = MM(bk_s2[0:1, 0:TT], SMALL[0:128, ONES_B:ONES_B + 1],
                            SCR[:, sl:sl + 512],
                            start=(c == 0), stop=(c == NCH - 1), inc=True)
                    slot_user[c % 2] = v2
                b.wait("dve", "pe", vx)
                if scrf_st["free"] is not None:
                    b.wait("dve", scrf_st["free"][0], scrf_st["free"][1])
                m_v = DVE("tensor_scalar_mul", SCRF[0:1, 0:TT], bk_sx[0:1, 0:TT],
                          1.0 / 1024.0)
                pool.release(i_sx, "dve", m_v)
                b.wait("dve", "dve", m_v)
                b.wait("dve", "pe", prev_br)
                t_v = DVE("tensor_mul", STAT[0:1, RSTD:RSTD + TT],
                          SCRF[0:1, 0:TT], SCRF[0:1, 0:TT])
                b.wait("dve", "pe", v2)
                b.wait("dve", "dve", t_v)
                var_v = DVE("scalar_tensor_tensor", bk_s2[0:1, 0:TT],
                            bk_s2[0:1, 0:TT], 1.0 / 1024.0,
                            STAT[0:1, RSTD:RSTD + TT], ALU.mult, ALU.subtract)
                b.wait("act", "dve", var_v)
                b.wait("act", "dve", e5_v)
                ln_v = ACT(STAT[0:1, RSTD:RSTD + TT], bk_s2[0:1, 0:TT],
                           AF.Ln, bias=STAT[0:1, col_eps5:col_eps5 + 1],
                           scale=1.0)
                pool.release(i_s2, "act", ln_v)
                b.wait("act", "act", ln_v)
                ex_v = ACT(STAT[0:1, RSTD:RSTD + TT], STAT[0:1, RSTD:RSTD + TT],
                           AF.Exp, scale=-0.5)
                bk_m, i_m = pool.acquire()
                bk_r, i_r = pool.acquire()
                b.wait("pe", "dve", ov1)
                b.wait("pe", "dve", m_v)
                bm = MM(bk_m[0:128, 0:TT], STAT[0:1, ONES_F:ONES_F + 128],
                        SCRF[0:1, 0:TT], start=True, stop=True, inc=True)
                b.wait("pe", "act", ex_v)
                br = MM(bk_r[0:128, 0:TT], STAT[0:1, ONES_F:ONES_F + 128],
                        STAT[0:1, RSTD:RSTD + TT], start=True, stop=True, inc=True)
                prev_br = br
                for c in range(NCH):
                    b.wait("dve", "pe", bm)
                    tv = DVE("tensor_sub", SCRF[:, 0:TT],
                             X[:, c * T + j * TT: c * T + j * TT + TT],
                             bk_m[0:128, 0:TT])
                    b.wait("dve", "pe", br)
                    b.wait("dve", "dve", tv)
                    last_n = DVE("tensor_mul",
                                 XN[:, c * T + j * TT: c * T + j * TT + TT],
                                 SCRF[:, 0:TT], bk_r[0:128, 0:TT])
                if layer == 1 and j == NTT - 1:
                    kept = (bk_m, i_m, bk_r, i_r)
                else:
                    pool.release(i_m, "dve", last_n)
                    pool.release(i_r, "dve", last_n)
                scrf_st["free"] = ("dve", last_n)
                tevs.append(("dve", last_n))
            return last_n, kept, tevs

        def evac_to(base):
            def dest(mb, j, bk):
                ev = ACT(BIG[:, base + mb * T + j * TT: base + mb * T + j * TT + TT],
                         bk[0:128, 0:TT], AF.Copy)
                return ("act", ev)
            return dest

        def projection(img, mbs, dest_fn, nkc=NCH, rhs=None, rhs_base=0,
                       rhs_ready=None, rhs_tevs=None):
            src = rhs if rhs is not None else XN
            last_dest = None
            for mb in mbs:
                par, sem, tot = band_load(dr[img][mb:mb + 1, :, :nkc * 128],
                                          cols=nkc * 128)
                b.wait("pe", sem, tot)
                if rhs_ready is not None:
                    b.wait("pe", rhs_ready[0], rhs_ready[1])
                banks = [pool.acquire() for _ in range(NTT)]
                lastv = None
                if rhs_tevs is not None:
                    for j in range(NTT):
                        b.wait("pe", rhs_tevs[j][0], rhs_tevs[j][1])
                        for c in range(nkc):
                            lastv = MM(
                                banks[j][0][0:128, 0:TT],
                                WB[:, par * 1024 + c * 128:
                                   par * 1024 + c * 128 + 128],
                                src[:, rhs_base + c * T + j * TT:
                                    rhs_base + c * T + j * TT + TT],
                                start=(c == 0), stop=(c == nkc - 1),
                                inc=(c == nkc - 1 and j == NTT - 1))
                else:
                    for c in range(nkc):
                        for j in range(NTT):
                            lastv = MM(
                                banks[j][0][0:128, 0:TT],
                                WB[:, par * 1024 + c * 128:
                                   par * 1024 + c * 128 + 128],
                                src[:, rhs_base + c * T + j * TT:
                                    rhs_base + c * T + j * TT + TT],
                                start=(c == 0), stop=(c == nkc - 1),
                                inc=(c == nkc - 1 and j == NTT - 1))
                band_release(par, "pe", lastv)
                for j in range(NTT):
                    bk, ib = banks[j]
                    b.wait("act", "pe", lastv)
                    b.wait("dve", "pe", lastv)
                    eng, val = dest_fn(mb, j, bk)
                    pool.release(ib, eng, val)
                    last_dest = (eng, val)
            return last_dest

        def v_project(img, mbs, vt_base, heads_per_mb, per_tt):
            for mi, mb in enumerate(mbs):
                par, sem, tot = band_load(dr[img][mb:mb + 1, :, :], cols=1024)
                b.wait("pe", sem, tot)
                banks = [pool.acquire() for _ in range(NTT)]
                lastv = None
                for c in range(NCH):
                    for j in range(NTT):
                        lastv = MM(
                            banks[j][0][0:128, 0:TT],
                            WB[:, par * 1024 + c * 128: par * 1024 + c * 128 + 128],
                            XN[:, c * T + j * TT: c * T + j * TT + TT],
                            start=(c == 0), stop=(c == NCH - 1),
                            inc=(c == NCH - 1 and j == NTT - 1))
                band_release(par, "pe", lastv)
                tp_prev = None
                for wave in range(2):
                    evs = {}
                    for jj in range(2):
                        j = wave * 2 + jj
                        bk, ib = banks[j]
                        b.wait("act", "pe", lastv)
                        b.wait("act", "pe", tp_prev)
                        evs[jj] = ACT(SCR[:, jj * TT:(jj + 1) * TT],
                                      bk[0:128, 0:TT], AF.Copy)
                        pool.release(ib, "act", evs[jj])
                    for ttw in range(8):
                        tt = wave * 8 + ttw
                        bk, ib = pool.acquire()
                        b.wait("pe", "act", evs[ttw // 4])
                        b.wait("pe", "d_pre", pre_tot)
                        tp_out = bk[0:128, 0:64].bitcast(BF16)

                        def tpfn(e, o=tp_out,
                                 i=SCR[:, ttw * 128:ttw * 128 + 128],
                                 idn=IDT[:, :]):
                            return e.transpose(o, i, idn)
                        tv = b.op("pe", tpfn, inc=True)
                        tp_prev = tv
                        base = vt_base + tt * per_tt + mi * heads_per_mb * 65
                        o_ap = BIG[:, base:base + heads_per_mb * 65].rearrange(
                            "p (h o) -> p h o", o=65)[:, :, 0:64]
                        i_ap = tp_out.rearrange("p (h o) -> p h o", o=64)[
                            :, 0:heads_per_mb, :]
                        b.wait("act", "pe", tv)
                        ev2 = ACT(o_ap, i_ap, AF.Copy)
                        pool.release(ib, "act", ev2)
            return

        def out_proj(img, o_ready, o_base, make_xn=False):
            lastadd = None
            xb = {}
            for mb in range(8):
                par, sem, tot = band_load(dr[img][mb:mb + 1, :, :], cols=1024)
                b.wait("pe", sem, tot)
                b.wait("pe", o_ready[0], o_ready[1])
                banks = [pool.acquire() for _ in range(NTT)]
                lastv = None
                for c in range(NCH):
                    for j in range(NTT):
                        lastv = MM(banks[j][0][0:128, 0:TT],
                                   WB[:, par * 1024 + c * 128: par * 1024 + c * 128 + 128],
                                   BIG[:, o_base + c * T + j * TT:
                                       o_base + c * T + j * TT + TT],
                                   start=(c == 0), stop=(c == NCH - 1),
                                   inc=(c == NCH - 1 and j == NTT - 1))
                band_release(par, "pe", lastv)
                for j in range(NTT):
                    bk, ib = banks[j]
                    b.wait("dve", "pe", lastv)
                    av = DVE("tensor_add", X[:, mb * T + j * TT: mb * T + j * TT + TT],
                             X[:, mb * T + j * TT: mb * T + j * TT + TT],
                             bk[0:128, 0:TT])
                    pool.release(ib, "dve", av)
                    lastadd = av
                if make_xn:
                    b.wait("act", "dve", lastadd)
                    xb[mb] = ACT(XN[:, mb * T:(mb + 1) * T],
                                 X[:, mb * T:(mb + 1) * T], AF.Copy)
            if make_xn:
                return lastadd, xb
            return lastadd

        def bscan(base_col, nch, acc_col, ready, init=True):
            if init:
                DVE("memset", STAT[:, acc_col:acc_col + 1], 0.0)
            last = None
            for c in range(nch):
                for j in range(NTT):
                    if ready is not None:
                        b.wait("dve", ready[0], ready[1])
                    rv = DVE("tensor_reduce", STAT[:, HN_QT:HN_QT + 1],
                             BIG[:, base_col + c * T + j * TT:
                                 base_col + c * T + j * TT + TT],
                             AX.X, ALU.max, apply_absolute_value=True)
                    b.wait("dve", "dve", rv)
                    last = DVE("tensor_max", STAT[:, acc_col:acc_col + 1],
                               STAT[:, acc_col:acc_col + 1],
                               STAT[:, HN_QT:HN_QT + 1])
            return last

        def bfinal(layer, scale, lq, lk):
            bq_, ibq = pool.acquire()
            bk_, ibk = pool.acquire()
            b.wait("pe", "dve", max(lq, lk))
            b.wait("pe", "d_pre", pre_tot)
            tq = MM(bq_[0:1, 0:128], STAT[:, HN_Q:HN_Q + 1], IDF[:, :],
                    start=True, stop=True, inc=True)
            tk = MM(bk_[0:1, 0:128], STAT[:, HN_K:HN_K + 1], IDF[:, :],
                    start=True, stop=True, inc=True)
            b.wait("dve", "pe", tk)
            rq = DVE("tensor_reduce", STAT[0:1, HN_Q:HN_Q + 1],
                     bq_[0:1, 0:128], AX.X, ALU.max)
            rk = DVE("tensor_reduce", STAT[0:1, HN_K:HN_K + 1],
                     bk_[0:1, 0:128], AX.X, ALU.max)
            pool.release(ibq, "dve", rq)
            pool.release(ibk, "dve", rk)
            b.wait("dve", "dve", rk)
            pv = DVE("tensor_mul", STAT[0:1, p_prod:p_prod + 1],
                     STAT[0:1, HN_Q:HN_Q + 1], STAT[0:1, HN_K:HN_K + 1])
            b.wait("dve", "dve", pv)
            nv = DVE("tensor_scalar_mul", STAT[0:1, p_t1:p_t1 + 1],
                     STAT[0:1, p_prod:p_prod + 1], -1.02 * scale)
            bk, ib = pool.acquire()
            b.wait("pe", "dve", nv)
            bv = MM(bk[0:128, 0:1], STAT[0:1, ONES_F:ONES_F + 128],
                    STAT[0:1, p_t1:p_t1 + 1], start=True, stop=True, inc=True)
            b.wait("dve", "pe", bv)
            cv = DVE("tensor_copy", STAT[:, col_negB[layer]:col_negB[layer] + 1],
                     bk[0:128, 0:1])
            pool.release(ib, "dve", cv)
            return cv

        PT1_BASE = 33344
        pt_state = {"i": 0, "cons": [], "slots": []}
        fin1_cell = [None]
        oev1_cell = [None]
        comb1_cell = [None]
        ffnq = []

        def pt_setup(slots, init_cons=None):
            pt_state["i"] = 0
            pt_state["slots"] = slots
            pt_state["cons"] = [init_cons] * len(slots)

        def pt_begin():
            i = pt_state["i"]
            pt_state["i"] = (i + 1) % len(pt_state["slots"])
            c = pt_state["cons"][i]
            if c is not None:
                b.wait("act", c[0], c[1])
            return i

        def pt_ap(i, N):
            tens, base = pt_state["slots"][i]
            return tens[:, base:base + N]

        def attn_l0(h, qt, negB_v, pend, ocell, allow_ffn=True):
            bh = 64 * (h % 2)
            ch = h // 2
            bkO, iO, bkO_cons = pool.acquire_deferred()
            bkO_wait = [bkO_cons]
            stb = []
            for _ in range(2):
                t_, i_, c_ = pool.acquire_deferred()
                stb.append([t_, i_, None, c_])
            n = 4 * qt + 4
            stage = []
            pv_last = [None]
            pend = list(pend) if pend else []

            def drain(k):
                while k > 0:
                    if pend:
                        pend.pop(0)()
                    elif ffnq:
                        ffnq.pop(0)()
                    else:
                        return
                    k -= 1

            def emit_A(idx):
                i = idx
                r = i - 4 * qt
                col0 = 128 * r if r >= 0 else 0
                N = 512 - col0
                bkS, iS, prev_ev, first_c = stb[idx % 2]
                if first_c is not None:
                    b.wait("pe", first_c[0], first_c[1])
                    stb[idx % 2][3] = None
                if prev_ev is not None:
                    b.wait("pe", "act", prev_ev)
                pe_mask = (r >= 0 and qt == 0)
                sv = MM(bkS[0:128, 0:N],
                        BIG[bh:bh + 64,
                            L0K + ch * T + i * 128: L0K + ch * T + i * 128 + 128],
                        BIG[bh:bh + 64,
                            L0Q + ch * T + qt * TT + col0: L0Q + ch * T + qt * TT + 512],
                        start=True, stop=not pe_mask, inc=not pe_mask)
                if pe_mask:
                    sv = MM(bkS[0:128, 0:128], MASKN[:, :], IDT[:, :],
                            start=False, stop=True, inc=True)
                par = pt_begin()
                b.wait("act", "pe", sv)
                b.wait("act", "dve", negB_v)
                ev = ACT(pt_ap(par, N), bkS[0:128, 0:N], AF.Exp,
                         bias=STAT[:, col_negB[0]:col_negB[0] + 1], scale=0.125)
                stb[idx % 2][2] = ev
                if r >= 0 and not pe_mask:
                    b.wait("dve", "act", ev)
                    mv = DVE("tensor_mul", pt_ap(par, 128),
                             pt_ap(par, 128), MASKT[:, :])
                    pw = ("dve", mv)
                else:
                    pw = ("act", ev)
                stage.append((idx, par, pw, col0, N))

            def emit_B():
                idx, par, pw, col0, N = stage.pop(0)
                if bkO_wait[0] is not None:
                    b.wait("pe", bkO_wait[0][0], bkO_wait[0][1])
                    bkO_wait[0] = None
                b.wait("pe", pw[0], pw[1])
                padm = min(128, 16640 - (idx * 1040 + h * 65))
                pv = MM(bkO[0:padm, col0:512],
                        BIG[:, L0VT + idx * 1040 + h * 65:
                            L0VT + idx * 1040 + h * 65 + padm],
                        pt_ap(par, N),
                        start=(idx == 0), stop=(idx == n - 1), inc=True)
                pt_state["cons"][par] = ("pe", pv)
                pv_last[0] = pv

            LAG = 3
            for idx in range(n):
                emit_A(idx)
                if idx >= 3 and pend:
                    pend.pop(0)()
                elif allow_ffn and ffnq:
                    ffnq.pop(0)()
                if idx >= LAG:
                    emit_B()
            while stage:
                emit_B()
            drain(len(pend))
            for bkS, iS, prev_ev, _fc in stb:
                pool.release(iS, "act", prev_ev)

            env = {}

            b.wait("dve", "pe", pv_last[0])
            env["rv"] = DVE("reciprocal", SCR[64:65, SC_Y:SC_Y + 512],
                            bkO[64:65, 0:512])

            def s_bcast():
                bkB, iB = pool.acquire()
                env["bkB"], env["iB"] = bkB, iB
                b.wait("pe", "dve", env["rv"])
                env["bv"] = MM(bkB[0:64, 0:512],
                               SMALL[64:65, ONES_B:ONES_B + 64],
                               SCR[64:65, SC_Y:SC_Y + 512],
                               start=True, stop=True, inc=True)

            def s_bev():
                b.wait("act", "pe", env["bv"])
                bev = ACT(SCR[0:64, SC_U:SC_U + 512],
                          env["bkB"][0:64, 0:512], AF.Copy)
                env["bev"] = bev
                pool.release(env["iB"], "act", bev)

            def s_final():
                ocol = L0Q + ch * T + qt * TT
                b.wait("dve", "act", env["bev"])
                if h % 2 == 0:
                    cv = DVE("tensor_mul", BIG[0:64, ocol:ocol + TT],
                             bkO[0:64, 0:512], SCR[0:64, SC_U:SC_U + 512])
                    pool.release(iO, "dve", cv)
                else:
                    yv = DVE("tensor_mul", SCR[0:64, SC_U:SC_U + 512],
                             bkO[0:64, 0:512], SCR[0:64, SC_U:SC_U + 512])
                    pool.release(iO, "dve", yv)
                    bkR, iR = pool.acquire()
                    b.wait("pe", "dve", yv)
                    rrv = MM(bkR[64:128, 0:512], IDT[0:64, 0:64],
                             SCR[0:64, SC_U:SC_U + 512], start=True, stop=True,
                             inc=True, tp=(0, 64))
                    b.wait("dve", "pe", rrv)
                    cv = DVE("tensor_copy", BIG[64:128, ocol:ocol + TT],
                             bkR[64:128, 0:512])
                    pool.release(iR, "dve", cv)
                ocell[0] = cv

            return [s_bcast, s_bev, s_final]

        def attn_l1(h, qt, negB_v, kb_tot, lam_v, pend, ocell):
            g = h // 4
            bh = 64 * (h % 2)
            ksrc = L1KA if 64 * (g % 2) == bh else L1KB
            kcol = ksrc + (g // 2) * T
            jO, jO_cons = pool.acquire_pair_deferred()
            bkO1, iO1 = ps[2 * jO], 2 * jO
            bkO2, iO2 = ps[2 * jO + 1], 2 * jO + 1
            jS_d = [pool.acquire_pair_deferred(), pool.acquire_pair_deferred()]
            jSs = [jS_d[0][0], jS_d[1][0]]
            st_wait = [jS_d[0][1], jS_d[1][1]]
            jO_wait = [jO_cons]
            st_ev = [None, None]
            n = 4 * qt + 4
            stage = []
            pv_last = [None]
            b.wait("pe", "d_kb", kb_tot)
            pend = list(pend) if pend else []

            def drain(k):
                while pend and k > 0:
                    pend.pop(0)()
                    k -= 1

            def emit_A(idx):
                i = idx
                r = i - 4 * qt
                col0 = 128 * r if r >= 0 else 0
                N = 512 - col0
                w = idx % 2
                jS = jSs[w]
                if st_wait[w] is not None:
                    for c_ in st_wait[w]:
                        if c_ is not None:
                            b.wait("pe", c_[0], c_[1])
                    st_wait[w] = None
                if st_ev[w] is not None:
                    b.wait("pe", "act", st_ev[w])
                svs = None
                pe_mask = (r >= 0 and qt == 0)
                for s in range(2):
                    rb = bh + 32 * s
                    svs = MM(ps[2 * jS + s][0:128, 0:N],
                             BIG[rb:rb + 32, kcol + i * 128: kcol + i * 128 + 128],
                             BIG[rb:rb + 32,
                                 L1Q + (h // 2) * T + qt * TT + col0:
                                 L1Q + (h // 2) * T + qt * TT + 512],
                             start=True, stop=not pe_mask,
                             inc=(not pe_mask and s == 1), tp=(rb, 0))
                    if pe_mask:
                        svs = MM(ps[2 * jS + s][0:128, 0:128],
                                 MASKN[:, :], IDT[:, :],
                                 start=False, stop=True, inc=(s == 1))
                par = pt_begin()
                ptt, ptb = pt_state["slots"][par]
                in_ap = ps_big[jS][0:128, 0:1024].rearrange(
                    "p (s x) -> p s x", s=2)[:, :, 0:N]
                out_ap = ptt[:, ptb:ptb + 1024].rearrange(
                    "p (s x) -> p s x", s=2)[:, :, 0:N]
                b.wait("act", "pe", svs)
                b.wait("act", "dve", negB_v)
                ev = ACT(out_ap, in_ap, AF.Exp,
                         bias=STAT[:, col_negB[1]:col_negB[1] + 1],
                         scale=0.17677669529663687)
                st_ev[w] = ev
                if r >= 0 and not pe_mask:
                    b.wait("dve", "act", ev)
                    for s2_ in range(2):
                        mo = ptb + 512 * s2_
                        mv = DVE("tensor_mul", ptt[:, mo:mo + 128],
                                 ptt[:, mo:mo + 128], MASKT[:, :])
                    pw = ("dve", mv)
                else:
                    pw = ("act", ev)
                stage.append((idx, par, pw, col0, N))

            def emit_B():
                idx, par, pw, col0, N = stage.pop(0)
                ptt, ptb = pt_state["slots"][par]
                pv = None
                if jO_wait[0] is not None:
                    for c_ in jO_wait[0]:
                        if c_ is not None:
                            b.wait("pe", c_[0], c_[1])
                    jO_wait[0] = None
                padm = min(128, 4160 + 2 * T - (idx * 260 + g * 65))
                for s in range(2):
                    b.wait("pe", pw[0], pw[1])
                    bkT = bkO1 if s == 0 else bkO2
                    pv = MM(bkT[0:padm, col0:512],
                            BIG[:, L1VT + idx * 260 + g * 65:
                                L1VT + idx * 260 + g * 65 + padm],
                            ptt[:, ptb + 512 * s: ptb + 512 * s + N],
                            start=(idx == 0), stop=(idx == n - 1), inc=(s == 1))
                pt_state["cons"][par] = ("pe", pv)
                pv_last[0] = pv

            LAG = 3
            for idx in range(n):
                emit_A(idx)
                if idx >= 2:
                    drain(1)
                if idx >= LAG:
                    emit_B()
            while stage:
                emit_B()
            while len(pend) > 4:
                pend.pop(0)()
            for w in range(2):
                for s in range(2):
                    pool.release(2 * jSs[w] + s, "act", st_ev[w])

            env = {}

            b.wait("act", "pe", pv_last[0])
            if comb1_cell[0] is not None:
                b.wait("act", "dve", comb1_cell[0])
            oe1 = ACT(SCR[0:65, 0:512], bkO1[0:65, 0:512], AF.Copy)
            oe2 = ACT(SCR[0:65, 512:1024], bkO2[0:65, 0:512], AF.Copy)
            env["oe"] = oe2
            pool.release(iO1, "act", oe2)
            pool.release(iO2, "act", oe2)
            b.wait("dve", "act", oe2)
            env["r1"] = DVE("reciprocal", SCR[64:65, SC_Y:SC_Y + 512],
                            SCR[64:65, 0:512])
            env["r2"] = DVE("reciprocal", SCR[64:65, SC_U:SC_U + 512],
                            SCR[64:65, 512:1024])
            b.wait("dve", "dve", env["r2"])
            b.wait("dve", "act", lam_v)
            env["r2l"] = DVE("tensor_scalar_mul",
                             SCR[64:65, SC_U:SC_U + 512],
                             SCR[64:65, SC_U:SC_U + 512],
                             STAT[64:65, 160 + h:161 + h])

            def s_bc1():
                jB = pool.acquire_pair()
                env["bkB1"], env["iB1"] = ps[2 * jB], 2 * jB
                env["bkB2"], env["iB2"] = ps[2 * jB + 1], 2 * jB + 1
                b.wait("pe", "dve", env["r1"])
                env["bv1"] = MM(env["bkB1"][0:64, 0:512],
                                SMALL[64:65, ONES_B:ONES_B + 64],
                                SCR[64:65, SC_Y:SC_Y + 512],
                                start=True, stop=True, inc=True)

            def s_bc2():
                b.wait("pe", "dve", env["r2l"])
                env["bv2"] = MM(env["bkB2"][0:64, 0:512],
                                SMALL[64:65, ONES_B:ONES_B + 64],
                                SCR[64:65, SC_U:SC_U + 512],
                                start=True, stop=True, inc=True)

            def s_t1():
                b.wait("dve", "pe", env["bv1"])
                b.wait("dve", "act", env["oe"])
                env["t1"] = DVE("tensor_mul", SCR[0:64, SC_Y:SC_Y + 512],
                                SCR[0:64, 0:512], env["bkB1"][0:64, 0:512])

            def s_u():
                b.wait("dve", "pe", env["bv2"])
                u = DVE("tensor_mul", SCR[0:64, SC_U:SC_U + 512],
                        SCR[0:64, 512:1024], env["bkB2"][0:64, 0:512])
                comb1_cell[0] = u
                b.wait("dve", "dve", u)
                yv = DVE("tensor_sub", SCR[0:64, SC_Y:SC_Y + 512],
                         SCR[0:64, SC_Y:SC_Y + 512],
                         SCR[0:64, SC_U:SC_U + 512])
                b.wait("dve", "dve", yv)
                env["sqv"] = DVE("tensor_mul", SCR[0:64, SC_U:SC_U + 512],
                                 SCR[0:64, SC_Y:SC_Y + 512],
                                 SCR[0:64, SC_Y:SC_Y + 512])
                env["uv"] = u

            def s_ss():
                b.wait("pe", "dve", env["sqv"])
                env["ssv"] = MM(env["bkB1"][0:1, 0:512],
                                SMALL[0:64, ONES_B:ONES_B + 1],
                                SCR[0:64, SC_U:SC_U + 512],
                                start=True, stop=True, inc=True)

            def s_ln():
                b.wait("act", "pe", env["ssv"])
                b.wait("act", "dve", e6_v)
                c = scrf_st["free"]
                if c is not None:
                    b.wait("act", c[0], c[1])
                lnv = ACT(SCRF[0:1, 0:512], env["bkB1"][0:1, 0:512], AF.Ln,
                          bias=STAT[0:1, col_eps6:col_eps6 + 1], scale=1.0 / 64.0)
                pool.release(env["iB1"], "act", lnv)
                env["lnv"] = lnv

            def s_exp():
                b.wait("act", "act", env["lnv"])
                env["xev"] = ACT(BIG[0:1, 32832:32832 + 512],
                                 SCRF[0:1, 0:512], AF.Exp, scale=-0.5)

            def s_rb():
                b.wait("pe", "dve", env["uv"])
                b.wait("pe", "act", env["xev"])
                env["rbv"] = MM(env["bkB2"][0:64, 0:512],
                                SMALL[0:1, ONES_B:ONES_B + 64],
                                BIG[0:1, 32832:32832 + 512],
                                start=True, stop=True, inc=True)

            def s_final():
                ocol = L1Q + (h // 2) * T + qt * TT
                b.wait("dve", "pe", env["rbv"])
                if h % 2 == 0:
                    cv = DVE("scalar_tensor_tensor", BIG[0:64, ocol:ocol + TT],
                             SCR[0:64, SC_Y:SC_Y + 512],
                             STAT[0:64, col_rmsw:col_rmsw + 1],
                             env["bkB2"][0:64, 0:512], ALU.mult, ALU.mult)
                    pool.release(env["iB2"], "dve", cv)
                else:
                    yv2 = DVE("scalar_tensor_tensor", SCR[0:64, SC_U:SC_U + 512],
                              SCR[0:64, SC_Y:SC_Y + 512],
                              STAT[0:64, col_rmsw:col_rmsw + 1],
                              env["bkB2"][0:64, 0:512], ALU.mult, ALU.mult)
                    pool.release(env["iB2"], "dve", yv2)
                    bkR, iR = pool.acquire()
                    b.wait("pe", "dve", yv2)
                    rrv = MM(bkR[64:128, 0:512], IDT[0:64, 0:64],
                             SCR[0:64, SC_U:SC_U + 512], start=True, stop=True,
                             inc=True, tp=(0, 64))
                    b.wait("dve", "pe", rrv)
                    cv = DVE("tensor_copy", BIG[64:128, ocol:ocol + TT],
                             bkR[64:128, 0:512])
                    pool.release(iR, "dve", cv)
                ocell[0] = cv
                fin1_cell[0] = cv

            return pend + [s_bc1, s_bc2, s_t1, s_u, s_ss,
                           s_ln, s_exp, s_rb, s_final]

        def finish():
            for e in ("pe", "act", "dve", "gp"):
                b.wait("sp", e, b.ctr[e])
            ytot = 0
            for c in range(NCH):
                ytot = DMA("sp", "d_y", yT[c * 128:(c + 1) * 128, :],
                           X[:, c * T:(c + 1) * T])
            if debug:
                dtot = DMA("sp", "d_dbg", dbg_xn[:, :], XN[:, :])
                dtot = DMA("sp", "d_dbg", dbg_big[:, :], BIG[:, :])
                dtot = DMA("sp", "d_dbg", dbg_stat[:, :], STAT[:, :])
                b.wait("sp", "d_dbg", dtot)
            b.wait("sp", "d_y", ytot)

        done = [False]

        def phase_end(name):
            if upto == name:
                finish()
                done[0] = True
            return done[0]

        while True:
            ln0_v, _, ln0_tevs = layernorm(0, None)
            if phase_end("ln0"):
                break

            vt0_ones = vt_ones(L0VT, 1040)
            qd = projection("wq0", list(range(8)), evac_to(L0Q),
                            rhs_tevs=ln0_tevs)
            lq0 = bscan(L0Q, NCH, HN_Q, qd)
            kd = projection("wk0", list(range(8)), evac_to(L0K))
            lk0 = bscan(L0K, NCH, HN_K, kd)
            b.wait("act", "dve", vt0_ones)
            v_project("wv0", list(range(8)), L0VT, 2, 1040)
            negB0_v = bfinal(0, 0.125, lq0, lk0)

            qst = {"mx_last": None}

            def quant_scan_step(hb):
                def f():
                    if hb == 0:
                        qst["mx_last"] = DVE("memset",
                                             STAT[:, col_mx:col_mx + 1], 0.0)
                    b.wait("sp", scrf_st["free"][0], scrf_st["free"][1])
                    qs_tot = DMA("sp", "d_qs", SCRF[:, 0:512],
                                 dr["ker0"][hb // 2, :,
                                            (hb % 2) * 512:(hb % 2) * 512 + 512])
                    b.wait("dve", "d_qs", qs_tot)
                    b.wait("dve", "dve", qst["mx_last"])
                    rv = DVE("tensor_reduce", STAT[:, col_mxt:col_mxt + 1],
                             SCRF[:, 0:512], AX.X, ALU.max,
                             apply_absolute_value=True)
                    b.wait("dve", "dve", rv)
                    qst["mx_last"] = DVE("tensor_max",
                                         STAT[:, col_mx:col_mx + 1],
                                         STAT[:, col_mx:col_mx + 1],
                                         STAT[:, col_mxt:col_mxt + 1])
                    scrf_st["free"] = ("dve", rv)
                return f

            for hb in range(16):
                ffnq.append(quant_scan_step(hb))
            qsc = {}

            def quant_tail():
                bmx, imx = pool.acquire()
                b.wait("pe", "dve", qst["mx_last"])
                b.wait("pe", "d_pre", pre_tot)
                tmx = MM(bmx[0:1, 0:128], STAT[:, col_mx:col_mx + 1], IDF[:, :],
                         start=True, stop=True, inc=True)
                b.wait("dve", "pe", tmx)
                mxr = DVE("tensor_reduce", STAT[0:1, p_t1:p_t1 + 1],
                          bmx[0:1, 0:128], AX.X, ALU.max)
                pool.release(imx, "dve", mxr)
                b.wait("act", "dve", mxr)
                sv2 = ACT(STAT[0:1, p_idx:p_idx + 1], STAT[0:1, p_t1:p_t1 + 1],
                          AF.Copy, scale=1.0 / 7.0)
                b.wait("dve", "act", sv2)
                iv = DVE("reciprocal", STAT[0:1, p_m:p_m + 1],
                         STAT[0:1, p_idx:p_idx + 1])
                bki, ii = pool.acquire()
                bks, is_ = pool.acquire()
                b.wait("pe", "dve", iv)
                b.wait("pe", "dve", ov1)
                v1 = MM(bki[0:128, 0:1], STAT[0:1, ONES_F:ONES_F + 128],
                        STAT[0:1, p_m:p_m + 1], start=True, stop=True, inc=True)
                b.wait("pe", "act", sv2)
                v2 = MM(bks[0:128, 0:1], STAT[0:1, ONES_F:ONES_F + 128],
                        STAT[0:1, p_idx:p_idx + 1], start=True, stop=True,
                        inc=True)
                b.wait("dve", "pe", v1)
                qsc["isc"] = DVE("tensor_copy", STAT[:, col_isc:col_isc + 1],
                                 bki[0:128, 0:1])
                b.wait("dve", "pe", v2)
                qsc["sc"] = DVE("tensor_copy", STAT[:, col_sc:col_sc + 1],
                                bks[0:128, 0:1])
                pool.release(ii, "dve", qsc["isc"])
                pool.release(is_, "dve", qsc["sc"])

            ffnq.append(quant_tail)

            if phase_end("qkv0"):
                break

            ffn_st = {"cons": wslot["cons"][0], "x_last": None}

            def ffn0_band_substeps(mb):
                bb = {}

                def s_q(half):
                    def f():
                        b.wait("sp", scrf_st["free"][0], scrf_st["free"][1])
                        qs_tot = DMA("sp", "d_qs", SCRF[:, 0:512],
                                     dr["ker0"][mb, :, half * 512:half * 512 + 512])
                        b.wait("dve", "d_qs", qs_tot)
                        b.wait("dve", "dve", qsc["isc"])
                        if ffn_st["cons"] is not None:
                            b.wait("dve", ffn_st["cons"][0], ffn_st["cons"][1])
                        t1v = DVE("tensor_scalar_mul", SCRF[:, 0:512],
                                  SCRF[:, 0:512], STAT[:, col_isc:col_isc + 1])
                        b.wait("dve", "dve", t1v)
                        t2v = DVE("tensor_scalar_add", SCRF[:, 0:512],
                                  SCRF[:, 0:512], MAGIC)
                        b.wait("dve", "dve", t2v)
                        t3v = DVE("tensor_scalar_sub", SCRF[:, 0:512],
                                  SCRF[:, 0:512], MAGIC)
                        b.wait("dve", "dve", t3v)
                        bb["qh%d" % half] = DVE(
                            "tensor_scalar", WB[:, half * 512:half * 512 + 512],
                            SCRF[:, 0:512], -8.0, 7.0, ALU.max, ALU.min)
                        scrf_st["free"] = ("dve", bb["qh%d" % half])
                    return f

                def s_mm(jp, cg):
                    def f():
                        if cg == 0:
                            jpr = pool.acquire_pair()
                            bb["banks%d" % jp] = [
                                (ps[2 * jpr], 2 * jpr),
                                (ps[2 * jpr + 1], 2 * jpr + 1)]
                        b.wait("pe", "dve", bb["qh%d" % cg])
                        banks = bb["banks%d" % jp]
                        for c in range(4 * cg, 4 * cg + 4):
                            for jj in range(2):
                                j = jp * 2 + jj
                                bb["lastv"] = MM(
                                    banks[jj][0][0:128, 0:TT],
                                    WB[:, c * 128:c * 128 + 128],
                                    XN[:, c * T + j * TT: c * T + j * TT + TT],
                                    start=(c == 0), stop=(c == NCH - 1),
                                    inc=(c == NCH - 1 and jj == 1))
                    return f

                def s_ev(jp):
                    def f():
                        for jj in range(2):
                            j = jp * 2 + jj
                            bk, ib = bb["banks%d" % jp][jj]
                            b.wait("dve", "pe", bb["lastv"])
                            b.wait("dve", "dve", qsc["sc"])
                            if ffn_st["x_last"] is not None:
                                b.wait("dve", "dve", ffn_st["x_last"])
                            ffn_st["x_last"] = DVE(
                                "scalar_tensor_tensor",
                                X[:, mb * T + j * TT: mb * T + j * TT + TT],
                                bk[0:128, 0:TT], STAT[:, col_sc:col_sc + 1],
                                X[:, mb * T + j * TT: mb * T + j * TT + TT],
                                ALU.mult, ALU.add)
                            pool.release(ib, "dve", ffn_st["x_last"])
                        if jp == 1:
                            ffn_st["cons"] = ("pe", bb["lastv"])
                    return f

                def s_bias():
                    b.wait("dve", "d_pre", pre_tot)
                    b.wait("dve", "dve", ffn_st["x_last"])
                    ffn_st["x_last"] = DVE(
                        "tensor_scalar_add", X[:, mb * T:(mb + 1) * T],
                        X[:, mb * T:(mb + 1) * T],
                        STAT[:, col_bias0 + mb:col_bias0 + mb + 1])

                return [s_q(0), s_q(1), s_mm(0, 0), s_mm(0, 1), s_ev(0),
                        s_mm(1, 0), s_mm(1, 1), s_ev(1), s_bias]

            for mb in range(8):
                ffnq.extend(ffn0_band_substeps(mb))

            wslot["par"] = 0
            pt_setup([(SCR, PT[0]), (SCR, PT[1]), (WB, 1024), (WB, 1536)],
                     init_cons=wslot["cons"][1])
            ocell = [None]
            pend = None
            for h in range(_NHEADS):
                for qt in range(NTT):
                    pend = attn_l0(h, qt, negB0_v, pend, ocell,
                                   allow_ffn=(h * NTT + qt >= 6))
            for st_ in pend:
                st_()
            while ffnq:
                ffnq.pop(0)()
            ocv = ocell[0]
            ptc = [c for c in pt_state["cons"][2:] if c is not None]
            if ptc:
                wslot["cons"][1] = ("pe", max(v for _, v in ptc))
            wslot["cons"][0] = ffn_st["cons"]
            x_last = ffn_st["x_last"]

            cA = wslot["cons"][0]
            if cA is not None:
                b.wait("gp", cA[0], cA[1])
            ta = DMA("gp", wslot["sem"][0], WB[:, 0:128], dr["a0"][:, :])
            cB = wslot["cons"][1]
            if cB is not None:
                b.wait("gp", cB[0], cB[1])
            tb = DMA("gp", wslot["sem"][1], WB[0:16, 1024:2048], dr["b0"][:, :])
            b.wait("pe", wslot["sem"][0], ta)
            xa_banks = [pool.acquire() for _ in range(NTT)]
            lastxa = None
            for c in range(NCH):
                for j in range(NTT):
                    lastxa = MM(xa_banks[j][0][0:16, 0:TT],
                                WB[:, c * 16:c * 16 + 16],
                                XN[:, c * T + j * TT: c * T + j * TT + TT],
                                start=(c == 0), stop=(c == NCH - 1),
                                inc=(c == NCH - 1 and j == NTT - 1))
            wslot["cons"][0] = ("pe", lastxa)
            xa_evs = []
            for j in range(NTT):
                bk, ib = xa_banks[j]
                b.wait("act", "pe", lastxa)
                ev = ACT(SCR[0:16, j * TT:(j + 1) * TT], bk[0:16, 0:TT], AF.Copy)
                pool.release(ib, "act", ev)
                xa_evs.append(ev)
            b.wait("pe", wslot["sem"][1], tb)
            lora_last = None
            for mb in range(8):
                banks = [pool.acquire() for _ in range(NTT)]
                lastv = None
                for j in range(NTT):
                    b.wait("pe", "act", xa_evs[j])
                    lastv = MM(banks[j][0][0:128, 0:TT],
                               WB[0:16, 1024 + mb * 128:1024 + mb * 128 + 128],
                               SCR[0:16, j * TT:(j + 1) * TT],
                               start=True, stop=True, inc=(j == NTT - 1))
                if mb == 7:
                    wslot["cons"][1] = ("pe", lastv)
                for j in range(NTT):
                    bk, ib = banks[j]
                    b.wait("dve", "pe", lastv)
                    b.wait("dve", "dve", x_last)
                    x_last = DVE("tensor_add",
                                 X[:, mb * T + j * TT: mb * T + j * TT + TT],
                                 X[:, mb * T + j * TT: mb * T + j * TT + TT],
                                 bk[0:128, 0:TT])
                    pool.release(ib, "dve", x_last)
            if phase_end("attn0"):
                break

            x_last, xb1 = out_proj("wp0", ("dve", ocv), L0Q, make_xn=True)
            if phase_end("l0"):
                break

            ln1_v, kept, _ = layernorm(1, None, xb_pre=xb1)
            bk_m, i_m, bk_r, i_r = kept

            xl_ap = X.ap().rearrange("p (c t) -> p c t", t=T)[:, :, T - 1:T]
            xnl_v = DVE("tensor_scalar", STAT[:, col_b1s:col_b1s + 8], xl_ap,
                        bk_m[0:128, 511:512], bk_r[0:128, 511:512],
                        ALU.subtract, ALU.mult)
            b.wait("dve", "dve", xnl_v)
            dbgc = DVE("tensor_copy", STAT[:, 796:804], STAT[:, col_b1s:col_b1s + 8])
            dbgc = DVE("tensor_copy", STAT[:, 804:805], bk_m[0:128, 511:512])
            dbgc = DVE("tensor_copy", STAT[:, 805:806], bk_r[0:128, 511:512])
            pool.release(i_m, "dve", dbgc)
            pool.release(i_r, "dve", dbgc)
            bz1, iz1 = pool.acquire()
            lastz1 = None
            for half in range(2):
                b.wait("sp", scrf_st["free"][0], scrf_st["free"][1])
                qs_tot = DMA("sp", "d_qs", SCRF[:, 0:512],
                             dr["wp1"][:, half * 512:half * 512 + 512])
                b.wait("pe", "d_qs", qs_tot)
                b.wait("pe", "dve", xnl_v)
                for cc in range(4):
                    c = half * 4 + cc
                    lastz1 = MM(bz1[0:128, 0:1], SCRF[:, cc * 128:cc * 128 + 128],
                                STAT[:, col_b1s + c:col_b1s + c + 1],
                                start=(c == 0), stop=(c == 7),
                                inc=(cc == 3))
                scrf_st["free"] = ("pe", lastz1)
            b.wait("act", "pe", lastz1)
            b.wait("act", "d_pre", pre_tot)
            g_v = ACT(STAT[:, col_g:col_g + 1], bz1[0:128, 0:1], AF.Gelu,
                      bias=STAT[:, col_bp1:col_bp1 + 1], scale=1.0)
            pool.release(iz1, "act", g_v)
            bz2, iz2 = pool.acquire()
            b.wait("pe", "act", g_v)
            z2_v = MM(bz2[0:1, 0:8], STAT[:, col_g:col_g + 1],
                      STAT[:, col_wp2:col_wp2 + 8], start=True, stop=True, inc=True)
            b.wait("dve", "pe", z2_v)
            lg_v = DVE("tensor_add", STAT[0:1, p_log:p_log + 8], bz2[0:1, 0:8],
                       STAT[0:1, p_bp2:p_bp2 + 8])
            pool.release(iz2, "dve", lg_v)
            b.wait("dve", "dve", lg_v)
            m_v = DVE("tensor_reduce", STAT[0:1, p_m:p_m + 1],
                      STAT[0:1, p_log:p_log + 8], AX.X, ALU.max)
            b.wait("dve", "dve", m_v)
            eq_v = DVE("tensor_scalar", STAT[0:1, p_eq:p_eq + 8],
                       STAT[0:1, p_log:p_log + 8], STAT[0:1, p_m:p_m + 1], None,
                       ALU.is_equal)
            b.wait("dve", "dve", eq_v)
            c_v = DVE("tensor_mul", STAT[0:1, p_cand:p_cand + 8],
                      STAT[0:1, p_eq:p_eq + 8], STAT[0:1, p_io99:p_io99 + 8])
            b.wait("dve", "dve", c_v)
            c2_v = DVE("tensor_scalar_add", STAT[0:1, p_cand:p_cand + 8],
                       STAT[0:1, p_cand:p_cand + 8], 99.0)
            b.wait("dve", "dve", c2_v)
            ix_v = DVE("tensor_reduce", STAT[0:1, p_idx:p_idx + 1],
                       STAT[0:1, p_cand:p_cand + 8], AX.X, ALU.min)
            b.wait("dve", "dve", ix_v)
            ixi_v = DVE("tensor_copy", IDXI[0:1, 0:1], STAT[0:1, p_idx:p_idx + 1])

            selcell = {}

            def selfn(e):
                r0 = e.alloc_register("selreg")
                e.reg_load(r0, IDXI[0:1, 0:1])
                selcell["r0"] = r0
                selcell["v"] = e.snap(r0, min_val=0, max_val=7)
                selcell["e"] = e
                return None

            def fresh_sel():
                return selcell["e"].snap(selcell["r0"], min_val=0, max_val=7)
            b.wait("gp", "dve", ixi_v)
            b.op("gp", selfn, inc=False)

            def dynfn(out_ap, src, tag=""):
                def fn(e, o=out_ap, s=src, t=tag):
                    try:
                        return e.dma_start(out=o, in_=s())
                    except Exception:
                        print("DYN DMA FAIL tag=", t)
                        raise
                return fn
            b.wait("gp", "pe", lastz1)
            sm_tot = b.dma("gp", "d_sm", dynfn(
                STAT[:, col_b1s:col_b1s + 16],
                lambda: dr["b1e"][bass.ds(selcell["v"], 1), :].rearrange(
                    "e (p q) -> (e p) q", p=128)))
            sm_tot = b.dma("gp", "d_sm", dynfn(
                STAT[:, col_b2s:col_b2s + 8],
                lambda: dr["b2e"][bass.ds(selcell["v"], 1), :].rearrange(
                    "e (p q) -> (e p) q", p=128)))

            b.wait("act", "d_pre", pre_tot)
            lam_v = ACT(STAT[64:65, 160:176], STAT[64:65, 176:192], AF.Sigmoid)

            for c in range(NCH):
                b.wait("dve", "dve", ln1_v)
                b.wait("dve", "dve", x_last)
                x_last = DVE("tensor_add", X[:, c * T:(c + 1) * T],
                             X[:, c * T:(c + 1) * T], XN[:, c * T:(c + 1) * T])
            if phase_end("router"):
                break

            b.wait("act", "d_sm", sm_tot)
            h_done = None
            for grp_i in range(8):
                parA, cA = slot_begin()
                parB, cB = slot_begin()
                for cs in (cA, cB):
                    if cs is not None:
                        b.wait("gp", cs[0], cs[1])
                w1tot = b.dma("gp", wslot["sem"][0], dynfn(
                    WB[:, 0:2048],
                    lambda g=grp_i: dr["w1e"][bass.ds(selcell["v"], 1), :].rearrange(
                        "e (m p x) -> (e p) m x", m=16, p=128)[:, 2 * g:2 * g + 2, :],
                    tag=f"w1 grp={grp_i}"))
                b.wait("pe", wslot["sem"][0], w1tot)
                lastv = None
                for sub in range(2):
                    mb = 2 * grp_i + sub
                    banks = [pool.acquire() for _ in range(NTT)]
                    for c in range(NCH):
                        for j in range(NTT):
                            lastv = MM(banks[j][0][0:128, 0:TT],
                                       WB[:, sub * 1024 + c * 128:
                                          sub * 1024 + c * 128 + 128],
                                       XN[:, c * T + j * TT: c * T + j * TT + TT],
                                       start=(c == 0), stop=(c == NCH - 1),
                                       inc=(c == NCH - 1 and j == NTT - 1))
                    for j in range(NTT):
                        bk, ib = banks[j]
                        b.wait("act", "pe", lastv)
                        h_done = ACT(BIG[:, L1H + mb * T + j * TT:
                                         L1H + mb * T + j * TT + TT],
                                     bk[0:128, 0:TT], AF.Gelu,
                                     bias=STAT[:, col_b1s + mb:col_b1s + mb + 1],
                                     scale=1.0)
                        pool.release(ib, "act", h_done)
                band_release(parA, "pe", lastv)
                band_release(parB, "pe", lastv)
            rs_done = None
            for mb in range(8):
                parA, cA = slot_begin()
                parB, cB = slot_begin()
                for cs in (cA, cB):
                    if cs is not None:
                        b.wait("gp", cs[0], cs[1])
                w2tot = b.dma("gp", wslot["sem"][1], dynfn(
                    WB[:, 0:2048],
                    lambda mb=mb: dr["w2e"][
                        bass.ds(selcell["v"], 1),
                        mb * 262144:(mb + 1) * 262144].rearrange(
                        "e (p q) -> (e p) q", p=128),
                    tag=f"w2 mb={mb}"))
                banks = [pool.acquire() for _ in range(NTT)]
                lastv = None
                b.wait("pe", "act", h_done)
                b.wait("pe", wslot["sem"][1], w2tot)
                for c in range(16):
                    for j in range(NTT):
                        lastv = MM(banks[j][0][0:128, 0:TT],
                                   WB[:, c * 128:c * 128 + 128],
                                   BIG[:, L1H + c * T + j * TT:
                                       L1H + c * T + j * TT + TT],
                                   start=(c == 0), stop=(c == 15),
                                   inc=(c == 15 and j == NTT - 1))
                band_release(parA, "pe", lastv)
                band_release(parB, "pe", lastv)
                for j in range(NTT):
                    bk, ib = banks[j]
                    b.wait("dve", "pe", lastv)
                    b.wait("dve", "d_sm", sm_tot)
                    rs_done = DVE("tensor_single_scalar",
                                  BIG[:, L1RS + mb * T + j * TT:
                                      L1RS + mb * T + j * TT + TT],
                                  bk[0:128, 0:TT],
                                  STAT[:, col_b2s + mb:col_b2s + mb + 1], ALU.add)
                    pool.release(ib, "dve", rs_done)
            bo3_v = DVE("tensor_scalar_mul", STAT[:, col_bo3:col_bo3 + 8],
                        STAT[:, col_bo:col_bo + 8], 0.3)
            for mb in range(8):
                par, sem, tot = band_load(dr["wof"][mb:mb + 1, :, :], cols=1024)
                b.wait("pe", sem, tot)
                b.wait("pe", "dve", rs_done)
                banks = [pool.acquire() for _ in range(NTT)]
                lastv = None
                for c in range(NCH):
                    for j in range(NTT):
                        lastv = MM(banks[j][0][0:128, 0:TT],
                                   WB[:, par * 1024 + c * 128:
                                      par * 1024 + c * 128 + 128],
                                   BIG[:, L1RS + c * T + j * TT:
                                       L1RS + c * T + j * TT + TT],
                                   start=(c == 0), stop=(c == NCH - 1),
                                   inc=(c == NCH - 1 and j == NTT - 1))
                band_release(par, "pe", lastv)
                for j in range(NTT):
                    bk, ib = banks[j]
                    b.wait("dve", "pe", lastv)
                    b.wait("dve", "dve", x_last)
                    x_last = DVE("scalar_tensor_tensor",
                                 X[:, mb * T + j * TT: mb * T + j * TT + TT],
                                 bk[0:128, 0:TT], 0.3,
                                 X[:, mb * T + j * TT: mb * T + j * TT + TT],
                                 ALU.mult, ALU.add)
                    pool.release(ib, "dve", x_last)
                b.wait("dve", "dve", bo3_v)
                b.wait("dve", "dve", x_last)
                x_last = DVE("tensor_scalar_add", X[:, mb * T:(mb + 1) * T],
                             X[:, mb * T:(mb + 1) * T],
                             STAT[:, col_bo3 + mb:col_bo3 + mb + 1])
            if phase_end("ffn1"):
                break

            b.wait("sp", "pe", b.ctr["pe"])
            cs_tot = DMA("sp", "d_cs", BIG[:, L1CS:L1CS + 2 * T], dr["cs"][:, :])
            b.wait("dve", "pe", b.ctr["pe"])
            vt1_ones = vt_ones(L1VT, 260)

            shuf = []
            for i2 in range(0, 32, 2):
                shuf += [i2 + 1, i2]

            def rope(bases, ready):
                done = None
                b.wait("dve", "d_cs", cs_tot)
                if ready is not None:
                    b.wait("dve", ready[0], ready[1])
                for base in bases:
                    for hf in range(2):
                        o = base + hf * 1024
                        co = L1CS + hf * 1024
                        so = L1CS + T + hf * 1024
                        sh_v = DVE("stream_shuffle", SCR[:, SC_Y:SC_Y + 1024],
                                   BIG[:, o:o + 1024], shuf)
                        b.wait("dve", "dve", sh_v)
                        s_v = DVE("tensor_mul", SCR[:, SC_Y:SC_Y + 1024],
                                  SCR[:, SC_Y:SC_Y + 1024],
                                  BIG[:, so:so + 1024])
                        c_v2 = DVE("tensor_mul", BIG[:, o:o + 1024],
                                   BIG[:, o:o + 1024], BIG[:, co:co + 1024])
                        b.wait("dve", "dve", s_v)
                        b.wait("dve", "dve", c_v2)
                        done = DVE("tensor_add", BIG[:, o:o + 1024],
                                   BIG[:, o:o + 1024], SCR[:, SC_Y:SC_Y + 1024])
                return done

            lq1 = lk1 = None
            for mb in range(8):
                qd = projection("wq1", [mb], evac_to(L1Q))
                ropeq_v = rope([L1Q + mb * T], qd)
                lq1 = bscan(L1Q + mb * T, 1, HN_Q, ("dve", ropeq_v),
                            init=(mb == 0))
            ropek_v = None
            for mb in range(2):
                kd = projection("wk1", [mb], evac_to(L1KA))
                ropek_v = rope([L1KA + mb * T], kd)
                lk1 = bscan(L1KA + mb * T, 1, HN_K, ("dve", ropek_v),
                            init=(mb == 0))
            b.wait("sp", "dve", ropek_v)
            kb_tot = 0
            for cc in range(2):
                kb_tot = DMA("sp", "d_kb",
                             BIG[0:64, L1KB + cc * T:L1KB + (cc + 1) * T],
                             BIG[64:128, L1KA + cc * T:L1KA + (cc + 1) * T])
                kb_tot = DMA("sp", "d_kb",
                             BIG[64:128, L1KB + cc * T:L1KB + (cc + 1) * T],
                             BIG[0:64, L1KA + cc * T:L1KA + (cc + 1) * T])
            b.wait("act", "dve", vt1_ones)
            v_project("wv1", list(range(2)), L1VT, 2, 260)
            negB1_v = bfinal(1, 32 ** -0.5, lq1, lk1)
            if phase_end("qkv1"):
                break

            pt_setup([(BIG, PT1_BASE), (BIG, PT1_BASE + 1024),
                      (BIG, PT1_BASE + 2048), (BIG, PT1_BASE + 3072)])
            ocell = [None]
            pend = None
            for h in range(_NHEADS1):
                for qt in range(NTT):
                    pend = attn_l1(h, qt, negB1_v, kb_tot, lam_v, pend, ocell)
            for st_ in pend:
                st_()
            ocv = ocell[0]
            if phase_end("attn1"):
                break

            x_last = out_proj("wo1", ("dve", ocv), L1Q)
            finish()
            break

        def run_stream(eng_name):
            def runner(e):
                for idx_it, item in enumerate(b.stream[eng_name]):
                    kind = item[0]
                    try:
                        if kind == "op":
                            _, fn, inc_eng = item
                            inst = fn(e)
                            if inc_eng is not None:
                                inst.then_inc(sems["c_" + inc_eng], 1)
                        elif kind == "wait":
                            _, sem, val = item
                            key = sems[sem] if sem in sems else sems["c_" + sem]
                            e.wait_ge(key, val)
                        else:
                            _, fn, sem = item
                            inst = fn(e)
                            inst.then_inc(sems[sem], 16)
                    except Exception:
                        import traceback
                        print(f"REPLAY FAIL {eng_name}[{idx_it}] kind={kind}")
                        traceback.print_exc()
                        raise
            return runner

        with nc.allow_low_precision(reason="bf16 elementwise on bf16 matmul operands"), \
             nc.Block() as block:
            block.tensor(run_stream("pe"))
            block.scalar(run_stream("act"))
            block.vector(run_stream("dve"))
            block.sync(run_stream("sp"))
            block.gpsimd(run_stream("gp"))

    nc.finalize()
    return nc



def _img(A):
    K, F = A.shape
    kc, mb = K // 128, F // 128
    return np.ascontiguousarray(
        A.reshape(kc, 128, mb, 128).transpose(2, 1, 0, 3)).reshape(mb, 128, kc * 128)


def _rope_tables():
    j = np.arange(16, dtype=np.float64)
    invf = 10000.0 ** (-j / 16.0)
    t = np.arange(T, dtype=np.float64)
    ang = np.outer(invf, t)
    rows_j = (np.arange(128) % 32) // 2
    C = np.cos(ang)[rows_j]
    S = np.sin(ang)[rows_j]
    sign = np.where(np.arange(128) % 2 == 0, -1.0, 1.0)[:, None]
    return np.concatenate([C, S * sign], axis=1).astype(ml_dtypes.bfloat16)


def prep_inputs(L0_attn_x, L0_attn_w_attn, L0_attn_w_proj, L0_ffn_kernel,
                L0_ffn_bias, L0_ffn_a, L0_ffn_b, L1_attn_Wq, L1_attn_Wkv,
                L1_attn_Wout, L1_attn_lambda_init, L1_attn_rms_weight,
                L1_ffn_Wp1, L1_ffn_bp1, L1_ffn_Wp2, L1_ffn_bp2, L1_ffn_W1,
                L1_ffn_b1, L1_ffn_W2, L1_ffn_b2, L1_ffn_Wo, L1_ffn_bo):
    f = np.asarray
    x = f(L0_attn_x, dtype=np.float32)
    Aqkv = np.ascontiguousarray(f(L0_attn_w_attn, dtype=np.float32).T)
    w = {
        "wq0": _img(Aqkv[:, 0:1024]),
        "wk0": _img(Aqkv[:, 1024:2048]),
        "wv0": _img(Aqkv[:, 2048:3072]),
        "wp0": _img(np.ascontiguousarray(f(L0_attn_w_proj, dtype=np.float32).T)),
        "ker0": _img(f(L0_ffn_kernel, dtype=np.float32)),
        "b0": np.ascontiguousarray(f(L0_ffn_b, dtype=np.float32)),
        "bias0": np.ascontiguousarray(
            f(L0_ffn_bias, dtype=np.float32).reshape(8, 128).T),
        "wq1": _img(np.ascontiguousarray(f(L1_attn_Wq, dtype=np.float32).T)),
        "wo1": _img(np.ascontiguousarray(f(L1_attn_Wout, dtype=np.float32).T)),
        "lam": f(L1_attn_lambda_init, dtype=np.float32).reshape(1, 16),
        "rmsw": f(L1_attn_rms_weight, dtype=np.float32).reshape(64, 1),
        "wp1": _img(f(L1_ffn_Wp1, dtype=np.float32))[0],
        "wp2": f(L1_ffn_Wp2, dtype=np.float32),
        "bp1": f(L1_ffn_bp1, dtype=np.float32).reshape(128, 1),
        "bp2": f(L1_ffn_bp2, dtype=np.float32).reshape(1, 8),
        "b1e": np.stack([np.ascontiguousarray(
            f(L1_ffn_b1, dtype=np.float32)[e].reshape(16, 128).T).reshape(-1)
            for e in range(8)]),
        "b2e": np.stack([np.ascontiguousarray(
            f(L1_ffn_b2, dtype=np.float32)[e].reshape(8, 128).T).reshape(-1)
            for e in range(8)]),
        "wof": _img(f(L1_ffn_Wo, dtype=np.float32)),
        "bof": np.ascontiguousarray(f(L1_ffn_bo, dtype=np.float32).reshape(8, 128).T),
    }
    a = f(L0_ffn_a, dtype=np.float32)
    w["a0"] = np.ascontiguousarray(
        a.reshape(8, 128, 16).transpose(1, 0, 2)).reshape(128, 128)
    Akv = np.ascontiguousarray(f(L1_attn_Wkv, dtype=np.float32).T)
    w["wk1"] = _img(Akv[:, 0:256])
    w["wv1"] = _img(Akv[:, 256:512])
    W1 = f(L1_ffn_W1, dtype=np.float32)
    w["w1e"] = np.stack([_img(W1[e]).reshape(-1) for e in range(8)])
    W2 = f(L1_ffn_W2, dtype=np.float32)
    w["w2e"] = np.stack([_img(W2[e]).reshape(-1) for e in range(8)])
    w["cs"] = _rope_tables()
    kk, qq = np.meshgrid(np.arange(128), np.arange(128), indexing="ij")
    w["mask"] = (qq >= kk).astype(ml_dtypes.bfloat16)
    w["maskn"] = np.where(qq > kk, np.float32(-1e9),
                          np.float32(0)).astype(ml_dtypes.bfloat16)
    w["ident"] = np.eye(128, dtype=ml_dtypes.bfloat16)
    w["identf"] = np.eye(128, dtype=np.float32)
    w["grp2"] = (np.arange(128)[:, None] // 64 ==
                 np.arange(2)[None, :]).astype(ml_dtypes.bfloat16)
    w["grp4"] = (np.arange(128)[:, None] // 32 ==
                 np.arange(4)[None, :]).astype(ml_dtypes.bfloat16)
    w["io99"] = (np.arange(8, dtype=np.float32) - 99.0).reshape(1, 8)
    in_maps = []
    for core in range(8):
        m = {"xT": np.ascontiguousarray(x[core].T)}
        m.update(w)
        in_maps.append(m)
    return in_maps


def kernel(**inputs):
    in_maps = prep_inputs(**inputs)
    nc = build_nc(upto=_UPTO, debug=_DEBUG)
    res = run_bass_kernel_spmd(nc, in_maps, core_ids=list(range(8)))
    out = np.stack([np.ascontiguousarray(r["yT"].T) for r in res.results])
    return out.astype(np.float32)


# version 26 (final)
# speedup vs baseline: 1.0371x; 1.0147x over previous
import contextlib

import numpy as np
import ml_dtypes

import concourse.bass as bass
import concourse.mybir as mybir
from concourse.bass_utils import run_bass_kernel_spmd

F32 = mybir.dt.float32
BF16 = mybir.dt.bfloat16
I32 = mybir.dt.int32
AF = mybir.ActivationFunctionType
ALU = mybir.AluOpType
AX = mybir.AxisListType

D = 1024
T = 2048
NCH = 8
NTT = 4
TT = 512
MAGIC = 12582912.0

_UPTO = "all"
_DEBUG = False
_NHEADS = 16
_NHEADS1 = 16

ENGS = ("pe", "act", "dve", "sp", "gp")


class Builder:
    def __init__(self):
        self.stream = {e: [] for e in ENGS}
        self.ctr = {e: 0 for e in ENGS}
        self.dmatot = {}
        self.waited = {}

    def op(self, eng, fn, inc=False):
        if inc:
            self.ctr[eng] += 1
            self.stream[eng].append(("op", fn, eng))
            return self.ctr[eng]
        self.stream[eng].append(("op", fn, None))
        return None

    def wait(self, eng, sem, val):
        if val is None or val <= 0:
            return
        key = (eng, sem)
        if self.waited.get(key, 0) >= val:
            return
        self.waited[key] = val
        self.stream[eng].append(("wait", sem, val))

    def dma(self, eng, sem, fn):
        self.dmatot[sem] = self.dmatot.get(sem, 0) + 16
        self.stream[eng].append(("dma", fn, sem))
        return self.dmatot[sem]


class BankPool:
    def __init__(self, b, tensors):
        self.b = b
        self.tensors = tensors
        self.free_q = list(range(len(tensors)))
        self.cons = {i: None for i in range(len(tensors))}

    def acquire(self):
        i = self.free_q.pop(0)
        c = self.cons[i]
        if c is not None:
            self.b.wait("pe", c[0], c[1])
        return self.tensors[i], i

    def acquire_pair(self):
        order = [i // 2 for i in self.free_q]
        cand = [j for k, j in enumerate(order) if j in order[k + 1:]]
        for j in cand + list(range(len(self.tensors) // 2)):
            if 2 * j in self.free_q and 2 * j + 1 in self.free_q:
                self.free_q.remove(2 * j)
                self.free_q.remove(2 * j + 1)
                for i in (2 * j, 2 * j + 1):
                    c = self.cons[i]
                    if c is not None:
                        self.b.wait("pe", c[0], c[1])
                return j
        raise RuntimeError("no aligned psum pair free")

    def acquire_deferred(self):
        i = self.free_q.pop(0)
        return self.tensors[i], i, self.cons[i]

    def acquire_pair_deferred(self):
        order = [i // 2 for i in self.free_q]
        cand = [j for k, j in enumerate(order) if j in order[k + 1:]]
        for j in cand + list(range(len(self.tensors) // 2)):
            if 2 * j in self.free_q and 2 * j + 1 in self.free_q:
                self.free_q.remove(2 * j)
                self.free_q.remove(2 * j + 1)
                return j, [self.cons[2 * j], self.cons[2 * j + 1]]
        raise RuntimeError("no aligned psum pair free")

    def release(self, i, eng, val):
        self.cons[i] = (eng, val)
        self.free_q.append(i)


def build_nc(upto="all", debug=False):
    _NHEADS = globals()["_NHEADS"]
    _NHEADS1 = globals()["_NHEADS1"]
    nc = bass.Bass("TRN2", target_bir_lowering=False)
    dr = {}

    def din(name, shape, dt=F32):
        dr[name] = nc.dram_tensor(name, list(shape), dt, kind="ExternalInput").ap()

    din("xT", (D, T))
    din("wq0", (8, 128, 1024)); din("wk0", (8, 128, 1024)); din("wv0", (8, 128, 1024))
    din("wp0", (8, 128, 1024))
    din("ker0", (8, 128, 1024))
    din("a0", (128, 128)); din("b0", (16, 1024)); din("bias0", (128, 8))
    din("wq1", (8, 128, 1024)); din("wk1", (2, 128, 1024)); din("wv1", (2, 128, 1024))
    din("wo1", (8, 128, 1024))
    din("lam", (1, 16)); din("rmsw", (64, 1))
    din("wp1", (128, 1024)); din("wp2", (128, 8)); din("bp1", (128, 1)); din("bp2", (1, 8))
    din("w1e", (8, 16 * 128 * 1024)); din("w2e", (8, 8 * 128 * 2048))
    din("b1e", (8, 2048)); din("b2e", (8, 1024))
    din("wof", (8, 128, 1024)); din("bof", (128, 8))
    din("cs", (128, 2 * T), BF16)
    din("mask", (128, 128), BF16)
    din("ident", (128, 128), BF16)
    din("identf", (128, 128))
    din("grp2", (128, 2), BF16); din("grp4", (128, 4), BF16)
    din("io99", (1, 8))

    yT = nc.dram_tensor("yT", [D, T], F32, kind="ExternalOutput").ap()
    dbg_xn = dbg_big = dbg_stat = None
    if debug:
        dbg_xn = nc.dram_tensor("dbg_xn", [128, NCH * T], BF16, kind="ExternalOutput").ap()
        dbg_big = nc.dram_tensor("dbg_big", [128, 49408], BF16, kind="ExternalOutput").ap()
        dbg_stat = nc.dram_tensor("dbg_stat", [128, 832], F32, kind="ExternalOutput").ap()

    b = Builder()

    with contextlib.ExitStack() as ctx:
        def sbuf(name, shape, dt):
            return ctx.enter_context(nc.sbuf_tensor(name, list(shape), dt))

        X = sbuf("X", (128, NCH * T), F32)
        XN = sbuf("XN", (128, NCH * T), BF16)
        BIG = sbuf("BIGB", (128, 49408), BF16)
        WB = sbuf("WB", (128, 2048), BF16)
        SCR = sbuf("SCR", (128, 2048), BF16)
        SCRF = sbuf("SCRF", (128, 512), F32)
        STAT = sbuf("STAT", (128, 832), F32)
        SMALL = sbuf("SMALL", (128, 72), BF16)
        MASKT = sbuf("MASKT", (128, 128), BF16)
        IDT = sbuf("IDT", (128, 128), BF16)
        IDF = sbuf("IDF", (128, 128), F32)
        IDXI = sbuf("IDXI", (1, 1), I32)
        ps_big = [ctx.enter_context(nc.psum_tensor(f"psb{i}", [128, 1024], F32))
                  for i in range(4)]
        ps = []
        for i in range(4):
            ps.append(ps_big[i][0:128, 0:512])
            ps.append(ps_big[i][0:128, 512:1024])

        sems = {}
        for e in ENGS:
            sems["c_" + e] = ctx.enter_context(nc.semaphore("c_" + e))
        for d in ("d_pre", "d_x", "d_wa", "d_wb", "d_qs", "d_cs", "d_kb",
                  "d_sm", "d_y", "d_mx", "d_dbg"):
            sems[d] = ctx.enter_context(nc.semaphore(d))

        HN_Q, HN_QT, HN_K, HN_KT = 0, 8, 16, 24
        ONES_F = 32
        RSTD = 160
        _statc = [672]

        def stat_col(n=1):
            c = _statc[0]; _statc[0] += n
            assert _statc[0] <= 744
            return c

        col_negB = [stat_col(), stat_col()]
        col_isc = stat_col(); col_sc = stat_col()
        col_bias0 = stat_col(8)
        col_bo = stat_col(8); col_bo3 = stat_col(8)
        col_b1s = stat_col(16); col_b2s = stat_col(8)
        col_bp1 = stat_col(); col_rmsw = stat_col()
        col_g = stat_col(); col_wp2 = stat_col(8)
        col_mx = stat_col(); col_mxt = stat_col()
        col_eps5 = stat_col(); col_eps6 = stat_col()
        _p0c = [744]

        def p0_col(n=1):
            c = _p0c[0]; _p0c[0] += n
            assert _p0c[0] <= 832
            return c

        p_io99 = p0_col(8); p_bp2 = p0_col(8); p_log = p0_col(8)
        p_eq = p0_col(8); p_cand = p0_col(8)
        p_m = p0_col(); p_idx = p0_col(); p_prod = p0_col(); p_t1 = p0_col()

        SM_G2, SM_G4, ONES_B = 0, 2, 6

        L0Q, L0K, L0VT = 0, 16384, 32768
        L1Q, L1KA, L1KB, L1VT, L1CS = 0, 16384, 20480, 24576, 28736
        L1H, L1RS = 0, 32768
        PT = (0, 512)
        SC_Y, SC_U = 1024, 1536

        pool = BankPool(b, ps)

        def MM(out_ap, lhsT, rhs, start, stop, inc=False, tp=None):
            def fn(e, o=out_ap, l=lhsT, r=rhs, st=start, sp=stop, t=tp):
                return e.matmul(o, l, r, start=st, stop=sp, tile_position=t)
            return b.op("pe", fn, inc=inc)

        def ACT(out_ap, in_ap, func, bias=0.0, scale=1.0, inc=True):
            def fn(e, o=out_ap, i=in_ap, f=func, bi=bias, sc=scale):
                return e.activation(o, i, f, bias=bi, scale=sc)
            return b.op("act", fn, inc=inc)

        def DVE(meth, *args, inc=True, **kw):
            def fn(e, m=meth, a=args, k=kw):
                return getattr(e, m)(*a, **k)
            return b.op("dve", fn, inc=inc)

        def DMA(eng, sem, out_ap, in_ap, **kw):
            def fn(e, o=out_ap, i=in_ap, k=kw):
                return e.dma_start(out=o, in_=i, **k)
            return b.dma(eng, sem, fn)

        xloads = []
        for c in range(NCH):
            xloads.append(DMA("sp", "d_x", X[:, c * T:(c + 1) * T],
                              dr["xT"][c * 128:(c + 1) * 128, :]))
        pre = 0
        pre = DMA("sp", "d_pre", MASKT[:, :], dr["mask"][:, :])
        pre = DMA("sp", "d_pre", IDT[:, :], dr["ident"][:, :])
        pre = DMA("sp", "d_pre", IDF[:, :], dr["identf"][:, :])
        pre = DMA("sp", "d_pre", SMALL[:, SM_G2:SM_G2 + 2], dr["grp2"][:, :])
        pre = DMA("sp", "d_pre", SMALL[:, SM_G4:SM_G4 + 4], dr["grp4"][:, :])
        pre = DMA("sp", "d_pre", STAT[0:1, p_io99:p_io99 + 8], dr["io99"][:, :])
        pre = DMA("sp", "d_pre", STAT[0:1, p_bp2:p_bp2 + 8], dr["bp2"][:, :])
        pre = DMA("sp", "d_pre", STAT[:, col_bias0:col_bias0 + 8], dr["bias0"][:, :])
        pre = DMA("sp", "d_pre", STAT[:, col_bo:col_bo + 8], dr["bof"][:, :])
        pre = DMA("sp", "d_pre", STAT[:, col_bp1:col_bp1 + 1], dr["bp1"][:, :])
        pre = DMA("sp", "d_pre", STAT[0:64, col_rmsw:col_rmsw + 1], dr["rmsw"][:, :])
        pre = DMA("sp", "d_pre", STAT[64:65, 176:192], dr["lam"][:, :])
        pre = DMA("sp", "d_pre", STAT[:, col_wp2:col_wp2 + 8], dr["wp2"][:, :])
        pre_tot = pre

        e5_v = DVE("memset", STAT[:, col_eps5:col_eps5 + 1], 1e-5, inc=True)
        e6_v = DVE("memset", STAT[:, col_eps6:col_eps6 + 1], 1e-6, inc=True)
        ov1 = DVE("memset", STAT[:, ONES_F:ONES_F + 128], 1.0, inc=True)
        b.wait("dve", "dve", ov1)
        ones_v = DVE("memset", SMALL[:, ONES_B:ONES_B + 64], 1.0, inc=True)

        def vt_ones(base, per_tt):
            ap3 = BIG[:, base:base + 16 * per_tt].rearrange(
                "p (t x) -> p t x", x=per_tt).rearrange(
                "p t (h o) -> p (t h) o", o=65)[:, :, 64:65]
            return DVE("memset", ap3, 1.0, inc=True)

        wslot = {"par": 0, "cons": {0: None, 1: None}, "sem": ("d_wa", "d_wb")}
        scrf_st = {"free": None}

        def slot_begin():
            par = wslot["par"]
            wslot["par"] ^= 1
            return par, wslot["cons"][par]

        def band_load(in_ap, cols=1024):
            par, c = slot_begin()
            if c is not None:
                b.wait("gp", c[0], c[1])
            tot = DMA("gp", wslot["sem"][par],
                      WB[:, par * 1024:par * 1024 + cols], in_ap)
            return par, wslot["sem"][par], tot

        def band_release(par, eng, val):
            wslot["cons"][par] = (eng, val)

        def layernorm(layer, x_ready, xb_pre=None):
            if xb_pre is not None:
                xb = xb_pre
            else:
                if x_ready is not None:
                    b.wait("act", x_ready[0], x_ready[1])
                xb = {}
                for c in range(NCH):
                    b.wait("act", "d_x", xloads[c])
                    xb[c] = ACT(XN[:, c * T:(c + 1) * T],
                                X[:, c * T:(c + 1) * T], AF.Copy)
            kept = None
            last_n = None
            prev_br = None
            tevs = []
            slot_user = {0: None, 1: None}
            for j in range(NTT):
                bk_sx, i_sx = pool.acquire()
                bk_s2, i_s2 = pool.acquire()
                sqv = {}
                vx = v2 = None
                for c in range(NCH):
                    sl = (c % 2) * 512
                    if slot_user[c % 2] is not None:
                        b.wait("act", "pe", slot_user[c % 2])
                    sqv[c] = ACT(SCR[:, sl:sl + 512],
                                 XN[:, c * T + j * TT: c * T + j * TT + TT],
                                 AF.Square)
                    b.wait("pe", "act", xb[c])
                    vx = MM(bk_sx[0:1, 0:TT], SMALL[0:128, ONES_B:ONES_B + 1],
                            XN[:, c * T + j * TT: c * T + j * TT + TT],
                            start=(c == 0), stop=(c == NCH - 1), inc=(c == NCH - 1))
                    b.wait("pe", "act", sqv[c])
                    v2 = MM(bk_s2[0:1, 0:TT], SMALL[0:128, ONES_B:ONES_B + 1],
                            SCR[:, sl:sl + 512],
                            start=(c == 0), stop=(c == NCH - 1), inc=True)
                    slot_user[c % 2] = v2
                b.wait("dve", "pe", vx)
                if scrf_st["free"] is not None:
                    b.wait("dve", scrf_st["free"][0], scrf_st["free"][1])
                m_v = DVE("tensor_scalar_mul", SCRF[0:1, 0:TT], bk_sx[0:1, 0:TT],
                          1.0 / 1024.0)
                pool.release(i_sx, "dve", m_v)
                b.wait("dve", "dve", m_v)
                b.wait("dve", "pe", prev_br)
                t_v = DVE("tensor_mul", STAT[0:1, RSTD:RSTD + TT],
                          SCRF[0:1, 0:TT], SCRF[0:1, 0:TT])
                b.wait("dve", "pe", v2)
                b.wait("dve", "dve", t_v)
                var_v = DVE("scalar_tensor_tensor", bk_s2[0:1, 0:TT],
                            bk_s2[0:1, 0:TT], 1.0 / 1024.0,
                            STAT[0:1, RSTD:RSTD + TT], ALU.mult, ALU.subtract)
                b.wait("act", "dve", var_v)
                b.wait("act", "dve", e5_v)
                ln_v = ACT(STAT[0:1, RSTD:RSTD + TT], bk_s2[0:1, 0:TT],
                           AF.Ln, bias=STAT[0:1, col_eps5:col_eps5 + 1],
                           scale=1.0)
                pool.release(i_s2, "act", ln_v)
                b.wait("act", "act", ln_v)
                ex_v = ACT(STAT[0:1, RSTD:RSTD + TT], STAT[0:1, RSTD:RSTD + TT],
                           AF.Exp, scale=-0.5)
                bk_m, i_m = pool.acquire()
                bk_r, i_r = pool.acquire()
                b.wait("pe", "dve", ov1)
                b.wait("pe", "dve", m_v)
                bm = MM(bk_m[0:128, 0:TT], STAT[0:1, ONES_F:ONES_F + 128],
                        SCRF[0:1, 0:TT], start=True, stop=True, inc=True)
                b.wait("pe", "act", ex_v)
                br = MM(bk_r[0:128, 0:TT], STAT[0:1, ONES_F:ONES_F + 128],
                        STAT[0:1, RSTD:RSTD + TT], start=True, stop=True, inc=True)
                prev_br = br
                for c in range(NCH):
                    b.wait("dve", "pe", bm)
                    tv = DVE("tensor_sub", SCRF[:, 0:TT],
                             X[:, c * T + j * TT: c * T + j * TT + TT],
                             bk_m[0:128, 0:TT])
                    b.wait("dve", "pe", br)
                    b.wait("dve", "dve", tv)
                    last_n = DVE("tensor_mul",
                                 XN[:, c * T + j * TT: c * T + j * TT + TT],
                                 SCRF[:, 0:TT], bk_r[0:128, 0:TT])
                if layer == 1 and j == NTT - 1:
                    kept = (bk_m, i_m, bk_r, i_r)
                else:
                    pool.release(i_m, "dve", last_n)
                    pool.release(i_r, "dve", last_n)
                scrf_st["free"] = ("dve", last_n)
                tevs.append(("dve", last_n))
            return last_n, kept, tevs

        def evac_to(base):
            def dest(mb, j, bk):
                ev = ACT(BIG[:, base + mb * T + j * TT: base + mb * T + j * TT + TT],
                         bk[0:128, 0:TT], AF.Copy)
                return ("act", ev)
            return dest

        def projection(img, mbs, dest_fn, nkc=NCH, rhs=None, rhs_base=0,
                       rhs_ready=None, rhs_tevs=None):
            src = rhs if rhs is not None else XN
            last_dest = None
            for mb in mbs:
                par, sem, tot = band_load(dr[img][mb:mb + 1, :, :nkc * 128],
                                          cols=nkc * 128)
                b.wait("pe", sem, tot)
                if rhs_ready is not None:
                    b.wait("pe", rhs_ready[0], rhs_ready[1])
                banks = [pool.acquire() for _ in range(NTT)]
                lastv = None
                if rhs_tevs is not None:
                    for j in range(NTT):
                        b.wait("pe", rhs_tevs[j][0], rhs_tevs[j][1])
                        for c in range(nkc):
                            lastv = MM(
                                banks[j][0][0:128, 0:TT],
                                WB[:, par * 1024 + c * 128:
                                   par * 1024 + c * 128 + 128],
                                src[:, rhs_base + c * T + j * TT:
                                    rhs_base + c * T + j * TT + TT],
                                start=(c == 0), stop=(c == nkc - 1),
                                inc=(c == nkc - 1 and j == NTT - 1))
                else:
                    for c in range(nkc):
                        for j in range(NTT):
                            lastv = MM(
                                banks[j][0][0:128, 0:TT],
                                WB[:, par * 1024 + c * 128:
                                   par * 1024 + c * 128 + 128],
                                src[:, rhs_base + c * T + j * TT:
                                    rhs_base + c * T + j * TT + TT],
                                start=(c == 0), stop=(c == nkc - 1),
                                inc=(c == nkc - 1 and j == NTT - 1))
                band_release(par, "pe", lastv)
                for j in range(NTT):
                    bk, ib = banks[j]
                    b.wait("act", "pe", lastv)
                    b.wait("dve", "pe", lastv)
                    eng, val = dest_fn(mb, j, bk)
                    pool.release(ib, eng, val)
                    last_dest = (eng, val)
            return last_dest

        def v_project(img, mbs, vt_base, heads_per_mb, per_tt):
            for mi, mb in enumerate(mbs):
                par, sem, tot = band_load(dr[img][mb:mb + 1, :, :], cols=1024)
                b.wait("pe", sem, tot)
                banks = [pool.acquire() for _ in range(NTT)]
                lastv = None
                for c in range(NCH):
                    for j in range(NTT):
                        lastv = MM(
                            banks[j][0][0:128, 0:TT],
                            WB[:, par * 1024 + c * 128: par * 1024 + c * 128 + 128],
                            XN[:, c * T + j * TT: c * T + j * TT + TT],
                            start=(c == 0), stop=(c == NCH - 1),
                            inc=(c == NCH - 1 and j == NTT - 1))
                band_release(par, "pe", lastv)
                tp_prev = None
                for wave in range(2):
                    evs = {}
                    for jj in range(2):
                        j = wave * 2 + jj
                        bk, ib = banks[j]
                        b.wait("act", "pe", lastv)
                        b.wait("act", "pe", tp_prev)
                        evs[jj] = ACT(SCR[:, jj * TT:(jj + 1) * TT],
                                      bk[0:128, 0:TT], AF.Copy)
                        pool.release(ib, "act", evs[jj])
                    for ttw in range(8):
                        tt = wave * 8 + ttw
                        bk, ib = pool.acquire()
                        b.wait("pe", "act", evs[ttw // 4])
                        b.wait("pe", "d_pre", pre_tot)
                        tp_out = bk[0:128, 0:64].bitcast(BF16)

                        def tpfn(e, o=tp_out,
                                 i=SCR[:, ttw * 128:ttw * 128 + 128],
                                 idn=IDT[:, :]):
                            return e.transpose(o, i, idn)
                        tv = b.op("pe", tpfn, inc=True)
                        tp_prev = tv
                        base = vt_base + tt * per_tt + mi * heads_per_mb * 65
                        o_ap = BIG[:, base:base + heads_per_mb * 65].rearrange(
                            "p (h o) -> p h o", o=65)[:, :, 0:64]
                        i_ap = tp_out.rearrange("p (h o) -> p h o", o=64)[
                            :, 0:heads_per_mb, :]
                        b.wait("act", "pe", tv)
                        ev2 = ACT(o_ap, i_ap, AF.Copy)
                        pool.release(ib, "act", ev2)
            return

        def out_proj(img, o_ready, o_base, make_xn=False):
            lastadd = None
            xb = {}
            for mb in range(8):
                par, sem, tot = band_load(dr[img][mb:mb + 1, :, :], cols=1024)
                b.wait("pe", sem, tot)
                b.wait("pe", o_ready[0], o_ready[1])
                banks = [pool.acquire() for _ in range(NTT)]
                lastv = None
                for c in range(NCH):
                    for j in range(NTT):
                        lastv = MM(banks[j][0][0:128, 0:TT],
                                   WB[:, par * 1024 + c * 128: par * 1024 + c * 128 + 128],
                                   BIG[:, o_base + c * T + j * TT:
                                       o_base + c * T + j * TT + TT],
                                   start=(c == 0), stop=(c == NCH - 1),
                                   inc=(c == NCH - 1 and j == NTT - 1))
                band_release(par, "pe", lastv)
                for j in range(NTT):
                    bk, ib = banks[j]
                    b.wait("dve", "pe", lastv)
                    av = DVE("tensor_add", X[:, mb * T + j * TT: mb * T + j * TT + TT],
                             X[:, mb * T + j * TT: mb * T + j * TT + TT],
                             bk[0:128, 0:TT])
                    pool.release(ib, "dve", av)
                    lastadd = av
                if make_xn:
                    b.wait("act", "dve", lastadd)
                    xb[mb] = ACT(XN[:, mb * T:(mb + 1) * T],
                                 X[:, mb * T:(mb + 1) * T], AF.Copy)
            if make_xn:
                return lastadd, xb
            return lastadd

        def bscan(base_col, nch, acc_col, ready, init=True):
            if init:
                DVE("memset", STAT[:, acc_col:acc_col + 1], 0.0)
            last = None
            for c in range(nch):
                for j in range(NTT):
                    if ready is not None:
                        b.wait("dve", ready[0], ready[1])
                    rv = DVE("tensor_reduce", STAT[:, HN_QT:HN_QT + 1],
                             BIG[:, base_col + c * T + j * TT:
                                 base_col + c * T + j * TT + TT],
                             AX.X, ALU.max, apply_absolute_value=True)
                    b.wait("dve", "dve", rv)
                    last = DVE("tensor_max", STAT[:, acc_col:acc_col + 1],
                               STAT[:, acc_col:acc_col + 1],
                               STAT[:, HN_QT:HN_QT + 1])
            return last

        def bfinal(layer, scale, lq, lk):
            bq_, ibq = pool.acquire()
            bk_, ibk = pool.acquire()
            b.wait("pe", "dve", max(lq, lk))
            b.wait("pe", "d_pre", pre_tot)
            tq = MM(bq_[0:1, 0:128], STAT[:, HN_Q:HN_Q + 1], IDF[:, :],
                    start=True, stop=True, inc=True)
            tk = MM(bk_[0:1, 0:128], STAT[:, HN_K:HN_K + 1], IDF[:, :],
                    start=True, stop=True, inc=True)
            b.wait("dve", "pe", tk)
            rq = DVE("tensor_reduce", STAT[0:1, HN_Q:HN_Q + 1],
                     bq_[0:1, 0:128], AX.X, ALU.max)
            rk = DVE("tensor_reduce", STAT[0:1, HN_K:HN_K + 1],
                     bk_[0:1, 0:128], AX.X, ALU.max)
            pool.release(ibq, "dve", rq)
            pool.release(ibk, "dve", rk)
            b.wait("dve", "dve", rk)
            pv = DVE("tensor_mul", STAT[0:1, p_prod:p_prod + 1],
                     STAT[0:1, HN_Q:HN_Q + 1], STAT[0:1, HN_K:HN_K + 1])
            b.wait("dve", "dve", pv)
            nv = DVE("tensor_scalar_mul", STAT[0:1, p_t1:p_t1 + 1],
                     STAT[0:1, p_prod:p_prod + 1], -1.02 * scale)
            bk, ib = pool.acquire()
            b.wait("pe", "dve", nv)
            bv = MM(bk[0:128, 0:1], STAT[0:1, ONES_F:ONES_F + 128],
                    STAT[0:1, p_t1:p_t1 + 1], start=True, stop=True, inc=True)
            b.wait("dve", "pe", bv)
            cv = DVE("tensor_copy", STAT[:, col_negB[layer]:col_negB[layer] + 1],
                     bk[0:128, 0:1])
            pool.release(ib, "dve", cv)
            return cv

        PT1_BASE = 33344
        pt_state = {"i": 0, "cons": [], "slots": []}
        fin1_cell = [None]
        oev1_cell = [None]
        comb1_cell = [None]
        ffnq = []

        def pt_setup(slots, init_cons=None):
            pt_state["i"] = 0
            pt_state["slots"] = slots
            pt_state["cons"] = [init_cons] * len(slots)

        def pt_begin():
            i = pt_state["i"]
            pt_state["i"] = (i + 1) % len(pt_state["slots"])
            c = pt_state["cons"][i]
            if c is not None:
                b.wait("act", c[0], c[1])
            return i

        def pt_ap(i, N):
            tens, base = pt_state["slots"][i]
            return tens[:, base:base + N]

        def attn_l0(h, qt, negB_v, pend, ocell, allow_ffn=True):
            bh = 64 * (h % 2)
            ch = h // 2
            bkO, iO, bkO_cons = pool.acquire_deferred()
            bkO_wait = [bkO_cons]
            stb = []
            for _ in range(2):
                t_, i_, c_ = pool.acquire_deferred()
                stb.append([t_, i_, None, c_])
            n = 4 * qt + 4
            stage = []
            pv_last = [None]
            pend = list(pend) if pend else []

            def drain(k):
                while k > 0:
                    if pend:
                        pend.pop(0)()
                    elif ffnq:
                        ffnq.pop(0)()
                    else:
                        return
                    k -= 1

            def emit_A(idx):
                i = idx
                r = i - 4 * qt
                col0 = 128 * r if r >= 0 else 0
                N = 512 - col0
                bkS, iS, prev_ev, first_c = stb[idx % 2]
                if first_c is not None:
                    b.wait("pe", first_c[0], first_c[1])
                    stb[idx % 2][3] = None
                if prev_ev is not None:
                    b.wait("pe", "act", prev_ev)
                sv = MM(bkS[0:128, 0:N],
                        BIG[bh:bh + 64,
                            L0K + ch * T + i * 128: L0K + ch * T + i * 128 + 128],
                        BIG[bh:bh + 64,
                            L0Q + ch * T + qt * TT + col0: L0Q + ch * T + qt * TT + 512],
                        start=True, stop=True, inc=True)
                par = pt_begin()
                b.wait("act", "pe", sv)
                b.wait("act", "dve", negB_v)
                ev = ACT(pt_ap(par, N), bkS[0:128, 0:N], AF.Exp,
                         bias=STAT[:, col_negB[0]:col_negB[0] + 1], scale=0.125)
                stb[idx % 2][2] = ev
                if r >= 0:
                    b.wait("dve", "act", ev)
                    mv = DVE("tensor_mul", pt_ap(par, 128),
                             pt_ap(par, 128), MASKT[:, :])
                    pw = ("dve", mv)
                else:
                    pw = ("act", ev)
                stage.append((idx, par, pw, col0, N))

            def emit_B():
                idx, par, pw, col0, N = stage.pop(0)
                if bkO_wait[0] is not None:
                    b.wait("pe", bkO_wait[0][0], bkO_wait[0][1])
                    bkO_wait[0] = None
                b.wait("pe", pw[0], pw[1])
                padm = min(128, 16640 - (idx * 1040 + h * 65))
                pv = MM(bkO[0:padm, col0:512],
                        BIG[:, L0VT + idx * 1040 + h * 65:
                            L0VT + idx * 1040 + h * 65 + padm],
                        pt_ap(par, N),
                        start=(idx == 0), stop=(idx == n - 1), inc=True)
                pt_state["cons"][par] = ("pe", pv)
                pv_last[0] = pv

            LAG = 3
            for idx in range(n):
                emit_A(idx)
                if idx >= 3 and pend:
                    pend.pop(0)()
                elif allow_ffn and ffnq:
                    ffnq.pop(0)()
                if idx >= LAG:
                    emit_B()
            while stage:
                emit_B()
            drain(len(pend))
            for bkS, iS, prev_ev, _fc in stb:
                pool.release(iS, "act", prev_ev)

            env = {}

            b.wait("dve", "pe", pv_last[0])
            env["rv"] = DVE("reciprocal", SCR[64:65, SC_Y:SC_Y + 512],
                            bkO[64:65, 0:512])

            def s_bcast():
                bkB, iB = pool.acquire()
                env["bkB"], env["iB"] = bkB, iB
                b.wait("pe", "dve", env["rv"])
                env["bv"] = MM(bkB[0:64, 0:512],
                               SMALL[64:65, ONES_B:ONES_B + 64],
                               SCR[64:65, SC_Y:SC_Y + 512],
                               start=True, stop=True, inc=True)

            def s_bev():
                b.wait("act", "pe", env["bv"])
                bev = ACT(SCR[0:64, SC_U:SC_U + 512],
                          env["bkB"][0:64, 0:512], AF.Copy)
                env["bev"] = bev
                pool.release(env["iB"], "act", bev)

            def s_final():
                ocol = L0Q + ch * T + qt * TT
                b.wait("dve", "act", env["bev"])
                if h % 2 == 0:
                    cv = DVE("tensor_mul", BIG[0:64, ocol:ocol + TT],
                             bkO[0:64, 0:512], SCR[0:64, SC_U:SC_U + 512])
                    pool.release(iO, "dve", cv)
                else:
                    yv = DVE("tensor_mul", SCR[0:64, SC_U:SC_U + 512],
                             bkO[0:64, 0:512], SCR[0:64, SC_U:SC_U + 512])
                    pool.release(iO, "dve", yv)
                    bkR, iR = pool.acquire()
                    b.wait("pe", "dve", yv)
                    rrv = MM(bkR[64:128, 0:512], IDT[0:64, 0:64],
                             SCR[0:64, SC_U:SC_U + 512], start=True, stop=True,
                             inc=True, tp=(0, 64))
                    b.wait("dve", "pe", rrv)
                    cv = DVE("tensor_copy", BIG[64:128, ocol:ocol + TT],
                             bkR[64:128, 0:512])
                    pool.release(iR, "dve", cv)
                ocell[0] = cv

            return [s_bcast, s_bev, s_final]

        def attn_l1(h, qt, negB_v, kb_tot, lam_v, pend, ocell):
            g = h // 4
            bh = 64 * (h % 2)
            ksrc = L1KA if 64 * (g % 2) == bh else L1KB
            kcol = ksrc + (g // 2) * T
            jO, jO_cons = pool.acquire_pair_deferred()
            bkO1, iO1 = ps[2 * jO], 2 * jO
            bkO2, iO2 = ps[2 * jO + 1], 2 * jO + 1
            jS_d = [pool.acquire_pair_deferred(), pool.acquire_pair_deferred()]
            jSs = [jS_d[0][0], jS_d[1][0]]
            st_wait = [jS_d[0][1], jS_d[1][1]]
            jO_wait = [jO_cons]
            st_ev = [None, None]
            n = 4 * qt + 4
            stage = []
            pv_last = [None]
            b.wait("pe", "d_kb", kb_tot)
            pend = list(pend) if pend else []

            def drain(k):
                while pend and k > 0:
                    pend.pop(0)()
                    k -= 1

            def emit_A(idx):
                i = idx
                r = i - 4 * qt
                col0 = 128 * r if r >= 0 else 0
                N = 512 - col0
                w = idx % 2
                jS = jSs[w]
                if st_wait[w] is not None:
                    for c_ in st_wait[w]:
                        if c_ is not None:
                            b.wait("pe", c_[0], c_[1])
                    st_wait[w] = None
                if st_ev[w] is not None:
                    b.wait("pe", "act", st_ev[w])
                svs = None
                for s in range(2):
                    rb = bh + 32 * s
                    svs = MM(ps[2 * jS + s][0:128, 0:N],
                             BIG[rb:rb + 32, kcol + i * 128: kcol + i * 128 + 128],
                             BIG[rb:rb + 32,
                                 L1Q + (h // 2) * T + qt * TT + col0:
                                 L1Q + (h // 2) * T + qt * TT + 512],
                             start=True, stop=True, inc=(s == 1), tp=(rb, 0))
                par = pt_begin()
                ptt, ptb = pt_state["slots"][par]
                in_ap = ps_big[jS][0:128, 0:1024].rearrange(
                    "p (s x) -> p s x", s=2)[:, :, 0:N]
                out_ap = ptt[:, ptb:ptb + 1024].rearrange(
                    "p (s x) -> p s x", s=2)[:, :, 0:N]
                b.wait("act", "pe", svs)
                b.wait("act", "dve", negB_v)
                ev = ACT(out_ap, in_ap, AF.Exp,
                         bias=STAT[:, col_negB[1]:col_negB[1] + 1],
                         scale=0.17677669529663687)
                st_ev[w] = ev
                if r >= 0:
                    b.wait("dve", "act", ev)
                    for s2_ in range(2):
                        mo = ptb + 512 * s2_
                        mv = DVE("tensor_mul", ptt[:, mo:mo + 128],
                                 ptt[:, mo:mo + 128], MASKT[:, :])
                    pw = ("dve", mv)
                else:
                    pw = ("act", ev)
                stage.append((idx, par, pw, col0, N))

            def emit_B():
                idx, par, pw, col0, N = stage.pop(0)
                ptt, ptb = pt_state["slots"][par]
                pv = None
                if jO_wait[0] is not None:
                    for c_ in jO_wait[0]:
                        if c_ is not None:
                            b.wait("pe", c_[0], c_[1])
                    jO_wait[0] = None
                padm = min(128, 4160 + 2 * T - (idx * 260 + g * 65))
                for s in range(2):
                    b.wait("pe", pw[0], pw[1])
                    bkT = bkO1 if s == 0 else bkO2
                    pv = MM(bkT[0:padm, col0:512],
                            BIG[:, L1VT + idx * 260 + g * 65:
                                L1VT + idx * 260 + g * 65 + padm],
                            ptt[:, ptb + 512 * s: ptb + 512 * s + N],
                            start=(idx == 0), stop=(idx == n - 1), inc=(s == 1))
                pt_state["cons"][par] = ("pe", pv)
                pv_last[0] = pv

            LAG = 3
            for idx in range(n):
                emit_A(idx)
                if idx >= 2:
                    drain(1)
                if idx >= LAG:
                    emit_B()
            while stage:
                emit_B()
            while len(pend) > 5:
                pend.pop(0)()
            for w in range(2):
                for s in range(2):
                    pool.release(2 * jSs[w] + s, "act", st_ev[w])

            env = {}

            b.wait("act", "pe", pv_last[0])
            if comb1_cell[0] is not None:
                b.wait("act", "dve", comb1_cell[0])
            oe1 = ACT(SCR[0:65, 0:512], bkO1[0:65, 0:512], AF.Copy)
            oe2 = ACT(SCR[0:65, 512:1024], bkO2[0:65, 0:512], AF.Copy)
            env["oe"] = oe2
            pool.release(iO1, "act", oe2)
            pool.release(iO2, "act", oe2)
            b.wait("dve", "act", oe2)
            env["r1"] = DVE("reciprocal", SCR[64:65, SC_Y:SC_Y + 512],
                            SCR[64:65, 0:512])
            env["r2"] = DVE("reciprocal", SCR[64:65, SC_U:SC_U + 512],
                            SCR[64:65, 512:1024])
            b.wait("dve", "dve", env["r2"])
            b.wait("dve", "act", lam_v)
            env["r2l"] = DVE("tensor_scalar_mul",
                             SCR[64:65, SC_U:SC_U + 512],
                             SCR[64:65, SC_U:SC_U + 512],
                             STAT[64:65, 160 + h:161 + h])

            def s_bc1():
                jB = pool.acquire_pair()
                env["bkB1"], env["iB1"] = ps[2 * jB], 2 * jB
                env["bkB2"], env["iB2"] = ps[2 * jB + 1], 2 * jB + 1
                b.wait("pe", "dve", env["r1"])
                env["bv1"] = MM(env["bkB1"][0:64, 0:512],
                                SMALL[64:65, ONES_B:ONES_B + 64],
                                SCR[64:65, SC_Y:SC_Y + 512],
                                start=True, stop=True, inc=True)

            def s_bc2():
                b.wait("pe", "dve", env["r2l"])
                env["bv2"] = MM(env["bkB2"][0:64, 0:512],
                                SMALL[64:65, ONES_B:ONES_B + 64],
                                SCR[64:65, SC_U:SC_U + 512],
                                start=True, stop=True, inc=True)

            def s_t1():
                b.wait("dve", "pe", env["bv1"])
                b.wait("dve", "act", env["oe"])
                env["t1"] = DVE("tensor_mul", SCR[0:64, SC_Y:SC_Y + 512],
                                SCR[0:64, 0:512], env["bkB1"][0:64, 0:512])

            def s_u():
                b.wait("dve", "pe", env["bv2"])
                u = DVE("tensor_mul", SCR[0:64, SC_U:SC_U + 512],
                        SCR[0:64, 512:1024], env["bkB2"][0:64, 0:512])
                comb1_cell[0] = u
                b.wait("dve", "dve", u)
                yv = DVE("tensor_sub", SCR[0:64, SC_Y:SC_Y + 512],
                         SCR[0:64, SC_Y:SC_Y + 512],
                         SCR[0:64, SC_U:SC_U + 512])
                b.wait("dve", "dve", yv)
                env["sqv"] = DVE("tensor_mul", SCR[0:64, SC_U:SC_U + 512],
                                 SCR[0:64, SC_Y:SC_Y + 512],
                                 SCR[0:64, SC_Y:SC_Y + 512])
                env["uv"] = u

            def s_ss():
                b.wait("pe", "dve", env["sqv"])
                env["ssv"] = MM(env["bkB1"][0:1, 0:512],
                                SMALL[0:64, ONES_B:ONES_B + 1],
                                SCR[0:64, SC_U:SC_U + 512],
                                start=True, stop=True, inc=True)

            def s_ln():
                b.wait("act", "pe", env["ssv"])
                b.wait("act", "dve", e6_v)
                c = scrf_st["free"]
                if c is not None:
                    b.wait("act", c[0], c[1])
                lnv = ACT(SCRF[0:1, 0:512], env["bkB1"][0:1, 0:512], AF.Ln,
                          bias=STAT[0:1, col_eps6:col_eps6 + 1], scale=1.0 / 64.0)
                pool.release(env["iB1"], "act", lnv)
                env["lnv"] = lnv

            def s_exp():
                b.wait("act", "act", env["lnv"])
                env["xev"] = ACT(BIG[0:1, 32832:32832 + 512],
                                 SCRF[0:1, 0:512], AF.Exp, scale=-0.5)

            def s_rb():
                b.wait("pe", "dve", env["uv"])
                b.wait("pe", "act", env["xev"])
                env["rbv"] = MM(env["bkB2"][0:64, 0:512],
                                SMALL[0:1, ONES_B:ONES_B + 64],
                                BIG[0:1, 32832:32832 + 512],
                                start=True, stop=True, inc=True)

            def s_final():
                ocol = L1Q + (h // 2) * T + qt * TT
                b.wait("dve", "pe", env["rbv"])
                if h % 2 == 0:
                    cv = DVE("scalar_tensor_tensor", BIG[0:64, ocol:ocol + TT],
                             SCR[0:64, SC_Y:SC_Y + 512],
                             STAT[0:64, col_rmsw:col_rmsw + 1],
                             env["bkB2"][0:64, 0:512], ALU.mult, ALU.mult)
                    pool.release(env["iB2"], "dve", cv)
                else:
                    yv2 = DVE("scalar_tensor_tensor", SCR[0:64, SC_U:SC_U + 512],
                              SCR[0:64, SC_Y:SC_Y + 512],
                              STAT[0:64, col_rmsw:col_rmsw + 1],
                              env["bkB2"][0:64, 0:512], ALU.mult, ALU.mult)
                    pool.release(env["iB2"], "dve", yv2)
                    bkR, iR = pool.acquire()
                    b.wait("pe", "dve", yv2)
                    rrv = MM(bkR[64:128, 0:512], IDT[0:64, 0:64],
                             SCR[0:64, SC_U:SC_U + 512], start=True, stop=True,
                             inc=True, tp=(0, 64))
                    b.wait("dve", "pe", rrv)
                    cv = DVE("tensor_copy", BIG[64:128, ocol:ocol + TT],
                             bkR[64:128, 0:512])
                    pool.release(iR, "dve", cv)
                ocell[0] = cv
                fin1_cell[0] = cv

            return pend + [s_bc1, s_bc2, s_t1, s_u, s_ss,
                           s_ln, s_exp, s_rb, s_final]

        def finish():
            for e in ("pe", "act", "dve", "gp"):
                b.wait("sp", e, b.ctr[e])
            ytot = 0
            for c in range(NCH):
                ytot = DMA("sp", "d_y", yT[c * 128:(c + 1) * 128, :],
                           X[:, c * T:(c + 1) * T])
            if debug:
                dtot = DMA("sp", "d_dbg", dbg_xn[:, :], XN[:, :])
                dtot = DMA("sp", "d_dbg", dbg_big[:, :], BIG[:, :])
                dtot = DMA("sp", "d_dbg", dbg_stat[:, :], STAT[:, :])
                b.wait("sp", "d_dbg", dtot)
            b.wait("sp", "d_y", ytot)

        done = [False]

        def phase_end(name):
            if upto == name:
                finish()
                done[0] = True
            return done[0]

        while True:
            ln0_v, _, ln0_tevs = layernorm(0, None)
            if phase_end("ln0"):
                break

            vt0_ones = vt_ones(L0VT, 1040)
            qd = projection("wq0", list(range(8)), evac_to(L0Q),
                            rhs_tevs=ln0_tevs)
            lq0 = bscan(L0Q, NCH, HN_Q, qd)
            kd = projection("wk0", list(range(8)), evac_to(L0K))
            lk0 = bscan(L0K, NCH, HN_K, kd)
            b.wait("act", "dve", vt0_ones)
            v_project("wv0", list(range(8)), L0VT, 2, 1040)
            negB0_v = bfinal(0, 0.125, lq0, lk0)

            qst = {"mx_last": None}

            def quant_scan_step(hb):
                def f():
                    if hb == 0:
                        qst["mx_last"] = DVE("memset",
                                             STAT[:, col_mx:col_mx + 1], 0.0)
                    b.wait("sp", scrf_st["free"][0], scrf_st["free"][1])
                    qs_tot = DMA("sp", "d_qs", SCRF[:, 0:512],
                                 dr["ker0"][hb // 2, :,
                                            (hb % 2) * 512:(hb % 2) * 512 + 512])
                    b.wait("dve", "d_qs", qs_tot)
                    b.wait("dve", "dve", qst["mx_last"])
                    rv = DVE("tensor_reduce", STAT[:, col_mxt:col_mxt + 1],
                             SCRF[:, 0:512], AX.X, ALU.max,
                             apply_absolute_value=True)
                    b.wait("dve", "dve", rv)
                    qst["mx_last"] = DVE("tensor_max",
                                         STAT[:, col_mx:col_mx + 1],
                                         STAT[:, col_mx:col_mx + 1],
                                         STAT[:, col_mxt:col_mxt + 1])
                    scrf_st["free"] = ("dve", rv)
                return f

            for hb in range(16):
                ffnq.append(quant_scan_step(hb))
            qsc = {}

            def quant_tail():
                bmx, imx = pool.acquire()
                b.wait("pe", "dve", qst["mx_last"])
                b.wait("pe", "d_pre", pre_tot)
                tmx = MM(bmx[0:1, 0:128], STAT[:, col_mx:col_mx + 1], IDF[:, :],
                         start=True, stop=True, inc=True)
                b.wait("dve", "pe", tmx)
                mxr = DVE("tensor_reduce", STAT[0:1, p_t1:p_t1 + 1],
                          bmx[0:1, 0:128], AX.X, ALU.max)
                pool.release(imx, "dve", mxr)
                b.wait("act", "dve", mxr)
                sv2 = ACT(STAT[0:1, p_idx:p_idx + 1], STAT[0:1, p_t1:p_t1 + 1],
                          AF.Copy, scale=1.0 / 7.0)
                b.wait("dve", "act", sv2)
                iv = DVE("reciprocal", STAT[0:1, p_m:p_m + 1],
                         STAT[0:1, p_idx:p_idx + 1])
                bki, ii = pool.acquire()
                bks, is_ = pool.acquire()
                b.wait("pe", "dve", iv)
                b.wait("pe", "dve", ov1)
                v1 = MM(bki[0:128, 0:1], STAT[0:1, ONES_F:ONES_F + 128],
                        STAT[0:1, p_m:p_m + 1], start=True, stop=True, inc=True)
                b.wait("pe", "act", sv2)
                v2 = MM(bks[0:128, 0:1], STAT[0:1, ONES_F:ONES_F + 128],
                        STAT[0:1, p_idx:p_idx + 1], start=True, stop=True,
                        inc=True)
                b.wait("dve", "pe", v1)
                qsc["isc"] = DVE("tensor_copy", STAT[:, col_isc:col_isc + 1],
                                 bki[0:128, 0:1])
                b.wait("dve", "pe", v2)
                qsc["sc"] = DVE("tensor_copy", STAT[:, col_sc:col_sc + 1],
                                bks[0:128, 0:1])
                pool.release(ii, "dve", qsc["isc"])
                pool.release(is_, "dve", qsc["sc"])

            ffnq.append(quant_tail)

            if phase_end("qkv0"):
                break

            ffn_st = {"cons": wslot["cons"][0], "x_last": None}

            def ffn0_band_substeps(mb):
                bb = {}

                def s_q(half):
                    def f():
                        b.wait("sp", scrf_st["free"][0], scrf_st["free"][1])
                        qs_tot = DMA("sp", "d_qs", SCRF[:, 0:512],
                                     dr["ker0"][mb, :, half * 512:half * 512 + 512])
                        b.wait("dve", "d_qs", qs_tot)
                        b.wait("dve", "dve", qsc["isc"])
                        if ffn_st["cons"] is not None:
                            b.wait("dve", ffn_st["cons"][0], ffn_st["cons"][1])
                        t1v = DVE("tensor_scalar_mul", SCRF[:, 0:512],
                                  SCRF[:, 0:512], STAT[:, col_isc:col_isc + 1])
                        b.wait("dve", "dve", t1v)
                        t2v = DVE("tensor_scalar_add", SCRF[:, 0:512],
                                  SCRF[:, 0:512], MAGIC)
                        b.wait("dve", "dve", t2v)
                        t3v = DVE("tensor_scalar_sub", SCRF[:, 0:512],
                                  SCRF[:, 0:512], MAGIC)
                        b.wait("dve", "dve", t3v)
                        bb["qh%d" % half] = DVE(
                            "tensor_scalar", WB[:, half * 512:half * 512 + 512],
                            SCRF[:, 0:512], -8.0, 7.0, ALU.max, ALU.min)
                        scrf_st["free"] = ("dve", bb["qh%d" % half])
                    return f

                def s_mm(jp, cg):
                    def f():
                        if cg == 0:
                            jpr = pool.acquire_pair()
                            bb["banks%d" % jp] = [
                                (ps[2 * jpr], 2 * jpr),
                                (ps[2 * jpr + 1], 2 * jpr + 1)]
                        b.wait("pe", "dve", bb["qh%d" % cg])
                        banks = bb["banks%d" % jp]
                        for c in range(4 * cg, 4 * cg + 4):
                            for jj in range(2):
                                j = jp * 2 + jj
                                bb["lastv"] = MM(
                                    banks[jj][0][0:128, 0:TT],
                                    WB[:, c * 128:c * 128 + 128],
                                    XN[:, c * T + j * TT: c * T + j * TT + TT],
                                    start=(c == 0), stop=(c == NCH - 1),
                                    inc=(c == NCH - 1 and jj == 1))
                    return f

                def s_ev(jp):
                    def f():
                        for jj in range(2):
                            j = jp * 2 + jj
                            bk, ib = bb["banks%d" % jp][jj]
                            b.wait("dve", "pe", bb["lastv"])
                            b.wait("dve", "dve", qsc["sc"])
                            if ffn_st["x_last"] is not None:
                                b.wait("dve", "dve", ffn_st["x_last"])
                            ffn_st["x_last"] = DVE(
                                "scalar_tensor_tensor",
                                X[:, mb * T + j * TT: mb * T + j * TT + TT],
                                bk[0:128, 0:TT], STAT[:, col_sc:col_sc + 1],
                                X[:, mb * T + j * TT: mb * T + j * TT + TT],
                                ALU.mult, ALU.add)
                            pool.release(ib, "dve", ffn_st["x_last"])
                        if jp == 1:
                            ffn_st["cons"] = ("pe", bb["lastv"])
                    return f

                def s_bias():
                    b.wait("dve", "d_pre", pre_tot)
                    b.wait("dve", "dve", ffn_st["x_last"])
                    ffn_st["x_last"] = DVE(
                        "tensor_scalar_add", X[:, mb * T:(mb + 1) * T],
                        X[:, mb * T:(mb + 1) * T],
                        STAT[:, col_bias0 + mb:col_bias0 + mb + 1])

                return [s_q(0), s_q(1), s_mm(0, 0), s_mm(0, 1), s_ev(0),
                        s_mm(1, 0), s_mm(1, 1), s_ev(1), s_bias]

            for mb in range(8):
                ffnq.extend(ffn0_band_substeps(mb))

            wslot["par"] = 0
            pt_setup([(SCR, PT[0]), (SCR, PT[1]), (WB, 1024), (WB, 1536)],
                     init_cons=wslot["cons"][1])
            ocell = [None]
            pend = None
            for h in range(_NHEADS):
                for qt in range(NTT):
                    pend = attn_l0(h, qt, negB0_v, pend, ocell,
                                   allow_ffn=(h * NTT + qt >= 6))
            for st_ in pend:
                st_()
            while ffnq:
                ffnq.pop(0)()
            ocv = ocell[0]
            ptc = [c for c in pt_state["cons"][2:] if c is not None]
            if ptc:
                wslot["cons"][1] = ("pe", max(v for _, v in ptc))
            wslot["cons"][0] = ffn_st["cons"]
            x_last = ffn_st["x_last"]

            cA = wslot["cons"][0]
            if cA is not None:
                b.wait("gp", cA[0], cA[1])
            ta = DMA("gp", wslot["sem"][0], WB[:, 0:128], dr["a0"][:, :])
            cB = wslot["cons"][1]
            if cB is not None:
                b.wait("gp", cB[0], cB[1])
            tb = DMA("gp", wslot["sem"][1], WB[0:16, 1024:2048], dr["b0"][:, :])
            b.wait("pe", wslot["sem"][0], ta)
            xa_banks = [pool.acquire() for _ in range(NTT)]
            lastxa = None
            for c in range(NCH):
                for j in range(NTT):
                    lastxa = MM(xa_banks[j][0][0:16, 0:TT],
                                WB[:, c * 16:c * 16 + 16],
                                XN[:, c * T + j * TT: c * T + j * TT + TT],
                                start=(c == 0), stop=(c == NCH - 1),
                                inc=(c == NCH - 1 and j == NTT - 1))
            wslot["cons"][0] = ("pe", lastxa)
            xa_evs = []
            for j in range(NTT):
                bk, ib = xa_banks[j]
                b.wait("act", "pe", lastxa)
                ev = ACT(SCR[0:16, j * TT:(j + 1) * TT], bk[0:16, 0:TT], AF.Copy)
                pool.release(ib, "act", ev)
                xa_evs.append(ev)
            b.wait("pe", wslot["sem"][1], tb)
            lora_last = None
            for mb in range(8):
                banks = [pool.acquire() for _ in range(NTT)]
                lastv = None
                for j in range(NTT):
                    b.wait("pe", "act", xa_evs[j])
                    lastv = MM(banks[j][0][0:128, 0:TT],
                               WB[0:16, 1024 + mb * 128:1024 + mb * 128 + 128],
                               SCR[0:16, j * TT:(j + 1) * TT],
                               start=True, stop=True, inc=(j == NTT - 1))
                if mb == 7:
                    wslot["cons"][1] = ("pe", lastv)
                for j in range(NTT):
                    bk, ib = banks[j]
                    b.wait("dve", "pe", lastv)
                    b.wait("dve", "dve", x_last)
                    x_last = DVE("tensor_add",
                                 X[:, mb * T + j * TT: mb * T + j * TT + TT],
                                 X[:, mb * T + j * TT: mb * T + j * TT + TT],
                                 bk[0:128, 0:TT])
                    pool.release(ib, "dve", x_last)
            if phase_end("attn0"):
                break

            x_last, xb1 = out_proj("wp0", ("dve", ocv), L0Q, make_xn=True)
            if phase_end("l0"):
                break

            ln1_v, kept, _ = layernorm(1, None, xb_pre=xb1)
            bk_m, i_m, bk_r, i_r = kept

            xl_ap = X.ap().rearrange("p (c t) -> p c t", t=T)[:, :, T - 1:T]
            xnl_v = DVE("tensor_scalar", STAT[:, col_b1s:col_b1s + 8], xl_ap,
                        bk_m[0:128, 511:512], bk_r[0:128, 511:512],
                        ALU.subtract, ALU.mult)
            b.wait("dve", "dve", xnl_v)
            dbgc = DVE("tensor_copy", STAT[:, 796:804], STAT[:, col_b1s:col_b1s + 8])
            dbgc = DVE("tensor_copy", STAT[:, 804:805], bk_m[0:128, 511:512])
            dbgc = DVE("tensor_copy", STAT[:, 805:806], bk_r[0:128, 511:512])
            pool.release(i_m, "dve", dbgc)
            pool.release(i_r, "dve", dbgc)
            bz1, iz1 = pool.acquire()
            lastz1 = None
            for half in range(2):
                b.wait("sp", scrf_st["free"][0], scrf_st["free"][1])
                qs_tot = DMA("sp", "d_qs", SCRF[:, 0:512],
                             dr["wp1"][:, half * 512:half * 512 + 512])
                b.wait("pe", "d_qs", qs_tot)
                b.wait("pe", "dve", xnl_v)
                for cc in range(4):
                    c = half * 4 + cc
                    lastz1 = MM(bz1[0:128, 0:1], SCRF[:, cc * 128:cc * 128 + 128],
                                STAT[:, col_b1s + c:col_b1s + c + 1],
                                start=(c == 0), stop=(c == 7),
                                inc=(cc == 3))
                scrf_st["free"] = ("pe", lastz1)
            b.wait("act", "pe", lastz1)
            b.wait("act", "d_pre", pre_tot)
            g_v = ACT(STAT[:, col_g:col_g + 1], bz1[0:128, 0:1], AF.Gelu,
                      bias=STAT[:, col_bp1:col_bp1 + 1], scale=1.0)
            pool.release(iz1, "act", g_v)
            bz2, iz2 = pool.acquire()
            b.wait("pe", "act", g_v)
            z2_v = MM(bz2[0:1, 0:8], STAT[:, col_g:col_g + 1],
                      STAT[:, col_wp2:col_wp2 + 8], start=True, stop=True, inc=True)
            b.wait("dve", "pe", z2_v)
            lg_v = DVE("tensor_add", STAT[0:1, p_log:p_log + 8], bz2[0:1, 0:8],
                       STAT[0:1, p_bp2:p_bp2 + 8])
            pool.release(iz2, "dve", lg_v)
            b.wait("dve", "dve", lg_v)
            m_v = DVE("tensor_reduce", STAT[0:1, p_m:p_m + 1],
                      STAT[0:1, p_log:p_log + 8], AX.X, ALU.max)
            b.wait("dve", "dve", m_v)
            eq_v = DVE("tensor_scalar", STAT[0:1, p_eq:p_eq + 8],
                       STAT[0:1, p_log:p_log + 8], STAT[0:1, p_m:p_m + 1], None,
                       ALU.is_equal)
            b.wait("dve", "dve", eq_v)
            c_v = DVE("tensor_mul", STAT[0:1, p_cand:p_cand + 8],
                      STAT[0:1, p_eq:p_eq + 8], STAT[0:1, p_io99:p_io99 + 8])
            b.wait("dve", "dve", c_v)
            c2_v = DVE("tensor_scalar_add", STAT[0:1, p_cand:p_cand + 8],
                       STAT[0:1, p_cand:p_cand + 8], 99.0)
            b.wait("dve", "dve", c2_v)
            ix_v = DVE("tensor_reduce", STAT[0:1, p_idx:p_idx + 1],
                       STAT[0:1, p_cand:p_cand + 8], AX.X, ALU.min)
            b.wait("dve", "dve", ix_v)
            ixi_v = DVE("tensor_copy", IDXI[0:1, 0:1], STAT[0:1, p_idx:p_idx + 1])

            selcell = {}

            def selfn(e):
                r0 = e.alloc_register("selreg")
                e.reg_load(r0, IDXI[0:1, 0:1])
                selcell["r0"] = r0
                selcell["v"] = e.snap(r0, min_val=0, max_val=7)
                selcell["e"] = e
                return None

            def fresh_sel():
                return selcell["e"].snap(selcell["r0"], min_val=0, max_val=7)
            b.wait("gp", "dve", ixi_v)
            b.op("gp", selfn, inc=False)

            def dynfn(out_ap, src, tag=""):
                def fn(e, o=out_ap, s=src, t=tag):
                    try:
                        return e.dma_start(out=o, in_=s())
                    except Exception:
                        print("DYN DMA FAIL tag=", t)
                        raise
                return fn
            b.wait("gp", "pe", lastz1)
            sm_tot = b.dma("gp", "d_sm", dynfn(
                STAT[:, col_b1s:col_b1s + 16],
                lambda: dr["b1e"][bass.ds(selcell["v"], 1), :].rearrange(
                    "e (p q) -> (e p) q", p=128)))
            sm_tot = b.dma("gp", "d_sm", dynfn(
                STAT[:, col_b2s:col_b2s + 8],
                lambda: dr["b2e"][bass.ds(selcell["v"], 1), :].rearrange(
                    "e (p q) -> (e p) q", p=128)))

            b.wait("act", "d_pre", pre_tot)
            lam_v = ACT(STAT[64:65, 160:176], STAT[64:65, 176:192], AF.Sigmoid)

            for c in range(NCH):
                b.wait("dve", "dve", ln1_v)
                b.wait("dve", "dve", x_last)
                x_last = DVE("tensor_add", X[:, c * T:(c + 1) * T],
                             X[:, c * T:(c + 1) * T], XN[:, c * T:(c + 1) * T])
            if phase_end("router"):
                break

            b.wait("act", "d_sm", sm_tot)
            h_done = None
            for grp_i in range(8):
                parA, cA = slot_begin()
                parB, cB = slot_begin()
                for cs in (cA, cB):
                    if cs is not None:
                        b.wait("gp", cs[0], cs[1])
                w1tot = b.dma("gp", wslot["sem"][0], dynfn(
                    WB[:, 0:2048],
                    lambda g=grp_i: dr["w1e"][bass.ds(selcell["v"], 1), :].rearrange(
                        "e (m p x) -> (e p) m x", m=16, p=128)[:, 2 * g:2 * g + 2, :],
                    tag=f"w1 grp={grp_i}"))
                b.wait("pe", wslot["sem"][0], w1tot)
                lastv = None
                for sub in range(2):
                    mb = 2 * grp_i + sub
                    banks = [pool.acquire() for _ in range(NTT)]
                    for c in range(NCH):
                        for j in range(NTT):
                            lastv = MM(banks[j][0][0:128, 0:TT],
                                       WB[:, sub * 1024 + c * 128:
                                          sub * 1024 + c * 128 + 128],
                                       XN[:, c * T + j * TT: c * T + j * TT + TT],
                                       start=(c == 0), stop=(c == NCH - 1),
                                       inc=(c == NCH - 1 and j == NTT - 1))
                    for j in range(NTT):
                        bk, ib = banks[j]
                        b.wait("act", "pe", lastv)
                        h_done = ACT(BIG[:, L1H + mb * T + j * TT:
                                         L1H + mb * T + j * TT + TT],
                                     bk[0:128, 0:TT], AF.Gelu,
                                     bias=STAT[:, col_b1s + mb:col_b1s + mb + 1],
                                     scale=1.0)
                        pool.release(ib, "act", h_done)
                band_release(parA, "pe", lastv)
                band_release(parB, "pe", lastv)
            rs_done = None
            for mb in range(8):
                parA, cA = slot_begin()
                parB, cB = slot_begin()
                for cs in (cA, cB):
                    if cs is not None:
                        b.wait("gp", cs[0], cs[1])
                w2tot = b.dma("gp", wslot["sem"][1], dynfn(
                    WB[:, 0:2048],
                    lambda mb=mb: dr["w2e"][
                        bass.ds(selcell["v"], 1),
                        mb * 262144:(mb + 1) * 262144].rearrange(
                        "e (p q) -> (e p) q", p=128),
                    tag=f"w2 mb={mb}"))
                banks = [pool.acquire() for _ in range(NTT)]
                lastv = None
                b.wait("pe", "act", h_done)
                b.wait("pe", wslot["sem"][1], w2tot)
                for c in range(16):
                    for j in range(NTT):
                        lastv = MM(banks[j][0][0:128, 0:TT],
                                   WB[:, c * 128:c * 128 + 128],
                                   BIG[:, L1H + c * T + j * TT:
                                       L1H + c * T + j * TT + TT],
                                   start=(c == 0), stop=(c == 15),
                                   inc=(c == 15 and j == NTT - 1))
                band_release(parA, "pe", lastv)
                band_release(parB, "pe", lastv)
                for j in range(NTT):
                    bk, ib = banks[j]
                    b.wait("dve", "pe", lastv)
                    b.wait("dve", "d_sm", sm_tot)
                    rs_done = DVE("tensor_single_scalar",
                                  BIG[:, L1RS + mb * T + j * TT:
                                      L1RS + mb * T + j * TT + TT],
                                  bk[0:128, 0:TT],
                                  STAT[:, col_b2s + mb:col_b2s + mb + 1], ALU.add)
                    pool.release(ib, "dve", rs_done)
            bo3_v = DVE("tensor_scalar_mul", STAT[:, col_bo3:col_bo3 + 8],
                        STAT[:, col_bo:col_bo + 8], 0.3)
            for mb in range(8):
                par, sem, tot = band_load(dr["wof"][mb:mb + 1, :, :], cols=1024)
                b.wait("pe", sem, tot)
                b.wait("pe", "dve", rs_done)
                banks = [pool.acquire() for _ in range(NTT)]
                lastv = None
                for c in range(NCH):
                    for j in range(NTT):
                        lastv = MM(banks[j][0][0:128, 0:TT],
                                   WB[:, par * 1024 + c * 128:
                                      par * 1024 + c * 128 + 128],
                                   BIG[:, L1RS + c * T + j * TT:
                                       L1RS + c * T + j * TT + TT],
                                   start=(c == 0), stop=(c == NCH - 1),
                                   inc=(c == NCH - 1 and j == NTT - 1))
                band_release(par, "pe", lastv)
                for j in range(NTT):
                    bk, ib = banks[j]
                    b.wait("dve", "pe", lastv)
                    b.wait("dve", "dve", x_last)
                    x_last = DVE("scalar_tensor_tensor",
                                 X[:, mb * T + j * TT: mb * T + j * TT + TT],
                                 bk[0:128, 0:TT], 0.3,
                                 X[:, mb * T + j * TT: mb * T + j * TT + TT],
                                 ALU.mult, ALU.add)
                    pool.release(ib, "dve", x_last)
                b.wait("dve", "dve", bo3_v)
                b.wait("dve", "dve", x_last)
                x_last = DVE("tensor_scalar_add", X[:, mb * T:(mb + 1) * T],
                             X[:, mb * T:(mb + 1) * T],
                             STAT[:, col_bo3 + mb:col_bo3 + mb + 1])
            if phase_end("ffn1"):
                break

            b.wait("sp", "pe", b.ctr["pe"])
            cs_tot = DMA("sp", "d_cs", BIG[:, L1CS:L1CS + 2 * T], dr["cs"][:, :])
            b.wait("dve", "pe", b.ctr["pe"])
            vt1_ones = vt_ones(L1VT, 260)

            shuf = []
            for i2 in range(0, 32, 2):
                shuf += [i2 + 1, i2]

            def rope(bases, ready):
                done = None
                b.wait("dve", "d_cs", cs_tot)
                if ready is not None:
                    b.wait("dve", ready[0], ready[1])
                for base in bases:
                    for hf in range(2):
                        o = base + hf * 1024
                        co = L1CS + hf * 1024
                        so = L1CS + T + hf * 1024
                        sh_v = DVE("stream_shuffle", SCR[:, SC_Y:SC_Y + 1024],
                                   BIG[:, o:o + 1024], shuf)
                        b.wait("dve", "dve", sh_v)
                        s_v = DVE("tensor_mul", SCR[:, SC_Y:SC_Y + 1024],
                                  SCR[:, SC_Y:SC_Y + 1024],
                                  BIG[:, so:so + 1024])
                        c_v2 = DVE("tensor_mul", BIG[:, o:o + 1024],
                                   BIG[:, o:o + 1024], BIG[:, co:co + 1024])
                        b.wait("dve", "dve", s_v)
                        b.wait("dve", "dve", c_v2)
                        done = DVE("tensor_add", BIG[:, o:o + 1024],
                                   BIG[:, o:o + 1024], SCR[:, SC_Y:SC_Y + 1024])
                return done

            lq1 = lk1 = None
            for mb in range(8):
                qd = projection("wq1", [mb], evac_to(L1Q))
                ropeq_v = rope([L1Q + mb * T], qd)
                lq1 = bscan(L1Q + mb * T, 1, HN_Q, ("dve", ropeq_v),
                            init=(mb == 0))
            ropek_v = None
            for mb in range(2):
                kd = projection("wk1", [mb], evac_to(L1KA))
                ropek_v = rope([L1KA + mb * T], kd)
                lk1 = bscan(L1KA + mb * T, 1, HN_K, ("dve", ropek_v),
                            init=(mb == 0))
            b.wait("sp", "dve", ropek_v)
            kb_tot = 0
            for cc in range(2):
                kb_tot = DMA("sp", "d_kb",
                             BIG[0:64, L1KB + cc * T:L1KB + (cc + 1) * T],
                             BIG[64:128, L1KA + cc * T:L1KA + (cc + 1) * T])
                kb_tot = DMA("sp", "d_kb",
                             BIG[64:128, L1KB + cc * T:L1KB + (cc + 1) * T],
                             BIG[0:64, L1KA + cc * T:L1KA + (cc + 1) * T])
            b.wait("act", "dve", vt1_ones)
            v_project("wv1", list(range(2)), L1VT, 2, 260)
            negB1_v = bfinal(1, 32 ** -0.5, lq1, lk1)
            if phase_end("qkv1"):
                break

            pt_setup([(BIG, PT1_BASE), (BIG, PT1_BASE + 1024),
                      (BIG, PT1_BASE + 2048), (BIG, PT1_BASE + 3072)])
            ocell = [None]
            pend = None
            for h in range(_NHEADS1):
                for qt in range(NTT):
                    pend = attn_l1(h, qt, negB1_v, kb_tot, lam_v, pend, ocell)
            for st_ in pend:
                st_()
            ocv = ocell[0]
            if phase_end("attn1"):
                break

            x_last = out_proj("wo1", ("dve", ocv), L1Q)
            finish()
            break

        def run_stream(eng_name):
            def runner(e):
                for idx_it, item in enumerate(b.stream[eng_name]):
                    kind = item[0]
                    try:
                        if kind == "op":
                            _, fn, inc_eng = item
                            inst = fn(e)
                            if inc_eng is not None:
                                inst.then_inc(sems["c_" + inc_eng], 1)
                        elif kind == "wait":
                            _, sem, val = item
                            key = sems[sem] if sem in sems else sems["c_" + sem]
                            e.wait_ge(key, val)
                        else:
                            _, fn, sem = item
                            inst = fn(e)
                            inst.then_inc(sems[sem], 16)
                    except Exception:
                        import traceback
                        print(f"REPLAY FAIL {eng_name}[{idx_it}] kind={kind}")
                        traceback.print_exc()
                        raise
            return runner

        with nc.allow_low_precision(reason="bf16 elementwise on bf16 matmul operands"), \
             nc.Block() as block:
            block.tensor(run_stream("pe"))
            block.scalar(run_stream("act"))
            block.vector(run_stream("dve"))
            block.sync(run_stream("sp"))
            block.gpsimd(run_stream("gp"))

    nc.finalize()
    return nc



def _img(A):
    K, F = A.shape
    kc, mb = K // 128, F // 128
    return np.ascontiguousarray(
        A.reshape(kc, 128, mb, 128).transpose(2, 1, 0, 3)).reshape(mb, 128, kc * 128)


def _rope_tables():
    j = np.arange(16, dtype=np.float64)
    invf = 10000.0 ** (-j / 16.0)
    t = np.arange(T, dtype=np.float64)
    ang = np.outer(invf, t)
    rows_j = (np.arange(128) % 32) // 2
    C = np.cos(ang)[rows_j]
    S = np.sin(ang)[rows_j]
    sign = np.where(np.arange(128) % 2 == 0, -1.0, 1.0)[:, None]
    return np.concatenate([C, S * sign], axis=1).astype(ml_dtypes.bfloat16)


def prep_inputs(L0_attn_x, L0_attn_w_attn, L0_attn_w_proj, L0_ffn_kernel,
                L0_ffn_bias, L0_ffn_a, L0_ffn_b, L1_attn_Wq, L1_attn_Wkv,
                L1_attn_Wout, L1_attn_lambda_init, L1_attn_rms_weight,
                L1_ffn_Wp1, L1_ffn_bp1, L1_ffn_Wp2, L1_ffn_bp2, L1_ffn_W1,
                L1_ffn_b1, L1_ffn_W2, L1_ffn_b2, L1_ffn_Wo, L1_ffn_bo):
    f = np.asarray
    x = f(L0_attn_x, dtype=np.float32)
    Aqkv = np.ascontiguousarray(f(L0_attn_w_attn, dtype=np.float32).T)
    w = {
        "wq0": _img(Aqkv[:, 0:1024]),
        "wk0": _img(Aqkv[:, 1024:2048]),
        "wv0": _img(Aqkv[:, 2048:3072]),
        "wp0": _img(np.ascontiguousarray(f(L0_attn_w_proj, dtype=np.float32).T)),
        "ker0": _img(f(L0_ffn_kernel, dtype=np.float32)),
        "b0": np.ascontiguousarray(f(L0_ffn_b, dtype=np.float32)),
        "bias0": np.ascontiguousarray(
            f(L0_ffn_bias, dtype=np.float32).reshape(8, 128).T),
        "wq1": _img(np.ascontiguousarray(f(L1_attn_Wq, dtype=np.float32).T)),
        "wo1": _img(np.ascontiguousarray(f(L1_attn_Wout, dtype=np.float32).T)),
        "lam": f(L1_attn_lambda_init, dtype=np.float32).reshape(1, 16),
        "rmsw": f(L1_attn_rms_weight, dtype=np.float32).reshape(64, 1),
        "wp1": _img(f(L1_ffn_Wp1, dtype=np.float32))[0],
        "wp2": f(L1_ffn_Wp2, dtype=np.float32),
        "bp1": f(L1_ffn_bp1, dtype=np.float32).reshape(128, 1),
        "bp2": f(L1_ffn_bp2, dtype=np.float32).reshape(1, 8),
        "b1e": np.stack([np.ascontiguousarray(
            f(L1_ffn_b1, dtype=np.float32)[e].reshape(16, 128).T).reshape(-1)
            for e in range(8)]),
        "b2e": np.stack([np.ascontiguousarray(
            f(L1_ffn_b2, dtype=np.float32)[e].reshape(8, 128).T).reshape(-1)
            for e in range(8)]),
        "wof": _img(f(L1_ffn_Wo, dtype=np.float32)),
        "bof": np.ascontiguousarray(f(L1_ffn_bo, dtype=np.float32).reshape(8, 128).T),
    }
    a = f(L0_ffn_a, dtype=np.float32)
    w["a0"] = np.ascontiguousarray(
        a.reshape(8, 128, 16).transpose(1, 0, 2)).reshape(128, 128)
    Akv = np.ascontiguousarray(f(L1_attn_Wkv, dtype=np.float32).T)
    w["wk1"] = _img(Akv[:, 0:256])
    w["wv1"] = _img(Akv[:, 256:512])
    W1 = f(L1_ffn_W1, dtype=np.float32)
    w["w1e"] = np.stack([_img(W1[e]).reshape(-1) for e in range(8)])
    W2 = f(L1_ffn_W2, dtype=np.float32)
    w["w2e"] = np.stack([_img(W2[e]).reshape(-1) for e in range(8)])
    w["cs"] = _rope_tables()
    kk, qq = np.meshgrid(np.arange(128), np.arange(128), indexing="ij")
    w["mask"] = (qq >= kk).astype(ml_dtypes.bfloat16)
    w["ident"] = np.eye(128, dtype=ml_dtypes.bfloat16)
    w["identf"] = np.eye(128, dtype=np.float32)
    w["grp2"] = (np.arange(128)[:, None] // 64 ==
                 np.arange(2)[None, :]).astype(ml_dtypes.bfloat16)
    w["grp4"] = (np.arange(128)[:, None] // 32 ==
                 np.arange(4)[None, :]).astype(ml_dtypes.bfloat16)
    w["io99"] = (np.arange(8, dtype=np.float32) - 99.0).reshape(1, 8)
    in_maps = []
    for core in range(8):
        m = {"xT": np.ascontiguousarray(x[core].T)}
        m.update(w)
        in_maps.append(m)
    return in_maps


def kernel(**inputs):
    in_maps = prep_inputs(**inputs)
    nc = build_nc(upto=_UPTO, debug=_DEBUG)
    res = run_bass_kernel_spmd(nc, in_maps, core_ids=list(range(8)))
    out = np.stack([np.ascontiguousarray(r["yT"].T) for r in res.results])
    return out.astype(np.float32)
